# version 03
# baseline (speedup 1.0000x reference)
import numpy as np
import ml_dtypes
import concourse.bass as bass
import concourse.mybir as mybir
from concourse.bass_utils import run_bass_kernel_spmd

F32 = mybir.dt.float32
BF16 = mybir.dt.bfloat16
F16 = mybir.dt.float16
I16 = mybir.dt.int16
U16 = mybir.dt.uint16
AL = mybir.AluOpType
AF = mybir.ActivationFunctionType
AX = mybir.AxisListType

B = 8
D = 2048
T = 2048
TH = 1024
NCH = 16
NH = 16
HD = 128
EE = 8
ER = 128
IH = 5632
GUO = 11264
NGC = 44
RSQ_HD = float(1.0 / np.sqrt(128.0))
EXP_BIAS = -4.0
BF = np.dtype(ml_dtypes.bfloat16)

_p = np.arange(D)
SIGD = 4 * (_p % 512) + _p // 512
_q = np.arange(IH)
SIGH = 4 * (_q % 1408) + _q // 1408



def _tile_chunks(wT):
    r = np.empty((NCH, NCH, 128, 128), np.float32)
    for oc in range(NCH):
        r[oc] = wT[:, oc * 128:(oc + 1) * 128].reshape(NCH, 128, 128)
    return np.ascontiguousarray(r)


def stage_shared(inputs):
    g = {}
    sd, sh = SIGD, SIGH
    wq = np.ascontiguousarray(inputs["L0_attn_w_project"]).T[sd]
    t = np.empty((NH, NCH, 128, 384), np.float32)
    for h in range(NH):
        cols = np.concatenate([wq[:, k * D + h * HD:k * D + (h + 1) * HD] for k in range(3)], axis=1)
        t[h] = cols.reshape(NCH, 128, 384)
    g["wqkv"] = np.ascontiguousarray(t)
    g["woT"] = _tile_chunks(np.ascontiguousarray(inputs["L0_attn_w_out"]).T[:, sd])
    g["baseT"] = _tile_chunks(np.ascontiguousarray(inputs["L0_ffn_base_w"]).T[sd][:, sd])
    g["base_b"] = np.ascontiguousarray(np.asarray(inputs["L0_ffn_base_b"])[sd].reshape(NCH, 128).T)
    g["routerT"] = np.ascontiguousarray(
        np.asarray(inputs["L0_ffn_router_w"]).T[sd].reshape(NCH, 128, EE).transpose(1, 0, 2).reshape(128, NCH * EE))
    g["router_b"] = np.ascontiguousarray(np.asarray(inputs["L0_ffn_router_b"]).reshape(EE, 1))
    g["aallT"] = np.ascontiguousarray(
        np.asarray(inputs["L0_ffn_lora_A"]).transpose(1, 0, 2).reshape(D, ER)[sd]
        .reshape(NCH, 128, ER).transpose(1, 0, 2).reshape(128, NCH * ER))
    g["ballT"] = np.ascontiguousarray(np.asarray(inputs["L0_ffn_lora_B"]).reshape(ER, D)[:, sd])
    g["rms_w"] = np.ascontiguousarray(np.asarray(inputs["L1_attn_rms_w"])[sd].reshape(NCH, 128).T)
    for nm, key in [("AwT", "L1_attn_A_w"), ("BwT", "L1_attn_B_w"), ("CawT", "L1_attn_Ca_w"),
                    ("CbwT", "L1_attn_Cb_w"), ("DwT", "L1_attn_D_w")]:
        g[nm] = _tile_chunks(np.ascontiguousarray(inputs[key]).T[sd])
    g["A_b"] = np.ascontiguousarray(np.asarray(inputs["L1_attn_A_b"]).reshape(NCH, 128).T)
    g["OwT"] = _tile_chunks(np.ascontiguousarray(inputs["L1_attn_out_w"]).T[:, sd])
    g["out_b"] = np.ascontiguousarray(np.asarray(inputs["L1_attn_out_b"])[sd].reshape(NCH, 128).T)
    rowp = np.concatenate([sh, IH + sh])
    gu16 = np.ascontiguousarray(np.asarray(inputs["L1_ffn_gate_up_packed"]).view(np.int16)[rowp].T)
    g["gu16"] = np.ascontiguousarray(gu16.reshape(4, 128, GUO))
    gsc = np.asarray(inputs["L1_ffn_gate_up_scales"])[:, rowp]
    g["gusc"] = np.ascontiguousarray(np.repeat(gsc, 32, axis=0).reshape(4, 128, GUO))
    dn16 = np.ascontiguousarray(np.asarray(inputs["L1_ffn_down_packed"])[sd].view(np.int16).T)
    g["dn16"] = np.ascontiguousarray(dn16.reshape(11, 128, D))
    dsc = np.asarray(inputs["L1_ffn_down_scales"])[:, sd]
    g["dnsc"] = np.ascontiguousarray(np.repeat(dsc, 32, axis=0).reshape(11, 128, D))
    return g



_DTSIZE = {F32: 4, BF16: 2, F16: 2, I16: 2, U16: 2}


class KB:

    def __init__(self, nc):
        self.nc = nc
        self.sems = {}
        self.cnt = {}
        self.guards = {}
        self._semctx = []
        self.dma_sems = set()
        base = (nc.sbuf_base + 31) // 32 * 32
        self.freelist = [(base, nc.sbuf_top)]
        self.live = {}
        self.hwm = 0

    def sem(self, name):
        if name not in self.sems:
            g = self.nc.semaphore(name)
            h = g.__enter__()
            self._semctx.append(g)
            self.sems[name] = h
            self.cnt[name] = 0
        return self.sems[name]

    def inc(self, instr, name, k=1):
        instr.then_inc(self.sem(name), k)
        self.cnt[name] += k
        if k >= 16 and k % 16 == 0:
            self.dma_sems.add(name)
        return self.cnt[name]

    def n(self, name):
        self.sem(name)
        return self.cnt[name]

    def w(self, eng, name, val):
        if val > 0:
            eng.wait_ge(self.sem(name), val)

    def dinc(self, instr, name, idx, k=16):
        self.inc(instr, f"{name}_{idx % 2}", k)

    def dw(self, eng, name, idx, k=16):
        eng.wait_ge(self.sem(f"{name}_{idx % 2}"), k * (idx // 2 + 1))

    def dwfull(self, eng, name, total0, total1):
        self.w(eng, f"{name}_0", total0)
        self.w(eng, f"{name}_1", total1)

    def vop(self, instr, name="vch"):
        return self.inc(instr, name, 1)

    def vwait(self, eng, name="vch"):
        self.w(eng, name, self.n(name))

    def sb(self, name, shape, dt):
        nb = 1
        for s in shape[1:]:
            nb *= s
        nb *= _DTSIZE[dt]
        nb = (nb + 31) // 32 * 32
        for idx, (lo, hi) in enumerate(self.freelist):
            if hi - lo >= nb:
                self.freelist[idx] = (lo + nb, hi)
                if lo + nb == hi:
                    self.freelist.pop(idx)
                self.live[name] = (lo, lo + nb)
                used = sum(b - a for (a, b) in (v for v in self.live.values()))
                self.hwm = max(self.hwm, used)
                return self.nc.alloc_sbuf_tensor_at(name, shape, dt, offset=lo)
        raise MemoryError(f"arena OOM for {name} need {nb} free={self.freelist}")

    def free(self, *names):
        for n in names:
            lo, hi = self.live.pop(n)
            self.freelist.append((lo, hi))
        self.freelist.sort()
        merged = []
        for lo, hi in self.freelist:
            if merged and merged[-1][1] >= lo:
                merged[-1] = (merged[-1][0], max(merged[-1][1], hi))
            else:
                merged.append((lo, hi))
        self.freelist = [tuple(x) for x in merged]

    def ps(self, name, shape, dt=F32):
        g = self.nc.psum_tensor(name, shape, dt)
        t = g.__enter__()
        self.guards[name] = g
        return t


class DQ:

    def __init__(self, kb, ins, scr):
        self.kb = kb
        self.units = [("gu", wb, sl) for wb in range(4) for sl in range(22)] + \
                     [("dn", wb, sl) for wb in range(11) for sl in range(4)]
        self.nu = len(self.units)
        self.ld_i = self.sc_i = self.vec_i = self.st_i = 0
        self.ins = ins
        self.scr = scr
        self.pk = [kb.sb(f"dq_pk{i}", [128, 512], I16) for i in range(2)]
        self.sc = [kb.sb(f"dq_sc{i}", [128, 512], F32) for i in range(2)]
        self.out = kb.sb("dq_out", [128, 4 * 512], BF16)
        self.t0 = kb.sb("dq_t0", [128, 512], I16)
        self.t1 = kb.sb("dq_t1", [128, 512], I16)
        self.t2 = kb.sb("dq_t2", [128, 512], I16)
        self.t3 = kb.sb("dq_t3", [128, 512], I16)

    def _src(self, u):
        kind, wb, sl = u
        if kind == "gu":
            return (self.ins["gu16"][wb, :, sl * 512:(sl + 1) * 512],
                    self.ins["gusc"][wb, :, sl * 512:(sl + 1) * 512])
        return (self.ins["dn16"][wb, :, sl * 512:(sl + 1) * 512],
                self.ins["dnsc"][wb, :, sl * 512:(sl + 1) * 512])

    def pump_sync(self, eng, n, lag=1):
        kb = self.kb
        for _ in range(n):
            progressed = False
            while self.st_i < self.nu and (self.st_i <= self.ld_i - lag or self.ld_i >= self.nu):
                self._store(eng)
                progressed = True
            if self.ld_i < self.nu:
                i = self.ld_i
                kb.w(eng, "dq_v", (i - 1) * 4)
                pk_src, _ = self._src(self.units[i])
                kb.dinc(eng.dma_start(out=self.pk[i % 2][:], in_=pk_src), "dq_pk", i)
                self.ld_i += 1
                progressed = True
            if not progressed:
                break

    def _store(self, eng):
        kb = self.kb
        i = self.st_i
        eng.wait_ge(kb.sem("dq_v"), (i + 1) * 4)
        kind, wb, sl = self.units[i]
        scrt = self.scr["wgu"] if kind == "gu" else self.scr["wdn"]
        ncols = GUO if kind == "gu" else D
        R = 4 if kind == "gu" else 11
        off = wb * 128 * ncols + sl * 512
        dst = bass.AP(scrt, off, [[ncols, 128], [R * 128 * ncols, 4], [1, 512]])
        srcap = self.out[:].rearrange("p (j o) -> p j o", j=4)
        kb.inc(eng.dma_start(out=dst, in_=srcap), "dq_st", 16)
        self.st_i += 1

    def sc_loads(self, eng, n):
        kb = self.kb
        for _ in range(n):
            if self.sc_i >= self.nu:
                return
            i = self.sc_i
            kb.w(eng, "dq_v", (i - 1) * 4)
            _, sc_src = self._src(self.units[i])
            kb.dinc(eng.dma_start(out=self.sc[i % 2][:], in_=sc_src), "dq_sc", i)
            self.sc_i += 1

    def vec(self, v, n):
        kb = self.kb
        for _ in range(n):
            if self.vec_i >= self.nu:
                return
            i = self.vec_i
            kb.dw(v, "dq_pk", i)
            kb.dw(v, "dq_sc", i)
            kb.w(v, "dq_st", 16 * i)
            W16, S = self.pk[i % 2], self.sc[i % 2]
            O = self.out
            for jp in range(4):
                sh = 4 * jp - 6
                kb.w(v, "dq_v", 4 * i + jp)
                kb.vwait(v)
                if sh < 0:
                    kb.vop(v.tensor_scalar(self.t0[:], W16[:], -sh, 0x1C0, AL.logical_shift_left, AL.bitwise_and))
                else:
                    kb.vop(v.tensor_scalar(self.t0[:], W16[:], sh, 0x1C0, AL.logical_shift_right, AL.bitwise_and))
                kb.vop(v.tensor_scalar(self.t1[:], W16[:], 4 * jp + 3, 15, AL.logical_shift_right, AL.logical_shift_left))
                kb.vwait(v)
                kb.vop(v.tensor_single_scalar(self.t2[:], self.t0[:], 64, AL.is_ge))
                kb.vwait(v)
                kb.vop(v.scalar_tensor_tensor(self.t2[:], self.t2[:], 16064, self.t0[:], AL.mult, AL.add))
                kb.vop(v.tensor_single_scalar(self.t3[:], self.t0[:], 128, AL.is_ge))
                kb.vwait(v)
                kb.vop(v.scalar_tensor_tensor(self.t2[:], self.t3[:], 64, self.t2[:], AL.mult, AL.add))
                kb.vwait(v)
                kb.vop(v.tensor_tensor(self.t3[:], self.t2[:], self.t1[:], AL.bitwise_or))
                kb.vwait(v)
                i2 = v.tensor_tensor(O[:, jp * 512:(jp + 1) * 512], self.t3[:].bitcast(BF16), S[:], AL.mult)
                kb.inc(i2, "dq_v", 1)
            self.vec_i += 1




def _att_sched():
    groups = []
    for qq in range(4):
        kts = list(range(4 * qq + 4))
        for i in range(0, len(kts), 2):
            rec = []
            for kt in kts[i:i + 2]:
                dt = kt - 4 * qq
                qs = max(0, 128 * dt)
                rec.append((kt, qs, dt if 0 <= dt <= 3 else None))
            groups.append((qq, rec))
    return groups


ATT_GROUPS = _att_sched()
ATT_NG = len(ATT_GROUPS)
ATT_NEXP = [1 if all(qs == 0 for (_, qs, _) in rec) else len(rec) for (_, rec) in ATT_GROUPS]
ATT_CUMEXP = np.cumsum(ATT_NEXP).tolist()
ATT_NMASK = [sum(1 for (_, _, dg) in rec if dg is not None) for (_, rec) in ATT_GROUPS]
ATT_CUMMASK = np.cumsum(ATT_NMASK).tolist()
ATT_QLAST = {}
for _gi, (_qq, _) in enumerate(ATT_GROUPS):
    ATT_QLAST[_qq] = _gi
ATT_TOTEXP = ATT_CUMEXP[-1]
ATT_TOTMASK = ATT_CUMMASK[-1]


def build_program(upto=99):
    nc = bass.Bass("TRN2", target_bir_lowering=False)
    ins = {}
    ins["xT"] = nc.dram_tensor("xT", [D, T], F32, kind="ExternalInput").ap()
    ins["wqkv"] = nc.dram_tensor("wqkv", [NH, NCH, 128, 384], F32, kind="ExternalInput").ap()
    for nm in ["woT", "baseT", "OwT", "AwT", "BwT", "CawT", "CbwT", "DwT"]:
        ins[nm] = nc.dram_tensor(nm, [NCH, NCH, 128, 128], F32, kind="ExternalInput").ap()
    for nm, shape in [("base_b", [128, NCH]), ("A_b", [128, NCH]), ("out_b", [128, NCH]),
                      ("rms_w", [128, NCH]), ("routerT", [128, NCH * EE]),
                      ("router_b", [EE, 1]), ("aallT", [128, NCH * ER]), ("ballT", [ER, D])]:
        ins[nm] = nc.dram_tensor(nm, shape, F32, kind="ExternalInput").ap()
    ins["gu16"] = nc.dram_tensor("gu16", [4, 128, GUO], I16, kind="ExternalInput").ap()
    ins["gusc"] = nc.dram_tensor("gusc", [4, 128, GUO], F16, kind="ExternalInput").ap()
    ins["dn16"] = nc.dram_tensor("dn16", [11, 128, D], I16, kind="ExternalInput").ap()
    ins["dnsc"] = nc.dram_tensor("dnsc", [11, 128, D], F16, kind="ExternalInput").ap()

    out = nc.dram_tensor("x2T", [D, T], F32, kind="ExternalOutput").ap()

    scr = {
        "x1": nc.dram_tensor("s_x1", [D, T], F32, kind="Internal"),
        "xn1": nc.dram_tensor("s_xn1", [D, T], BF16, kind="Internal"),
        "y": nc.dram_tensor("s_y", [D, T], BF16, kind="Internal"),
        "p1": nc.dram_tensor("s_p1", [D, T], F32, kind="Internal"),
        "wgu": nc.dram_tensor("s_wgu", [D, GUO], BF16, kind="Internal"),
        "wdn": nc.dram_tensor("s_wdn", [IH, D], BF16, kind="Internal"),
    }

    ident_np = np.eye(128, dtype=BF).view(np.uint16)
    identf_np = np.eye(128, dtype=np.float32)
    trimask_np = (np.arange(128)[:, None] <= np.arange(128)[None, :]).astype(BF).view(np.uint16)
    emap_np = np.zeros((EE, ER), BF)
    for e in range(EE):
        emap_np[e, e * 16:(e + 1) * 16] = 1.0
    c_ident = nc.inline_tensor(ident_np, "c_ident")
    c_identf = nc.inline_tensor(identf_np, "c_identf")
    c_tri = nc.inline_tensor(trimask_np, "c_tri")
    c_emap = nc.inline_tensor(emap_np.view(np.uint16), "c_emap")

    kb = KB(nc)
    sb, free = kb.sb, kb.free

    ONESF = sb("ONESF", [128, 128], F32)
    ONESB = sb("ONESB", [128, 128], BF16)
    IDENT = sb("IDENT", [128, 128], BF16)
    IDENTF = sb("IDENTF", [128, 128], F32)
    TRIM = sb("TRIM", [128, 128], BF16)
    EMAP = sb("EMAP", [EE, ER], BF16)
    CEXPS = sb("CEXPS", [128, 1], F32)
    CEXPB = sb("CEXPB", [128, 1], F32)
    CN2K = sb("CN2K", [128, 1], F32)
    BBH = sb("BBH", [128, NCH], F32)
    OBH = sb("OBH", [128, NCH], F32)
    ABH = sb("ABH", [128, NCH], F32)
    RWH = sb("RWH", [128, NCH], F32)
    RBH = sb("RBH", [EE, 1], F32)
    BAL = sb("BAL", [ER, D], BF16)
    AAL = sb("AAL", [128, NCH * ER], BF16)
    RTW = sb("RTW", [128, NCH * EE], BF16)
    XN0 = sb("XN0", [128, NCH * T], BF16)
    WQH = [sb(f"WQH{i}", [128, NCH * 384], BF16) for i in range(2)]
    XB = [sb(f"XB{i}", [128, TH], F32) for i in range(2)]
    dq = DQ(kb, ins, scr)

    blockg = nc.Block()
    block = blockg.__enter__()

    @block.gpsimd
    def _(g):
        for tile, val in [(ONESF, 1.0), (ONESB, 1.0), (CEXPS, RSQ_HD),
                          (CEXPB, EXP_BIAS), (CN2K, 1.0 / D)]:
            kb.inc(g.memset(tile[:], val), "inim", 1)
        for dst, src in [(BBH[:], ins["base_b"]), (OBH[:], ins["out_b"]),
                         (ABH[:], ins["A_b"]), (RWH[:], ins["rms_w"]),
                         (RBH[:], ins["router_b"]), (BAL[:], ins["ballT"]),
                         (AAL[:], ins["aallT"]), (RTW[:], ins["routerT"])]:
            kb.inc(g.dma_start(out=dst, in_=src), "ini", 16)
        kb.inc(g.dma_start(out=IDENT[:].bitcast(U16), in_=c_ident.ap()), "ini", 16)
        kb.inc(g.dma_start(out=IDENTF[:], in_=c_identf.ap()), "ini", 16)
        kb.inc(g.dma_start(out=TRIM[:].bitcast(U16), in_=c_tri.ap()), "ini", 16)
        kb.inc(g.dma_start(out=EMAP[:].bitcast(U16), in_=c_emap.ap()), "ini", 16)
    INI = kb.n("ini")
    INIM = kb.n("inim")

    M1R = sb("M1R", [128, T], F32)
    RSTD = sb("RSTD", [128, T], F32)
    S1A = sb("S1A", [128, T], F32)
    S2A = sb("S2A", [128, T], F32)
    SQT = sb("SQT", [128, TH], F32)
    TR0 = sb("TR0", [128, TH], F32)
    p_big = kb.ps("p_big", [128, 2048], F32)
    p_m1 = kb.ps("p_m1", [128, 512], F32)
    p_m2 = kb.ps("p_m2", [128, 512], F32)
    p_tr = kb.ps("p_tr", [128, 1024], BF16)
    p_st = kb.ps("p_st", [128, 512], F32)

    def xsrc(hc):
        c, th = hc // 2, hc % 2
        return ins["xT"][c * 128:(c + 1) * 128, th * TH:(th + 1) * TH]

    @block.sync
    def _(sync):
        for hc in range(32):
            kb.w(sync, "f0", 2 * (hc - 1))
            kb.dinc(sync.dma_start(out=XB[hc % 2][:], in_=xsrc(hc)), "xl0", hc)
        dq.pump_sync(sync, 2, lag=99)

    @block.scalar
    def _(s):
        for hc in range(32):
            kb.dw(s, "xl0", hc)
            kb.w(s, "f0", 2 * hc)
            kb.inc(s.activation(SQT[:], XB[hc % 2][:], AF.Square), "sq0", 1)

    @block.vector
    def _(v):
        for hc in range(32):
            th = hc % 2
            kb.dw(v, "xl0", hc)
            v.wait_ge(kb.sem("sq0"), hc + 1)
            if hc < 2:
                i1 = v.tensor_copy(S1A[:, th * TH:(th + 1) * TH], XB[hc % 2][:])
                i2 = v.tensor_copy(S2A[:, th * TH:(th + 1) * TH], SQT[:])
            else:
                i1 = v.tensor_add(S1A[:, th * TH:(th + 1) * TH], S1A[:, th * TH:(th + 1) * TH], XB[hc % 2][:])
                i2 = v.tensor_add(S2A[:, th * TH:(th + 1) * TH], S2A[:, th * TH:(th + 1) * TH], SQT[:])
            kb.inc(i1, "f0", 1)
            kb.inc(i2, "f0", 1)

    @block.tensor
    def _(t):
        t.wait_ge(kb.sem("ini"), INI)
        t.wait_ge(kb.sem("inim"), INIM)
        t.wait_ge(kb.sem("f0"), 64)
        for s4 in range(4):
            i = t.matmul(p_big[:, s4 * 512:(s4 + 1) * 512], ONESF[:],
                         S1A[:, s4 * 512:(s4 + 1) * 512], start=True, stop=True)
        kb.inc(i, "c0t", 1)
        t.wait_ge(kb.sem("c0s"), 1)
        for s4 in range(4):
            i = t.matmul(p_big[:, s4 * 512:(s4 + 1) * 512], ONESF[:],
                         S2A[:, s4 * 512:(s4 + 1) * 512], start=True, stop=True)
        kb.inc(i, "c0t", 1)

    @block.scalar
    def _(s):
        s.wait_ge(kb.sem("c0t"), 1)
        kb.inc(s.activation(M1R[:], p_big[:], AF.Identity, scale=CN2K[:, 0:1]), "c0s", 1)
        s.wait_ge(kb.sem("c0t"), 2)
        kb.inc(s.activation(S2A[:], p_big[:], AF.Identity, scale=CN2K[:, 0:1]), "c0s", 1)
        s.wait_ge(kb.sem("c0v"), 1)
        kb.inc(s.activation(RSTD[:], S1A[:], AF.Sqrt), "c0s", 1)

    @block.vector
    def _(v):
        v.wait_ge(kb.sem("c0s"), 2)
        kb.vop(v.tensor_tensor(S1A[:], M1R[:], M1R[:], AL.mult))
        kb.vwait(v)
        kb.vop(v.tensor_tensor(S2A[:], S2A[:], S1A[:], AL.subtract))
        kb.vwait(v)
        kb.vop(v.tensor_scalar(S2A[:], S2A[:], 1e-5, None, AL.add, AL.bypass))
        kb.vwait(v)
        kb.inc(v.reciprocal(S1A[:], S2A[:]), "c0v", 1)

    @block.sync
    def _(sync):
        for hc in range(32):
            if hc < 2:
                sync.wait_ge(kb.sem("f0"), 64)
                sync.wait_ge(kb.sem("sq0"), 32)
            else:
                kb.w(sync, "a0", hc - 1)
            kb.dinc(sync.dma_start(out=XB[hc % 2][:], in_=xsrc(hc)), "xl0b", hc)
        dq.pump_sync(sync, 4)

    @block.vector
    def _(v):
        v.wait_ge(kb.sem("c0s"), 3)
        for hc in range(32):
            c, th = hc // 2, hc % 2
            kb.dw(v, "xl0b", hc)
            kb.w(v, "a0", hc)
            kb.vop(v.tensor_tensor(TR0[:], XB[hc % 2][:], M1R[:, th * TH:(th + 1) * TH], AL.subtract))
            kb.vwait(v)
            i = v.tensor_tensor(XN0[:, c * T + th * TH:c * T + (th + 1) * TH], TR0[:],
                                RSTD[:, th * TH:(th + 1) * TH], AL.mult)
            kb.inc(i, "a0", 1)
        dq.vec(v, 4)

    free("S1A", "S2A", "SQT", "TR0", "M1R", "RSTD", "XB0", "XB1")

    if upto >= 1:
        OA = sb("OA", [128, NH * T], BF16)
        QT = [sb(f"QT{i}", [128, T], BF16) for i in range(2)]
        KT = [sb(f"KT{i}", [128, T], BF16) for i in range(2)]
        VTF = [sb(f"VTF{i}", [128, T], BF16) for i in range(2)]
        EB = [sb(f"EB{i}", [128, 1024], BF16) for i in range(2)]
        RD = sb("RD", [128, 512], F32)

        @block.gpsimd
        def _(g):
            for h in range(NH):
                kb.w(g, "wqc", h - 1)
                for half in range(2):
                    src = ins["wqkv"][h, half * 8:(half + 1) * 8, :, :].transpose([1, 0, 2])
                    dst = WQH[h % 2][:].rearrange("p (c o) -> p c o", o=384)[:, half * 8:(half + 1) * 8]
                    kb.dinc(g.dma_start(out=dst, in_=src), "wq", h, 16)
                if h >= 1:
                    dq.sc_loads(g, 8 if h == 1 else 5)
            dq.sc_loads(g, 6)

        @block.tensor
        def _(t):
            t.wait_ge(kb.sem("ini"), INI)
            t.wait_ge(kb.sem("inim"), INIM)

            def qkv_piece(hh, j):
                pi, s4 = j // 4, j % 4
                q = hh * 12 + j
                if j == 0:
                    kb.dw(t, "wq", hh, 32)
                kb.w(t, "qe", q)
                for c in range(NCH):
                    if hh == 0 and j == 0:
                        t.wait_ge(kb.sem("a0"), 2 * c + 2)
                    i = t.matmul(p_st[:, 0:512],
                                 WQH[hh % 2][:, c * 384 + pi * 128:c * 384 + pi * 128 + 128],
                                 XN0[:, c * T + s4 * 512:c * T + (s4 + 1) * 512],
                                 start=(c == 0), stop=(c == NCH - 1))
                if j == 11:
                    kb.inc(i, "wqc", 1)
                else:
                    kb.inc(i, "qp", 1)

            for j in range(12):
                qkv_piece(0, j)

            for h in range(NH):
                t.wait_ge(kb.sem("qe"), h * 12 + 12)
                for bt in range(2):
                    if 2 * h + bt > 0:
                        t.wait_ge(kb.sem("vtk"), 2 * h + bt)
                    for j in range(8):
                        kc = bt * 8 + j
                        i = t.transpose(p_tr[:, j * 128:(j + 1) * 128],
                                        VTF[h % 2][:, kc * 128:(kc + 1) * 128], IDENT[:])
                    kb.inc(i, "ttr", 1)

                base_exp = h * ATT_TOTEXP
                base_msk = h * ATT_TOTMASK
                base_grp = h * ATT_NG
                base_oan = h * 4

                def s_group(gi):
                    qq, rec = ATT_GROUPS[gi]
                    if gi >= 2:
                        t.wait_ge(kb.sem("exf"), base_exp + ATT_CUMEXP[gi - 2])
                    elif h > 0:
                        t.wait_ge(kb.sem("exf"), base_exp)
                    half = (gi % 2) * 1024
                    for kti, (kt, qs, dg) in enumerate(rec):
                        i = t.matmul(p_big[:, half + kti * 512 + qs: half + (kti + 1) * 512],
                                     KT[h % 2][:, kt * 128:(kt + 1) * 128],
                                     QT[h % 2][:, qq * 512 + qs:(qq + 1) * 512],
                                     start=True, stop=True)
                    kb.inc(i, "sgr", 1)

                def do_group(gi):
                    qq, rec = ATT_GROUPS[gi]
                    t.wait_ge(kb.sem("exf"), base_exp + ATT_CUMEXP[gi])
                    if ATT_CUMMASK[gi]:
                        t.wait_ge(kb.sem("msk"), base_msk + ATT_CUMMASK[gi])
                    t.wait_ge(kb.sem("vtk"), 2 * h + 2)
                    first = (gi == 0 or ATT_GROUPS[gi - 1][0] != qq)
                    if first:
                        kb.w(t, "oan", base_oan + qq)
                    for kti, (kt, qs, dg) in enumerate(rec):
                        st = first and kti == 0
                        last = (ATT_QLAST[qq] == gi) and (kti == len(rec) - 1)
                        t.matmul(p_m2[:, qs:512], ONESB[:],
                                 EB[gi % 2][:, kti * 512 + qs:(kti + 1) * 512],
                                 start=st, stop=last)
                        i = t.matmul(p_m1[:, qs:512], VTF[h % 2][:, kt * 128:(kt + 1) * 128],
                                     EB[gi % 2][:, kti * 512 + qs:(kti + 1) * 512],
                                     start=st, stop=last)
                    kb.inc(i, "oco", 1)

                for gi in range(ATT_NG):
                    s_group(gi)
                    if gi >= 1:
                        do_group(gi - 1)
                    if h + 1 < NH and gi < 12:
                        qkv_piece(h + 1, gi)
                do_group(ATT_NG - 1)

        @block.scalar
        def _(s):
            s.wait_ge(kb.sem("ini"), INI)
            s.wait_ge(kb.sem("inim"), INIM)

            def piece_evac(hh, j):
                q = hh * 12 + j
                pi, s4 = j // 4, j % 4
                dst = [QT, KT, VTF][pi][hh % 2]
                if j == 11:
                    kb.w(s, "wqc", hh + 1)
                else:
                    s.wait_ge(kb.sem("qp"), q + 1 - hh)
                if hh >= 2 and j == 0:
                    kb.w(s, "oco", (hh - 1) * ATT_NG)
                kb.inc(s.activation(dst[:, s4 * 512:(s4 + 1) * 512], p_st[:, 0:512], AF.Copy),
                       "qe", 1)

            for j in range(12):
                piece_evac(0, j)
            for h in range(NH):
                base_exp = h * ATT_TOTEXP
                base_grp = h * ATT_NG
                for gi, (qq, rec) in enumerate(ATT_GROUPS):
                    s.wait_ge(kb.sem("sgr"), base_grp + gi + 1)
                    kb.w(s, "oco", base_grp + gi - 1)
                    half = (gi % 2) * 1024
                    if ATT_NEXP[gi] == 1:
                        n = len(rec) * 512
                        i = s.activation(EB[gi % 2][:, 0:n], p_big[:, half:half + n], AF.Exp,
                                         bias=CEXPB[:, 0:1], scale=CEXPS[:, 0:1])
                        kb.inc(i, "exf", 1)
                    else:
                        for kti, (kt, qs, dg) in enumerate(rec):
                            i = s.activation(EB[gi % 2][:, kti * 512 + qs:(kti + 1) * 512],
                                             p_big[:, half + kti * 512 + qs:half + (kti + 1) * 512],
                                             AF.Exp, bias=CEXPB[:, 0:1], scale=CEXPS[:, 0:1])
                            kb.inc(i, "exf", 1)
                    if h + 1 < NH and gi < 12:
                        piece_evac(h + 1, gi)

        @block.vector
        def _(v):
            for h in range(NH):
                for bt in range(2):
                    v.wait_ge(kb.sem("ttr"), 2 * h + bt + 1)
                    i = v.tensor_copy(VTF[h % 2][:, bt * 1024:(bt + 1) * 1024], p_tr[:])
                    kb.inc(i, "vtk", 1)
                base_exp = h * ATT_TOTEXP
                base_msk = h * ATT_TOTMASK
                base_grp = h * ATT_NG
                base_oan = h * 4
                for gi, (qq, rec) in enumerate(ATT_GROUPS):
                    if ATT_NMASK[gi]:
                        v.wait_ge(kb.sem("exf"), base_exp + ATT_CUMEXP[gi])
                        for kti, (kt, qs, dg) in enumerate(rec):
                            if dg is None:
                                continue
                            sl = EB[gi % 2][:, kti * 512 + 128 * dg:kti * 512 + 128 * dg + 128]
                            kb.inc(v.tensor_tensor(sl, sl, TRIM[:], AL.mult), "msk", 1)
                    if ATT_QLAST[qq] == gi:
                        v.wait_ge(kb.sem("oco"), base_grp + gi + 1)
                        kb.w(v, "oan", base_oan + qq)
                        kb.vop(v.reciprocal(RD[:], p_m2[:]))
                        kb.vwait(v)
                        i = v.scalar_tensor_tensor(OA[:, h * T + qq * 512:h * T + (qq + 1) * 512],
                                                   p_m1[:], 1.0, RD[:], AL.mult, AL.mult)
                        kb.inc(i, "oan", 1)
                dq.vec(v, 5)

        @block.sync
        def _(sync):
            dq.pump_sync(sync, 78)

        OAN_ALL = 4 * NH

    if upto >= 2:
        free("QT0", "QT1", "KT0", "KT1", "EB0", "EB1", "RD", "VTF0", "VTF1", "WQH0", "WQH1")
        WPA = [sb(f"WPA{i}", [128, NCH * 128], BF16) for i in range(2)]
        WPB = [sb(f"WPB{i}", [128, NCH * 128], BF16) for i in range(2)]
        X1B = [sb(f"X1B{i}", [128, TH], F32) for i in range(2)]
        XB = [sb(f"XB{i}", [128, TH], F32) for i in range(2)]
        LG8 = sb("LG8", [EE, T], F32)
        GX = sb("GX", [128, T], BF16)
        MIDG = sb("MIDG", [128, T], BF16)
        GT8 = sb("GT8", [EE, T], BF16)
        LGD = sb("LGD", [128, 128], F32)
        EL8 = sb("EL8", [128, 128], F32)
        G2D = sb("G2D", [128, 128], BF16)
        TMP8 = sb("TMP8", [128, 8], F32)
        COLS = sb("COLS", [128, 8], F32)
        MRK = sb("MRK", [128, 4], F32)
        NM1A = sb("NM1A", [128, NCH], F32)

        @block.gpsimd
        def _(g):
            for oc in range(NCH):
                if oc < 2:
                    g.wait_ge(kb.sem("oco"), NH * ATT_NG)
                else:
                    kb.w(g, "ept", 2 * (oc - 1))
                for hf in range(2):
                    kb.inc(g.dma_start(
                        out=WPA[oc % 2][:].rearrange("p (c o) -> p c o", o=128)[:, hf * 8:(hf + 1) * 8],
                        in_=ins["baseT"][oc, hf * 8:(hf + 1) * 8].transpose([1, 0, 2])), f"wp2_{oc % 2}", 16)
                    kb.inc(g.dma_start(
                        out=WPB[oc % 2][:].rearrange("p (c o) -> p c o", o=128)[:, hf * 8:(hf + 1) * 8],
                        in_=ins["woT"][oc, hf * 8:(hf + 1) * 8].transpose([1, 0, 2])), f"wp2_{oc % 2}", 16)

        @block.tensor
        def _(t):
            t.wait_ge(kb.sem("exf"), NH * ATT_TOTEXP)
            for c in range(NCH):
                for s4 in range(4):
                    i = t.matmul(p_big[0:EE, s4 * 512:(s4 + 1) * 512],
                                 RTW[:, c * EE:(c + 1) * EE],
                                 XN0[:, c * T + s4 * 512:c * T + (s4 + 1) * 512],
                                 start=(c == 0), stop=(c == NCH - 1))
            kb.inc(i, "c2t", 1)
            t.wait_ge(kb.sem("c2s"), 1)
            t.wait_ge(kb.sem("oan"), OAN_ALL)
            for tc in range(NCH):
                i = t.transpose(p_m1[:, tc * 8:(tc + 1) * 8],
                                LG8[0:EE, tc * 128:(tc + 1) * 128], IDENTF[0:EE, 0:EE])
            kb.inc(i, "c2t", 1)
            t.wait_ge(kb.sem("vtk"), 2 * NH)
            for half in range(2):
                t.wait_ge(kb.sem("c2v"), half + 1)
                if half == 1:
                    t.wait_ge(kb.sem("c2v"), 3)
                for j in range(8):
                    tc = half * 8 + j
                    i = t.transpose(p_tr[0:8, j * 128:(j + 1) * 128],
                                    G2D[:, tc * 8:(tc + 1) * 8], IDENT[:])
                kb.inc(i, "c2t", 1)
            t.wait_ge(kb.sem("c2v"), 4)
            for s4 in range(4):
                i = t.matmul(p_big[:, s4 * 512:(s4 + 1) * 512], EMAP[0:EE, :],
                             GT8[0:EE, s4 * 512:(s4 + 1) * 512], start=True, stop=True)
            kb.inc(i, "c2t", 1)
            t.wait_ge(kb.sem("c2v"), 5)
            for c in range(NCH):
                for s4 in range(4):
                    i = t.matmul(p_big[:, s4 * 512:(s4 + 1) * 512], AAL[:, c * ER:(c + 1) * ER],
                                 XN0[:, c * T + s4 * 512:c * T + (s4 + 1) * 512],
                                 start=(c == 0), stop=(c == NCH - 1))
            kb.inc(i, "c2t", 1)
            t.wait_ge(kb.sem("c2v"), 6)
            for oc in range(NCH):
                kb.dw(t, "wp2", oc, 64)
                for th in range(2):
                    fi = 2 * oc + th
                    kb.w(t, "epv", fi - 1)
                    half = (fi % 2) * 1024
                    tb = th * TH
                    for c in range(NCH):
                        for s2 in range(2):
                            t.matmul(p_big[:, half + s2 * 512:half + (s2 + 1) * 512],
                                     WPA[oc % 2][:, c * 128:(c + 1) * 128],
                                     XN0[:, c * T + tb + s2 * 512:c * T + tb + (s2 + 1) * 512],
                                     start=(c == 0), stop=False)
                    for c in range(NCH):
                        for s2 in range(2):
                            t.matmul(p_big[:, half + s2 * 512:half + (s2 + 1) * 512],
                                     WPB[oc % 2][:, c * 128:(c + 1) * 128],
                                     OA[:, c * T + tb + s2 * 512:c * T + tb + (s2 + 1) * 512],
                                     start=False, stop=False)
                    for s2 in range(2):
                        i = t.matmul(p_big[:, half + s2 * 512:half + (s2 + 1) * 512],
                                     BAL[:, oc * 128:(oc + 1) * 128],
                                     MIDG[:, tb + s2 * 512:tb + (s2 + 1) * 512],
                                     start=False, stop=True)
                    kb.inc(i, "ept", 1)

        @block.scalar
        def _(s):
            s.wait_ge(kb.sem("c2t"), 1)
            kb.inc(s.activation(LG8[0:EE, :], p_big[0:EE, :], AF.Identity, bias=RBH[0:EE, 0:1]), "c2s", 1)
            s.wait_ge(kb.sem("c2v0"), 1)
            for tc in range(NCH):
                i = s.activation(EL8[:, tc * 8:(tc + 1) * 8], LGD[:, tc * 8:(tc + 1) * 8],
                                 AF.Exp, bias=NM1A[:, tc:tc + 1], scale=1.0)
            kb.inc(i, "c2s", 1)

        @block.sync
        def _(sync):
            for hc in range(32):
                if hc < 2:
                    sync.wait_ge(kb.sem("a0"), 32)
                else:
                    kb.w(sync, "epv", hc - 1)
                kb.dinc(sync.dma_start(out=XB[hc % 2][:], in_=xsrc(hc)), "xl2", hc)

        @block.scalar
        def _(s):
            for hc in range(32):
                c, th = hc // 2, hc % 2
                s.wait_ge(kb.sem("epv"), hc + 1)
                kb.dinc(s.dma_start(out=scr["x1"].ap()[c * 128:(c + 1) * 128, th * TH:(th + 1) * TH],
                                   in_=X1B[hc % 2][:]), "x1s", hc)

        @block.vector
        def _(v):
            v.wait_ge(kb.sem("c2t"), 2)
            kb.vop(v.tensor_copy(LGD[:], p_m1[:, 0:128]))
            for tc in range(NCH):
                sl = LGD[:, tc * 8:(tc + 1) * 8]
                kb.vwait(v)
                kb.vop(v.reduce_max(COLS[:, 0:1], sl, axis=AX.X))
                kb.vwait(v)
                i = v.tensor_scalar(NM1A[:, tc:tc + 1], COLS[:, 0:1], -1.0, None, AL.mult, AL.bypass)
                kb.vop(i)
            kb.vwait(v)
            kb.inc(v.tensor_copy(MRK[:, 0:1], CN2K[:, 0:1]), "c2v0", 1)
            v.wait_ge(kb.sem("c2s"), 2)
            for half in range(2):
                for j in range(8):
                    tc = half * 8 + j
                    sl = LGD[:, tc * 8:(tc + 1) * 8]
                    kb.vwait(v)
                    kb.vop(v.reduce_max(COLS[:, 0:1], sl, axis=AX.X))
                    kb.vwait(v)
                    kb.vop(v.tensor_single_scalar(TMP8[:], sl, COLS[:, 0:1], AL.is_equal))
                    kb.vwait(v)
                    kb.vop(v.scalar_tensor_tensor(TMP8[:], TMP8[:], -1e9, sl, AL.mult, AL.add))
                    kb.vwait(v)
                    kb.vop(v.reduce_max(COLS[:, 1:2], TMP8[:], axis=AX.X))
                    kb.vwait(v)
                    kb.vop(v.tensor_single_scalar(TMP8[:], sl, COLS[:, 1:2], AL.is_ge))
                    kb.vwait(v)
                    kb.vop(v.tensor_tensor(TMP8[:], TMP8[:], EL8[:, tc * 8:(tc + 1) * 8], AL.mult))
                    kb.vwait(v)
                    kb.vop(v.reduce_sum(COLS[:, 3:4], TMP8[:], axis=AX.X))
                    kb.vwait(v)
                    kb.vop(v.reciprocal(COLS[:, 4:5], COLS[:, 3:4]))
                    kb.vwait(v)
                    i = v.tensor_single_scalar(G2D[:, tc * 8:(tc + 1) * 8], TMP8[:], COLS[:, 4:5], AL.mult)
                    kb.vop(i)
                kb.vwait(v)
                kb.inc(v.tensor_copy(MRK[:, 1 + half:2 + half], CN2K[:, 0:1]), "c2v", 1)
            for half in range(2):
                v.wait_ge(kb.sem("c2t"), 3 + half)
                i = v.tensor_copy(GT8[0:EE, half * 1024:(half + 1) * 1024], p_tr[0:8, :])
                kb.inc(i, "c2v", 1)
            v.wait_ge(kb.sem("c2t"), 5)
            kb.inc(v.tensor_copy(GX[:], p_big[:]), "c2v", 1)
            v.wait_ge(kb.sem("c2t"), 6)
            kb.inc(v.scalar_tensor_tensor(MIDG[:], p_big[:], 2.0, GX[:], AL.mult, AL.mult), "c2v", 1)
            for hc in range(32):
                v.wait_ge(kb.sem("ept"), hc + 1)
                kb.dw(v, "xl2", hc)
                if hc >= 2:
                    kb.dw(v, "x1s", hc - 2)
                half = (hc % 2) * 1024
                oc = hc // 2
                i = v.scalar_tensor_tensor(X1B[hc % 2][:], p_big[:, half:half + 1024],
                                           BBH[:, oc:oc + 1], XB[hc % 2][:], AL.add, AL.add)
                kb.inc(i, "epv", 1)

    if upto >= 3:
        free("LG8", "LGD", "TMP8", "COLS", "MRK", "NM1A", "EL8", "G2D", "GT8", "GX", "MIDG")
        free("OA", "XB0", "XB1")
        S1A = sb("S1A3", [128, T], F32)
        S2A = sb("S2A3", [128, T], F32)
        SQT = sb("SQT3", [128, TH], F32)
        M1R = sb("M1R3", [128, T], F32)
        RSTD = sb("RSTD3", [128, T], F32)
        GZ = sb("GZ3", [128, T], F32)
        TR0 = sb("TR03", [128, TH], F32)
        TR1 = sb("TR13", [128, T], F32)
        XN1S = [sb(f"XN1S{i}", [128, TH], BF16) for i in range(2)]
        Z = XN0

        def x1src(hc):
            c, th = hc // 2, hc % 2
            return scr["x1"].ap()[c * 128:(c + 1) * 128, th * TH:(th + 1) * TH]

        @block.scalar
        def _(s):
            kb.dwfull(s, "x1s", 256, 256)
            kb.dinc(s.dma_start(out=X1B[0][:], in_=x1src(0)), "x1l3", 0)
            kb.dinc(s.dma_start(out=X1B[1][:], in_=x1src(1)), "x1l3", 1)
            for hc in range(32):
                kb.dw(s, "x1l3", hc)
                kb.w(s, "f3", 2 * hc)
                kb.inc(s.activation(SQT[:], X1B[hc % 2][:], AF.Square), "sq3", 1)
                if hc + 2 < 32:
                    kb.w(s, "f3", 2 * (hc + 1))
                    kb.dinc(s.dma_start(out=X1B[hc % 2][:], in_=x1src(hc + 2)), "x1l3", hc + 2)

        @block.vector
        def _(v):
            for hc in range(32):
                th = hc % 2
                kb.dw(v, "x1l3", hc)
                v.wait_ge(kb.sem("sq3"), hc + 1)
                if hc < 2:
                    i1 = v.tensor_copy(S1A[:, th * TH:(th + 1) * TH], X1B[hc % 2][:])
                    i2 = v.tensor_copy(S2A[:, th * TH:(th + 1) * TH], SQT[:])
                else:
                    i1 = v.tensor_add(S1A[:, th * TH:(th + 1) * TH], S1A[:, th * TH:(th + 1) * TH], X1B[hc % 2][:])
                    i2 = v.tensor_add(S2A[:, th * TH:(th + 1) * TH], S2A[:, th * TH:(th + 1) * TH], SQT[:])
                kb.inc(i1, "f3", 1)
                kb.inc(i2, "f3", 1)

        @block.gpsimd
        def _(g):
            dq.sc_loads(g, 3)

        @block.tensor
        def _(t):
            t.wait_ge(kb.sem("f3"), 64)
            t.wait_ge(kb.sem("epv"), 32)
            for s4 in range(4):
                i = t.matmul(p_big[:, s4 * 512:(s4 + 1) * 512], ONESF[:],
                             S1A[:, s4 * 512:(s4 + 1) * 512], start=True, stop=True)
            kb.inc(i, "c3t", 1)
            t.wait_ge(kb.sem("c3s"), 1)
            for s4 in range(4):
                i = t.matmul(p_big[:, s4 * 512:(s4 + 1) * 512], ONESF[:],
                             S2A[:, s4 * 512:(s4 + 1) * 512], start=True, stop=True)
            kb.inc(i, "c3t", 1)

        @block.scalar
        def _(s):
            s.wait_ge(kb.sem("c3t"), 1)
            kb.inc(s.activation(M1R[:], p_big[:], AF.Identity, scale=CN2K[:, 0:1]), "c3s", 1)
            s.wait_ge(kb.sem("c3t"), 2)
            kb.inc(s.activation(S2A[:], p_big[:], AF.Identity, scale=CN2K[:, 0:1]), "c3s", 1)
            s.wait_ge(kb.sem("c3v"), 1)
            kb.inc(s.activation(RSTD[:], TR1[:], AF.Sqrt), "c3s", 1)
            s.wait_ge(kb.sem("c3v"), 2)
            kb.inc(s.activation(GZ[:], TR1[:], AF.Sqrt), "c3s", 1)

        @block.vector
        def _(v):
            v.wait_ge(kb.sem("c3s"), 2)
            kb.vop(v.tensor_tensor(S1A[:], M1R[:], M1R[:], AL.mult))
            kb.vwait(v)
            kb.vop(v.tensor_tensor(S2A[:], S2A[:], S1A[:], AL.subtract))
            kb.vwait(v)
            kb.vop(v.tensor_scalar(S1A[:], S2A[:], 1e-5, None, AL.add, AL.bypass))
            kb.vwait(v)
            kb.inc(v.reciprocal(TR1[:], S1A[:]), "c3v", 1)
            v.wait_ge(kb.sem("c3s"), 3)
            kb.vwait(v)
            kb.w(v, "c3v", 1)
            kb.vop(v.tensor_tensor(S2A[:], S2A[:], TR1[:], AL.mult))
            kb.vwait(v)
            kb.vop(v.tensor_scalar(S2A[:], S2A[:], 1e-5, None, AL.add, AL.bypass))
            kb.vwait(v)
            kb.inc(v.reciprocal(TR1[:], S2A[:]), "c3v", 1)
            v.wait_ge(kb.sem("c3s"), 4)
            kb.vwait(v)
            kb.inc(v.tensor_tensor(GZ[:], GZ[:], RSTD[:], AL.mult), "c3v", 1)
            for hc in range(32):
                c, th = hc // 2, hc % 2
                kb.dw(v, "x1l3b", hc)
                kb.w(v, "p3z", hc)
                kb.vop(v.tensor_tensor(TR0[:], X1B[hc % 2][:], M1R[:, th * TH:(th + 1) * TH], AL.subtract))
                if hc >= 2:
                    kb.dw(v, "xn1s", hc - 2)
                kb.vwait(v)
                i = v.tensor_tensor(XN1S[hc % 2][:], TR0[:], RSTD[:, th * TH:(th + 1) * TH], AL.mult)
                kb.inc(i, "p3x", 1)
                i = v.scalar_tensor_tensor(Z[:, c * T + th * TH:c * T + (th + 1) * TH], TR0[:],
                                           RWH[:, c:c + 1], GZ[:, th * TH:(th + 1) * TH],
                                           AL.mult, AL.mult)
                kb.inc(i, "p3z", 1)
            dq.vec(v, 2)

        @block.scalar
        def _(s):
            for hc in range(32):
                if hc < 2:
                    s.wait_ge(kb.sem("f3"), 64)
                    s.wait_ge(kb.sem("sq3"), 32)
                else:
                    kb.w(s, "p3z", hc - 1)
                kb.dinc(s.dma_start(out=X1B[hc % 2][:], in_=x1src(hc)), "x1l3b", hc)

        @block.sync
        def _(sync):
            for hc in range(32):
                c, th = hc // 2, hc % 2
                sync.wait_ge(kb.sem("p3x"), hc + 1)
                kb.dinc(sync.dma_start(out=scr["xn1"].ap()[c * 128:(c + 1) * 128, th * TH:(th + 1) * TH],
                                      in_=XN1S[hc % 2][:]), "xn1s", hc)
            dq.pump_sync(sync, 24)

        free("S1A3", "S2A3", "SQT3", "M1R3", "TR03", "TR13", "WPB0", "WPB1", "RSTD3", "GZ3")

    if upto >= 4:
        WM = [[sb(f"WM{w}_{i}", [128, NCH * 128], BF16) for i in range(2)] for w in range(5)]
        AAc = sb("AAc", [128, T], F32)
        UUc = sb("UUc", [128, T], F32)
        HHc = sb("HHc", [128, T], F32)
        CAc = sb("CAc", [128, T], BF16)
        CCc = sb("CCc", [128, T], BF16)
        DDc = sb("DDc", [128, T], BF16)
        YB = [sb(f"YB{i}", [128, T], BF16) for i in range(2)]
        WKEYS = ["CawT", "CbwT", "AwT", "DwT", "BwT"]
        EVENG = ["s", "v", "s", "s", "v"]

        @block.gpsimd
        def _(g):
            for nn in range(NCH):
                if nn < 2:
                    g.wait_ge(kb.sem("ept"), 32)
                    g.wait_ge(kb.sem("p3z"), 32)
                else:
                    kb.w(g, "p4s", 6 * (nn - 1))
                    kb.w(g, "p4v", 4 * (nn - 1))
                for w in range(5):
                    for hf in range(2):
                        kb.inc(g.dma_start(
                            out=WM[w][nn % 2][:].rearrange("p (c o) -> p c o", o=128)[:, hf * 8:(hf + 1) * 8],
                            in_=ins[WKEYS[w]][nn, hf * 8:(hf + 1) * 8].transpose([1, 0, 2])), f"wm_{nn % 2}", 16)
                if nn >= 1:
                    dq.sc_loads(g, 3)

        @block.tensor
        def _(t):
            t.wait_ge(kb.sem("c3v"), 3)
            for nn in range(NCH):
                kb.dw(t, "wm", nn, 160)
                for w in range(5):
                    for th in range(2):
                        fi = 10 * nn + 2 * w + th
                        pf = fi - 2
                        if pf >= 0:
                            peng = EVENG[(pf % 10) // 2]
                            cnt = sum(1 for q in range(pf + 1) if EVENG[(q % 10) // 2] == peng)
                            kb.w(t, "p4" + peng, cnt)
                        half = (fi % 2) * 1024
                        for c in range(NCH):
                            if nn == 0 and w == 0 and th == 0:
                                t.wait_ge(kb.sem("p3z"), 2 * c + 2)
                            for s2 in range(2):
                                i = t.matmul(p_big[:, half + s2 * 512:half + (s2 + 1) * 512],
                                             WM[w][nn % 2][:, c * 128:(c + 1) * 128],
                                             Z[:, c * T + th * TH + s2 * 512:c * T + th * TH + (s2 + 1) * 512],
                                             start=(c == 0), stop=(c == NCH - 1))
                        kb.inc(i, "p4t", 1)

        @block.scalar
        def _(s):
            for nn in range(NCH):
                for w, th, dst, func, bias in [
                        (0, 0, CAc, AF.Silu, None), (0, 1, CAc, AF.Silu, None),
                        (2, 0, AAc, AF.Sigmoid, ABH), (2, 1, AAc, AF.Sigmoid, ABH),
                        (3, 0, DDc, AF.Sigmoid, None), (3, 1, DDc, AF.Sigmoid, None)]:
                    fi = 10 * nn + 2 * w + th
                    s.wait_ge(kb.sem("p4t"), fi + 1)
                    if nn > 0:
                        kb.w(s, "scn", nn)
                        kb.w(s, "p4y", nn)
                    half = (fi % 2) * 1024
                    kw = {}
                    if bias is not None:
                        kw["bias"] = bias[:, nn:nn + 1]
                    i = s.activation(dst[:, th * 1024:(th + 1) * 1024], p_big[:, half:half + 1024],
                                     func, **kw)
                    kb.inc(i, "p4s", 1)
                if nn >= 1:
                    s.wait_ge(kb.sem("p4y"), nn)
                    kb.dinc(s.dma_start(out=scr["y"].ap()[(nn - 1) * 128:nn * 128, :],
                                        in_=YB[(nn - 1) % 2][:]), "ys", nn - 1)

        @block.vector
        def _(v):
            for nn in range(NCH):
                if nn > 0:
                    kb.w(v, "scn", nn)
                for w, th in [(1, 0), (1, 1), (4, 0), (4, 1)]:
                    fi = 10 * nn + 2 * w + th
                    v.wait_ge(kb.sem("p4t"), fi + 1)
                    half = (fi % 2) * 1024
                    if w == 1:
                        v.wait_ge(kb.sem("p4s"), 6 * nn + th + 1)
                        i = v.scalar_tensor_tensor(CCc[:, th * 1024:(th + 1) * 1024],
                                                   p_big[:, half:half + 1024], 1.0,
                                                   CAc[:, th * 1024:(th + 1) * 1024], AL.mult, AL.mult)
                    else:
                        v.wait_ge(kb.sem("p4s"), 6 * nn + 4 + th + 1)
                        i = v.scalar_tensor_tensor(UUc[:, th * 1024:(th + 1) * 1024],
                                                   p_big[:, half:half + 1024], 1.0,
                                                   DDc[:, th * 1024:(th + 1) * 1024], AL.mult, AL.mult)
                    kb.inc(i, "p4v", 1)
                v.wait_ge(kb.sem("p4s"), 6 * nn + 4)
                kb.w(v, "p4v", 4 * nn + 4)
                i = v.tensor_tensor_scan(HHc[:], AAc[:], UUc[:], 0.0, AL.mult, AL.add)
                kb.inc(i, "scn", 1)
                if nn >= 2:
                    kb.dw(v, "ys", nn - 2)
                kb.w(v, "scn", nn + 1)
                i = v.scalar_tensor_tensor(YB[nn % 2][:], HHc[:], 1.0, CCc[:], AL.mult, AL.mult)
                kb.inc(i, "p4y", 1)
                dq.vec(v, 3)

        @block.scalar
        def _(s):
            s.wait_ge(kb.sem("p4y"), NCH)
            kb.dinc(s.dma_start(out=scr["y"].ap()[(NCH - 1) * 128:NCH * 128, :],
                                in_=YB[(NCH - 1) % 2][:]), "ys", NCH - 1)

        @block.sync
        def _(sync):
            dq.pump_sync(sync, 30)

    if upto >= 5:
        free("AAc", "UUc", "HHc", "CAc", "CCc", "DDc",
             *[f"WM{w}_{i}" for w in range(5) for i in range(2)])
        YTl = [sb(f"YT{c}", [128, T], BF16) for c in range(NCH)]
        P1S = [sb(f"P1S{i}", [128, TH], F32) for i in range(2)]

        @block.sync
        def _(sync):
            sync.wait_ge(kb.sem("p4y"), NCH)
            for c in range(NCH):
                kb.dw(sync, "ys", c)
                kb.inc(sync.dma_start(out=YTl[c][:],
                                      in_=scr["y"].ap()[c * 128:(c + 1) * 128, :]), "yl", 16)
            sync.wait_ge(kb.sem("p4t"), 160)
            kb.dwfull(sync, "xn1s", 256, 256)
            for c in range(NCH):
                kb.inc(sync.dma_start(out=XN0[:, c * T:(c + 1) * T],
                                      in_=scr["xn1"].ap()[c * 128:(c + 1) * 128, :]), "xn1l", 16)

        @block.gpsimd
        def _(g):
            for oc in range(NCH):
                if oc < 2:
                    g.wait_ge(kb.sem("ept"), 2 * NCH)
                else:
                    kb.w(g, "p5t", 2 * (oc - 1))
                for hf in range(2):
                    kb.inc(g.dma_start(
                        out=WPA[oc % 2][:].rearrange("p (c o) -> p c o", o=128)[:, hf * 8:(hf + 1) * 8],
                        in_=ins["OwT"][oc, hf * 8:(hf + 1) * 8].transpose([1, 0, 2])), f"wo5_{oc % 2}", 16)
                dq.sc_loads(g, 2)

        @block.tensor
        def _(t):
            t.wait_ge(kb.sem("p4s"), 96)
            t.wait_ge(kb.sem("p4v"), 64)
            t.wait_ge(kb.sem("yl"), 16 * NCH)
            for oc in range(NCH):
                kb.dw(t, "wo5", oc, 32)
                for th in range(2):
                    fi = 2 * oc + th
                    kb.w(t, "p5v", fi - 1)
                    half = (fi % 2) * 1024
                    tb = th * TH
                    for c in range(NCH):
                        for s2 in range(2):
                            i = t.matmul(p_big[:, half + s2 * 512:half + (s2 + 1) * 512],
                                         WPA[oc % 2][:, c * 128:(c + 1) * 128],
                                         YTl[c][:, tb + s2 * 512:tb + (s2 + 1) * 512],
                                         start=(c == 0), stop=(c == NCH - 1))
                    kb.inc(i, "p5t", 1)

        @block.scalar
        def _(s):
            for hc in range(32):
                c, th = hc // 2, hc % 2
                if hc < 2:
                    kb.dwfull(s, "xn1s", 256, 256)
                    s.wait_ge(kb.sem("p3z"), 32)
                else:
                    kb.w(s, "p5v", hc - 1)
                kb.dinc(s.dma_start(out=XN1S[hc % 2][:],
                                    in_=scr["xn1"].ap()[c * 128:(c + 1) * 128, th * TH:(th + 1) * TH]),
                        "p5l", hc)
                kb.dinc(s.dma_start(out=X1B[hc % 2][:], in_=x1src(hc)), "p5l", hc)
                if hc >= 2:
                    ph, pc, pt = hc - 2, (hc - 2) // 2, (hc - 2) % 2
                    kb.dinc(s.dma_start(out=scr["p1"].ap()[pc * 128:(pc + 1) * 128, pt * TH:(pt + 1) * TH],
                                        in_=P1S[ph % 2][:]), "p1st", ph)

        @block.vector
        def _(v):
            for hc in range(32):
                oc = hc // 2
                v.wait_ge(kb.sem("p5t"), hc + 1)
                kb.dw(v, "p5l", hc, 32)
                if hc >= 2:
                    kb.dw(v, "p1st", hc - 2)
                half = (hc % 2) * 1024
                kb.vop(v.scalar_tensor_tensor(P1S[hc % 2][:], p_big[:, half:half + 1024],
                                              OBH[:, oc:oc + 1], XN1S[hc % 2][:], AL.add, AL.add))
                kb.vwait(v)
                i = v.tensor_add(P1S[hc % 2][:], P1S[hc % 2][:], X1B[hc % 2][:])
                kb.inc(i, "p5v", 1)
                dq.vec(v, 1)

        @block.scalar
        def _(s):
            for hc in range(30, 32):
                c, th = hc // 2, hc % 2
                s.wait_ge(kb.sem("p5v"), hc + 1)
                kb.dinc(s.dma_start(out=scr["p1"].ap()[c * 128:(c + 1) * 128, th * TH:(th + 1) * TH],
                                    in_=P1S[hc % 2][:]), "p1st", hc)


        DQST = dq.nu * 16

    if upto >= 6:
        free("X1B0", "X1B1", "XN1S0", "XN1S1", "P1S0", "P1S1", "YB0", "YB1",
             "dq_pk0", "dq_pk1", "dq_sc0", "dq_sc1", "dq_out",
             "dq_t0", "dq_t1", "dq_t2", "dq_t3",
             "WPA0", "WPA1", *[f"YT{c}" for c in range(NCH)])
        XN1R = XN0
        HQ = sb("HQ", [128, NGC * 512], BF16)
        WG6 = [sb(f"WG6{i}", [128, NCH * 256], BF16) for i in range(2)]
        WGU6 = [sb(f"WGU6{i}", [128, NCH * 256], BF16) for i in range(2)]
        WD6 = [sb(f"WD6{i}", [128, 11 * 512], BF16) for i in range(2)]
        SG6 = sb("SG6", [128, 512], BF16)
        YF = sb("YF", [128, NCH * 512], BF16)
        SQ6 = sb("SQ6", [128, 512], F32)
        S2F = sb("S2F", [128, 512], F32)
        GF = sb("GF", [128, 512], F32)
        P1B = [sb(f"P1B{i}", [128, 512], F32) for i in range(2)]
        X2B = [sb(f"X2B{i}", [128, 512], F32) for i in range(2)]

        @block.sync
        def _(sync):
            sync.wait_ge(kb.sem("p5t"), 2 * NCH)
            kb.dwfull(sync, "p1st", 256, 256)
            sync.wait_ge(kb.sem("dq_st"), DQST)
            for tq in range(4):
                for g2 in range(22):
                    li = tq * 22 + g2
                    if tq > 0 and 2 <= g2 <= 17:
                        sdc = g2 - 2
                        sli = (tq - 1) * 16 + sdc
                        sync.wait_ge(kb.sem("p6x2"), sli + 1)
                        kb.dinc(sync.dma_start(out=out[sdc * 128:(sdc + 1) * 128, (tq - 1) * 512:tq * 512],
                                               in_=X2B[sdc % 2][:]), "outs", sli)
                    kb.w(sync, "hqv", (li - 1) * 2)
                    for hf in range(2):
                        kb.inc(sync.dma_start(
                            out=WG6[li % 2][:].rearrange("p (c o) -> p c o", o=256)[:, hf * 8:(hf + 1) * 8],
                            in_=scr["wgu"].ap().rearrange("(c p) o -> p c o", p=128)[
                                :, hf * 8:(hf + 1) * 8, g2 * 256:(g2 + 1) * 256]), f"wg6_{li % 2}", 16)
                        kb.inc(sync.dma_start(
                            out=WGU6[li % 2][:].rearrange("p (c o) -> p c o", o=256)[:, hf * 8:(hf + 1) * 8],
                            in_=scr["wgu"].ap().rearrange("(c p) o -> p c o", p=128)[
                                :, hf * 8:(hf + 1) * 8, IH + g2 * 256:IH + (g2 + 1) * 256]), f"wg6_{li % 2}", 16)
                for dg in range(4):
                    for kq in range(4):
                        di = tq * 16 + dg * 4 + kq
                        kb.w(sync, "p6dk", di - 1)
                        kb.inc(sync.dma_start(
                            out=WD6[di % 2][:].rearrange("p (c o) -> p c o", o=512),
                            in_=scr["wdn"].ap().rearrange("(c p) o -> p c o", p=128)[
                                :, kq * 11:(kq + 1) * 11, dg * 512:(dg + 1) * 512]), f"wd6_{di % 2}", 16)

        @block.tensor
        def _(t):
            t.wait_ge(kb.sem("xn1l"), 16 * NCH)
            for tq in range(4):
                tb = tq * 512
                for g2 in range(22):
                    li = tq * 22 + g2
                    kb.dw(t, "wg6", li, 64)
                    for j in range(2):
                        gi6 = li * 2 + j
                        kb.w(t, "p6gs", gi6)
                        kb.w(t, "hqv", gi6)
                        for c in range(NCH):
                            i = t.matmul(p_m1[:, 0:512],
                                         WG6[li % 2][:, c * 256 + j * 128:c * 256 + (j + 1) * 128],
                                         XN1R[:, c * T + tb:c * T + tb + 512],
                                         start=(c == 0), stop=(c == NCH - 1))
                        kb.inc(i, "p6g", 1)
                        for c in range(NCH):
                            i = t.matmul(p_m2[:, 0:512],
                                         WGU6[li % 2][:, c * 256 + j * 128:c * 256 + (j + 1) * 128],
                                         XN1R[:, c * T + tb:c * T + tb + 512],
                                         start=(c == 0), stop=(c == NCH - 1))
                        kb.inc(i, "p6u", 1)
                for dg in range(4):
                    for kq in range(4):
                        di = tq * 16 + dg * 4 + kq
                        kb.dw(t, "wd6", di, 16)
                        if kq == 0:
                            t.wait_ge(kb.sem("hqv"), 44 * (tq + 1))
                            for dc4 in range(4):
                                fli = tq * 16 + dg * 4 + dc4
                                kb.w(t, "p6sq", fli - 3)
                        for ch in range(11):
                            for dc4 in range(4):
                                i = t.matmul(p_big[:, dc4 * 512:(dc4 + 1) * 512],
                                             WD6[di % 2][:, ch * 512 + dc4 * 128:ch * 512 + dc4 * 128 + 128],
                                             HQ[:, (kq * 11 + ch) * 512:(kq * 11 + ch + 1) * 512],
                                             start=(kq == 0 and ch == 0), stop=(kq == 3 and ch == 10))
                        kb.inc(i, "p6dk", 1)
                t.wait_ge(kb.sem("p6sf"), 16 * (tq + 1))
                kb.inc(t.matmul(p_st[:, 0:512], ONESF[:], S2F[:], start=True, stop=True),
                       "p6st", 1)

        @block.scalar
        def _(s):
            kb.dwfull(s, "p1st", 256, 256)
            kb.dinc(s.dma_start(out=P1B[0][:], in_=scr["p1"].ap()[0:128, 0:512]), "p1l6", 0)
            kb.dinc(s.dma_start(out=P1B[1][:], in_=scr["p1"].ap()[128:256, 0:512]), "p1l6", 1)
            for tq in range(4):
                for og in range(44):
                    gi6 = tq * 44 + og
                    if tq > 0 and og % 2 == 0 and 4 <= og <= 34:
                        pli = (tq - 1) * 16 + og // 2
                        pdc, ptq = pli % 16, pli // 16
                        kb.w(s, "p6x2", pli - 1)
                        kb.dinc(s.dma_start(out=P1B[pdc % 2][:],
                                            in_=scr["p1"].ap()[pdc * 128:(pdc + 1) * 128,
                                                               ptq * 512:(ptq + 1) * 512]),
                                "p1l6", pli)
                    s.wait_ge(kb.sem("p6g"), gi6 + 1)
                    kb.w(s, "hqv", gi6)
                    kb.inc(s.activation(SG6[:], p_m1[:, 0:512], AF.Silu), "p6gs", 1)
                for dgi in range(4):
                    s.wait_ge(kb.sem("p6dk"), tq * 16 + (dgi + 1) * 4)
                    for dc4 in range(4):
                        fli = tq * 16 + dgi * 4 + dc4
                        i = s.activation(YF[:, (dgi * 4 + dc4) * 512:(dgi * 4 + dc4 + 1) * 512],
                                         p_big[:, dc4 * 512:(dc4 + 1) * 512], AF.Copy)
                        kb.inc(i, "p6ys", 1)
                        kb.w(s, "p6sf", fli)
                        i = s.activation(SQ6[:], p_big[:, dc4 * 512:(dc4 + 1) * 512], AF.Square)
                        kb.inc(i, "p6sq", 1)
                s.wait_ge(kb.sem("p6v2"), tq + 1)
                kb.w(s, "p6x2", 16 * tq)
                kb.inc(s.activation(GF[:], S2F[:], AF.Sqrt), "p6s2", 1)
            for li in range(3 * 16 + 2, 64):
                dc = li % 16
                kb.w(s, "p6x2", li - 1)
                kb.dinc(s.dma_start(out=P1B[dc % 2][:],
                                    in_=scr["p1"].ap()[dc * 128:(dc + 1) * 128, 3 * 512:4 * 512]),
                        "p1l6", li)

        @block.vector
        def _(v):
            def x2_chunk(li):
                dc, ltq = li % 16, li // 16
                kb.dw(v, "p1l6", li)
                if li >= 2:
                    kb.dw(v, "outs", li - 2)
                v.wait_ge(kb.sem("p6s2"), ltq + 1)
                kb.vwait(v)
                kb.vop(v.scalar_tensor_tensor(X2B[dc % 2][:], YF[:, dc * 512:(dc + 1) * 512], 1.0,
                                              GF[:], AL.mult, AL.mult))
                kb.vwait(v)
                i = v.tensor_add(X2B[dc % 2][:], X2B[dc % 2][:], P1B[dc % 2][:])
                kb.inc(i, "p6x2", 1)

            for tq in range(4):
                for og in range(44):
                    gi6 = tq * 44 + og
                    if og == 0 and tq > 0:
                        v.wait_ge(kb.sem("p6dk"), 16 * tq)
                    v.wait_ge(kb.sem("p6gs"), gi6 + 1)
                    v.wait_ge(kb.sem("p6u"), gi6 + 1)
                    i = v.scalar_tensor_tensor(HQ[:, og * 512:(og + 1) * 512],
                                               p_m2[:, 0:512], 1.0, SG6[:], AL.mult, AL.mult)
                    kb.inc(i, "hqv", 1)
                    if tq > 0 and og % 2 == 1 and og < 32:
                        x2_chunk((tq - 1) * 16 + og // 2)
                for fi16 in range(16):
                    v.wait_ge(kb.sem("p6sq"), tq * 16 + fi16 + 1)
                    if fi16 == 0:
                        kb.w(v, "p6s2", tq)
                        i = v.tensor_copy(S2F[:], SQ6[:])
                    else:
                        i = v.tensor_add(S2F[:], S2F[:], SQ6[:])
                    kb.inc(i, "p6sf", 1)
                v.wait_ge(kb.sem("p6st"), tq + 1)
                kb.vwait(v)
                kb.vop(v.tensor_scalar(S2F[:], p_st[:, 0:512], 1.0 / D, 1e-6, AL.mult, AL.add))
                kb.vwait(v)
                kb.inc(v.reciprocal(S2F[:], S2F[:]), "p6v2", 1)
            for li in range(3 * 16, 64):
                x2_chunk(li)


        @block.sync
        def _(sync):
            for dc in range(NCH):
                li = 3 * 16 + dc
                sync.wait_ge(kb.sem("p6x2"), li + 1)
                kb.dinc(sync.dma_start(out=out[dc * 128:(dc + 1) * 128, 3 * 512:4 * 512],
                                       in_=X2B[dc % 2][:]), "outs", li)
            kb.dwfull(sync, "outs", 16 * 32, 16 * 32)

    while dq.st_i < dq.nu or dq.vec_i < dq.nu or dq.sc_i < dq.nu or dq.ld_i < dq.nu:
        @block.gpsimd
        def _(g):
            dq.sc_loads(g, 6)

        @block.vector
        def _(v):
            dq.vec(v, 6)

        @block.sync
        def _(sync):
            dq.pump_sync(sync, 6)

    @block.sync
    def _(sync):
        for nm in sorted(kb.dma_sems):
            sync.wait_ge(kb.sem(nm), kb.cnt[nm])

    blockg.__exit__(None, None, None)
    for g in kb._semctx[::-1]:
        g.__exit__(None, None, None)
    return nc


_CACHE = {}


def kernel(**inputs):
    if "nc" not in _CACHE:
        _CACHE["nc"] = build_program()
    nc = _CACHE["nc"]
    shared = stage_shared(inputs)
    x = np.asarray(inputs["L0_attn_x"])
    in_maps = []
    for c in range(B):
        m = dict(shared)
        m["xT"] = np.ascontiguousarray(x[c].T[SIGD])
        in_maps.append(m)
    res = run_bass_kernel_spmd(nc, in_maps, core_ids=list(range(B)))
    outp = np.empty((B, T, D), np.float32)
    for c in range(B):
        x2T = res.results[c]["x2T"]
        fm = np.empty((D, T), np.float32)
        fm[SIGD] = x2T
        outp[c] = fm.T
    return outp


# version 04
# speedup vs baseline: 1.3024x; 1.3024x over previous
import numpy as np
import ml_dtypes
import concourse.bass as bass
import concourse.mybir as mybir
from concourse.bass_utils import run_bass_kernel_spmd

F32 = mybir.dt.float32
BF16 = mybir.dt.bfloat16
F16 = mybir.dt.float16
I16 = mybir.dt.int16
U16 = mybir.dt.uint16
AL = mybir.AluOpType
AF = mybir.ActivationFunctionType
AX = mybir.AxisListType

B = 8
D = 2048
T = 2048
TH = 1024
NCH = 16
NH = 16
HD = 128
EE = 8
ER = 128
IH = 5632
GUO = 11264
NGC = 44
RSQ_HD = float(1.0 / np.sqrt(128.0))
EXP_BIAS = -4.0
BF = np.dtype(ml_dtypes.bfloat16)

_p = np.arange(D)
SIGD = 4 * (_p % 512) + _p // 512
_q = np.arange(IH)
SIGH = 4 * (_q % 1408) + _q // 1408



def _tile_chunks(wT):
    r = np.empty((NCH, NCH, 128, 128), np.float32)
    for oc in range(NCH):
        r[oc] = wT[:, oc * 128:(oc + 1) * 128].reshape(NCH, 128, 128)
    return np.ascontiguousarray(r)


def stage_shared(inputs):
    g = {}
    sd, sh = SIGD, SIGH
    wq = np.ascontiguousarray(inputs["L0_attn_w_project"]).T[sd]
    t = np.empty((NH, NCH, 128, 384), np.float32)
    for h in range(NH):
        cols = np.concatenate([wq[:, k * D + h * HD:k * D + (h + 1) * HD] for k in range(3)], axis=1)
        t[h] = cols.reshape(NCH, 128, 384)
    g["wqkv"] = np.ascontiguousarray(t)
    g["woT"] = _tile_chunks(np.ascontiguousarray(inputs["L0_attn_w_out"]).T[:, sd])
    g["baseT"] = _tile_chunks(np.ascontiguousarray(inputs["L0_ffn_base_w"]).T[sd][:, sd])
    g["base_b"] = np.ascontiguousarray(np.asarray(inputs["L0_ffn_base_b"])[sd].reshape(NCH, 128).T)
    g["routerT"] = np.ascontiguousarray(
        np.asarray(inputs["L0_ffn_router_w"]).T[sd].reshape(NCH, 128, EE).transpose(1, 0, 2).reshape(128, NCH * EE))
    g["router_b"] = np.ascontiguousarray(np.asarray(inputs["L0_ffn_router_b"]).reshape(EE, 1))
    g["aallT"] = np.ascontiguousarray(
        np.asarray(inputs["L0_ffn_lora_A"]).transpose(1, 0, 2).reshape(D, ER)[sd]
        .reshape(NCH, 128, ER).transpose(1, 0, 2).reshape(128, NCH * ER))
    g["ballT"] = np.ascontiguousarray(np.asarray(inputs["L0_ffn_lora_B"]).reshape(ER, D)[:, sd])
    g["rms_w"] = np.ascontiguousarray(np.asarray(inputs["L1_attn_rms_w"])[sd].reshape(NCH, 128).T)
    for nm, key in [("AwT", "L1_attn_A_w"), ("BwT", "L1_attn_B_w"), ("CawT", "L1_attn_Ca_w"),
                    ("CbwT", "L1_attn_Cb_w"), ("DwT", "L1_attn_D_w")]:
        g[nm] = _tile_chunks(np.ascontiguousarray(inputs[key]).T[sd])
    g["A_b"] = np.ascontiguousarray(np.asarray(inputs["L1_attn_A_b"]).reshape(NCH, 128).T)
    g["OwT"] = _tile_chunks(np.ascontiguousarray(inputs["L1_attn_out_w"]).T[:, sd])
    g["out_b"] = np.ascontiguousarray(np.asarray(inputs["L1_attn_out_b"])[sd].reshape(NCH, 128).T)
    rowp = np.concatenate([sh, IH + sh])
    gu16 = np.ascontiguousarray(np.asarray(inputs["L1_ffn_gate_up_packed"]).view(np.int16)[rowp].T)
    g["gu16"] = np.ascontiguousarray(gu16.reshape(4, 128, GUO))
    gsc = np.asarray(inputs["L1_ffn_gate_up_scales"])[:, rowp]
    g["gusc"] = np.ascontiguousarray(np.repeat(gsc, 32, axis=0).reshape(4, 128, GUO))
    dn16 = np.ascontiguousarray(np.asarray(inputs["L1_ffn_down_packed"])[sd].view(np.int16).T)
    g["dn16"] = np.ascontiguousarray(dn16.reshape(11, 128, D))
    dsc = np.asarray(inputs["L1_ffn_down_scales"])[:, sd]
    g["dnsc"] = np.ascontiguousarray(np.repeat(dsc, 32, axis=0).reshape(11, 128, D))
    return g



_DTSIZE = {F32: 4, BF16: 2, F16: 2, I16: 2, U16: 2}


class KB:

    def __init__(self, nc):
        self.nc = nc
        self.sems = {}
        self.cnt = {}
        self.guards = {}
        self._semctx = []
        self.dma_sems = set()
        base = (nc.sbuf_base + 31) // 32 * 32
        self.freelist = [(base, nc.sbuf_top)]
        self.live = {}
        self.hwm = 0

    def sem(self, name):
        if name not in self.sems:
            g = self.nc.semaphore(name)
            h = g.__enter__()
            self._semctx.append(g)
            self.sems[name] = h
            self.cnt[name] = 0
        return self.sems[name]

    def inc(self, instr, name, k=1):
        instr.then_inc(self.sem(name), k)
        self.cnt[name] += k
        if k >= 16 and k % 16 == 0:
            self.dma_sems.add(name)
        return self.cnt[name]

    def n(self, name):
        self.sem(name)
        return self.cnt[name]

    def w(self, eng, name, val):
        if val > 0:
            eng.wait_ge(self.sem(name), val)

    def dinc(self, instr, name, idx, k=16):
        self.inc(instr, f"{name}_{idx % 2}", k)

    def dw(self, eng, name, idx, k=16):
        eng.wait_ge(self.sem(f"{name}_{idx % 2}"), k * (idx // 2 + 1))

    def dwfull(self, eng, name, total0, total1):
        self.w(eng, f"{name}_0", total0)
        self.w(eng, f"{name}_1", total1)

    def vop(self, instr, name="vch"):
        return self.inc(instr, name, 1)

    def vwait(self, eng, name="vch"):
        self.w(eng, name, self.n(name))

    def sb(self, name, shape, dt):
        nb = 1
        for s in shape[1:]:
            nb *= s
        nb *= _DTSIZE[dt]
        nb = (nb + 31) // 32 * 32
        for idx, (lo, hi) in enumerate(self.freelist):
            if hi - lo >= nb:
                self.freelist[idx] = (lo + nb, hi)
                if lo + nb == hi:
                    self.freelist.pop(idx)
                self.live[name] = (lo, lo + nb)
                used = sum(b - a for (a, b) in (v for v in self.live.values()))
                self.hwm = max(self.hwm, used)
                return self.nc.alloc_sbuf_tensor_at(name, shape, dt, offset=lo)
        raise MemoryError(f"arena OOM for {name} need {nb} free={self.freelist}")

    def free(self, *names):
        for n in names:
            lo, hi = self.live.pop(n)
            self.freelist.append((lo, hi))
        self.freelist.sort()
        merged = []
        for lo, hi in self.freelist:
            if merged and merged[-1][1] >= lo:
                merged[-1] = (merged[-1][0], max(merged[-1][1], hi))
            else:
                merged.append((lo, hi))
        self.freelist = [tuple(x) for x in merged]

    def ps(self, name, shape, dt=F32):
        g = self.nc.psum_tensor(name, shape, dt)
        t = g.__enter__()
        self.guards[name] = g
        return t


class DQ:

    def __init__(self, kb, ins, scr):
        self.kb = kb
        self.units = [("gu", wb, sl) for wb in range(4) for sl in range(22)] + \
                     [("dn", wb, sl) for wb in range(11) for sl in range(4)]
        self.nu = len(self.units)
        self.ld_i = self.sc_i = self.vec_i = self.st_i = 0
        self.ins = ins
        self.scr = scr
        self.pk = [kb.sb(f"dq_pk{i}", [128, 512], I16) for i in range(2)]
        self.sc = [kb.sb(f"dq_sc{i}", [128, 512], F32) for i in range(2)]
        self.out = kb.sb("dq_out", [128, 4 * 512], BF16)
        self.t0 = kb.sb("dq_t0", [128, 512], I16)
        self.t1 = kb.sb("dq_t1", [128, 512], I16)
        self.t2 = kb.sb("dq_t2", [128, 512], I16)
        self.t3 = kb.sb("dq_t3", [128, 512], I16)

    def _src(self, u):
        kind, wb, sl = u
        if kind == "gu":
            return (self.ins["gu16"][wb, :, sl * 512:(sl + 1) * 512],
                    self.ins["gusc"][wb, :, sl * 512:(sl + 1) * 512])
        return (self.ins["dn16"][wb, :, sl * 512:(sl + 1) * 512],
                self.ins["dnsc"][wb, :, sl * 512:(sl + 1) * 512])

    def pump_sync(self, eng, n, lag=1):
        kb = self.kb
        for _ in range(n):
            progressed = False
            while self.st_i < self.nu and (self.st_i <= self.ld_i - lag or self.ld_i >= self.nu):
                self._store(eng)
                progressed = True
            if self.ld_i < self.nu:
                i = self.ld_i
                kb.w(eng, "dq_v", (i - 1) * 4)
                pk_src, _ = self._src(self.units[i])
                kb.dinc(eng.dma_start(out=self.pk[i % 2][:], in_=pk_src), "dq_pk", i)
                self.ld_i += 1
                progressed = True
            if not progressed:
                break

    def _store(self, eng):
        kb = self.kb
        i = self.st_i
        eng.wait_ge(kb.sem("dq_v"), (i + 1) * 4)
        kind, wb, sl = self.units[i]
        scrt = self.scr["wgu"] if kind == "gu" else self.scr["wdn"]
        ncols = GUO if kind == "gu" else D
        R = 4 if kind == "gu" else 11
        off = wb * 128 * ncols + sl * 512
        dst = bass.AP(scrt, off, [[ncols, 128], [R * 128 * ncols, 4], [1, 512]])
        srcap = self.out[:].rearrange("p (j o) -> p j o", j=4)
        kb.inc(eng.dma_start(out=dst, in_=srcap), "dq_st", 16)
        self.st_i += 1

    def sc_loads(self, eng, n):
        kb = self.kb
        for _ in range(n):
            if self.sc_i >= self.nu:
                return
            i = self.sc_i
            kb.w(eng, "dq_v", (i - 1) * 4)
            _, sc_src = self._src(self.units[i])
            kb.dinc(eng.dma_start(out=self.sc[i % 2][:], in_=sc_src), "dq_sc", i)
            self.sc_i += 1

    def vec(self, v, n):
        kb = self.kb
        for _ in range(n):
            if self.vec_i >= self.nu:
                return
            i = self.vec_i
            kb.dw(v, "dq_pk", i)
            kb.dw(v, "dq_sc", i)
            kb.w(v, "dq_st", 16 * i)
            W16, S = self.pk[i % 2], self.sc[i % 2]
            O = self.out
            for jp in range(4):
                sh = 4 * jp - 6
                kb.w(v, "dq_v", 4 * i + jp)
                kb.vwait(v)
                if sh < 0:
                    kb.vop(v.tensor_scalar(self.t0[:], W16[:], -sh, 0x1C0, AL.logical_shift_left, AL.bitwise_and))
                else:
                    kb.vop(v.tensor_scalar(self.t0[:], W16[:], sh, 0x1C0, AL.logical_shift_right, AL.bitwise_and))
                kb.vop(v.tensor_scalar(self.t1[:], W16[:], 4 * jp + 3, 15, AL.logical_shift_right, AL.logical_shift_left))
                kb.vwait(v)
                kb.vop(v.tensor_single_scalar(self.t2[:], self.t0[:], 64, AL.is_ge))
                kb.vwait(v)
                kb.vop(v.scalar_tensor_tensor(self.t2[:], self.t2[:], 16064, self.t0[:], AL.mult, AL.add))
                kb.vop(v.tensor_single_scalar(self.t3[:], self.t0[:], 128, AL.is_ge))
                kb.vwait(v)
                kb.vop(v.scalar_tensor_tensor(self.t2[:], self.t3[:], 64, self.t2[:], AL.mult, AL.add))
                kb.vwait(v)
                kb.vop(v.tensor_tensor(self.t3[:], self.t2[:], self.t1[:], AL.bitwise_or))
                kb.vwait(v)
                i2 = v.tensor_tensor(O[:, jp * 512:(jp + 1) * 512], self.t3[:].bitcast(BF16), S[:], AL.mult)
                kb.inc(i2, "dq_v", 1)
            self.vec_i += 1




def _att_sched():
    groups = []
    for qq in range(4):
        kts = list(range(4 * qq + 4))
        for i in range(0, len(kts), 2):
            rec = []
            for kt in kts[i:i + 2]:
                dt = kt - 4 * qq
                qs = max(0, 128 * dt)
                rec.append((kt, qs, dt if 0 <= dt <= 3 else None))
            groups.append((qq, rec))
    return groups


ATT_GROUPS = _att_sched()
ATT_NG = len(ATT_GROUPS)
ATT_NEXP = [1 if all(qs == 0 for (_, qs, _) in rec) else len(rec) for (_, rec) in ATT_GROUPS]
ATT_CUMEXP = np.cumsum(ATT_NEXP).tolist()
ATT_NMASK = [sum(1 for (_, _, dg) in rec if dg is not None) for (_, rec) in ATT_GROUPS]
ATT_CUMMASK = np.cumsum(ATT_NMASK).tolist()
ATT_QLAST = {}
for _gi, (_qq, _) in enumerate(ATT_GROUPS):
    ATT_QLAST[_qq] = _gi
ATT_TOTEXP = ATT_CUMEXP[-1]
ATT_TOTMASK = ATT_CUMMASK[-1]


def build_program(upto=99):
    nc = bass.Bass("TRN2", target_bir_lowering=False)
    ins = {}
    ins["xT"] = nc.dram_tensor("xT", [D, T], F32, kind="ExternalInput").ap()
    ins["wqkv"] = nc.dram_tensor("wqkv", [NH, NCH, 128, 384], F32, kind="ExternalInput").ap()
    for nm in ["woT", "baseT", "OwT", "AwT", "BwT", "CawT", "CbwT", "DwT"]:
        ins[nm] = nc.dram_tensor(nm, [NCH, NCH, 128, 128], F32, kind="ExternalInput").ap()
    for nm, shape in [("base_b", [128, NCH]), ("A_b", [128, NCH]), ("out_b", [128, NCH]),
                      ("rms_w", [128, NCH]), ("routerT", [128, NCH * EE]),
                      ("router_b", [EE, 1]), ("aallT", [128, NCH * ER]), ("ballT", [ER, D])]:
        ins[nm] = nc.dram_tensor(nm, shape, F32, kind="ExternalInput").ap()
    ins["gu16"] = nc.dram_tensor("gu16", [4, 128, GUO], I16, kind="ExternalInput").ap()
    ins["gusc"] = nc.dram_tensor("gusc", [4, 128, GUO], F16, kind="ExternalInput").ap()
    ins["dn16"] = nc.dram_tensor("dn16", [11, 128, D], I16, kind="ExternalInput").ap()
    ins["dnsc"] = nc.dram_tensor("dnsc", [11, 128, D], F16, kind="ExternalInput").ap()

    out = nc.dram_tensor("x2T", [D, T], F32, kind="ExternalOutput").ap()

    scr = {
        "x1": nc.dram_tensor("s_x1", [D, T], F32, kind="Internal"),
        "xn1": nc.dram_tensor("s_xn1", [D, T], BF16, kind="Internal"),
        "y": nc.dram_tensor("s_y", [D, T], BF16, kind="Internal"),
        "p1": nc.dram_tensor("s_p1", [D, T], F32, kind="Internal"),
        "wgu": nc.dram_tensor("s_wgu", [D, GUO], BF16, kind="Internal"),
        "wdn": nc.dram_tensor("s_wdn", [IH, D], BF16, kind="Internal"),
    }

    ident_np = np.eye(128, dtype=BF).view(np.uint16)
    identf_np = np.eye(128, dtype=np.float32)
    trimask_np = (np.arange(128)[:, None] <= np.arange(128)[None, :]).astype(BF).view(np.uint16)
    emap_np = np.zeros((EE, ER), BF)
    for e in range(EE):
        emap_np[e, e * 16:(e + 1) * 16] = 1.0
    c_ident = nc.inline_tensor(ident_np, "c_ident")
    c_identf = nc.inline_tensor(identf_np, "c_identf")
    c_tri = nc.inline_tensor(trimask_np, "c_tri")
    c_emap = nc.inline_tensor(emap_np.view(np.uint16), "c_emap")

    kb = KB(nc)
    sb, free = kb.sb, kb.free

    ONESF = sb("ONESF", [128, 128], F32)
    ONESB = sb("ONESB", [128, 128], BF16)
    IDENT = sb("IDENT", [128, 128], BF16)
    IDENTF = sb("IDENTF", [128, 128], F32)
    TRIM = sb("TRIM", [128, 128], BF16)
    EMAP = sb("EMAP", [EE, ER], BF16)
    CEXPS = sb("CEXPS", [128, 1], F32)
    CEXPB = sb("CEXPB", [128, 1], F32)
    CN2K = sb("CN2K", [128, 1], F32)
    BBH = sb("BBH", [128, NCH], F32)
    OBH = sb("OBH", [128, NCH], F32)
    ABH = sb("ABH", [128, NCH], F32)
    RWH = sb("RWH", [128, NCH], F32)
    RBH = sb("RBH", [EE, 1], F32)
    BAL = sb("BAL", [ER, D], BF16)
    AAL = sb("AAL", [128, NCH * ER], BF16)
    RTW = sb("RTW", [128, NCH * EE], BF16)
    XN0 = sb("XN0", [128, NCH * T], BF16)
    WQH = [sb(f"WQH{i}", [128, NCH * 384], BF16) for i in range(2)]
    XB = [sb(f"XB{i}", [128, TH], F32) for i in range(2)]
    dq = DQ(kb, ins, scr)

    blockg = nc.Block()
    block = blockg.__enter__()

    @block.gpsimd
    def _(g):
        for tile, val in [(ONESF, 1.0), (ONESB, 1.0), (CEXPS, RSQ_HD),
                          (CEXPB, EXP_BIAS), (CN2K, 1.0 / D)]:
            kb.inc(g.memset(tile[:], val), "inim", 1)
        for dst, src in [(BBH[:], ins["base_b"]), (OBH[:], ins["out_b"]),
                         (ABH[:], ins["A_b"]), (RWH[:], ins["rms_w"]),
                         (RBH[:], ins["router_b"]), (BAL[:], ins["ballT"]),
                         (AAL[:], ins["aallT"]), (RTW[:], ins["routerT"])]:
            kb.inc(g.dma_start(out=dst, in_=src), "ini", 16)
        kb.inc(g.dma_start(out=IDENT[:].bitcast(U16), in_=c_ident.ap()), "ini", 16)
        kb.inc(g.dma_start(out=IDENTF[:], in_=c_identf.ap()), "ini", 16)
        kb.inc(g.dma_start(out=TRIM[:].bitcast(U16), in_=c_tri.ap()), "ini", 16)
        kb.inc(g.dma_start(out=EMAP[:].bitcast(U16), in_=c_emap.ap()), "ini", 16)
    INI = kb.n("ini")
    INIM = kb.n("inim")

    M1R = sb("M1R", [128, T], F32)
    RSTD = sb("RSTD", [128, T], F32)
    S1A = sb("S1A", [128, T], F32)
    S2A = sb("S2A", [128, T], F32)
    SQT = sb("SQT", [128, TH], F32)
    TR0 = sb("TR0", [128, TH], F32)
    p_big = kb.ps("p_big", [128, 2048], F32)
    p_m1 = kb.ps("p_m1", [128, 512], F32)
    p_m2 = kb.ps("p_m2", [128, 512], F32)
    p_tr = kb.ps("p_tr", [128, 1024], BF16)
    p_st = kb.ps("p_st", [128, 512], F32)

    def xsrc(hc):
        c, th = hc // 2, hc % 2
        return ins["xT"][c * 128:(c + 1) * 128, th * TH:(th + 1) * TH]

    @block.sync
    def _(sync):
        for hc in range(32):
            kb.w(sync, "f0", 2 * (hc - 1))
            kb.dinc(sync.dma_start(out=XB[hc % 2][:], in_=xsrc(hc)), "xl0", hc)
        dq.pump_sync(sync, 2, lag=99)

    @block.scalar
    def _(s):
        for hc in range(32):
            kb.dw(s, "xl0", hc)
            kb.w(s, "f0", 2 * hc)
            kb.inc(s.activation(SQT[:], XB[hc % 2][:], AF.Square), "sq0", 1)

    @block.vector
    def _(v):
        for hc in range(32):
            th = hc % 2
            kb.dw(v, "xl0", hc)
            v.wait_ge(kb.sem("sq0"), hc + 1)
            if hc < 2:
                i1 = v.tensor_copy(S1A[:, th * TH:(th + 1) * TH], XB[hc % 2][:])
                i2 = v.tensor_copy(S2A[:, th * TH:(th + 1) * TH], SQT[:])
            else:
                i1 = v.tensor_add(S1A[:, th * TH:(th + 1) * TH], S1A[:, th * TH:(th + 1) * TH], XB[hc % 2][:])
                i2 = v.tensor_add(S2A[:, th * TH:(th + 1) * TH], S2A[:, th * TH:(th + 1) * TH], SQT[:])
            kb.inc(i1, "f0", 1)
            kb.inc(i2, "f0", 1)

    @block.tensor
    def _(t):
        t.wait_ge(kb.sem("ini"), INI)
        t.wait_ge(kb.sem("inim"), INIM)
        t.wait_ge(kb.sem("f0"), 64)
        for s4 in range(4):
            i = t.matmul(p_big[:, s4 * 512:(s4 + 1) * 512], ONESF[:],
                         S1A[:, s4 * 512:(s4 + 1) * 512], start=True, stop=True)
        kb.inc(i, "c0t", 1)
        t.wait_ge(kb.sem("c0s"), 1)
        for s4 in range(4):
            i = t.matmul(p_big[:, s4 * 512:(s4 + 1) * 512], ONESF[:],
                         S2A[:, s4 * 512:(s4 + 1) * 512], start=True, stop=True)
        kb.inc(i, "c0t", 1)

    @block.scalar
    def _(s):
        s.wait_ge(kb.sem("c0t"), 1)
        kb.inc(s.activation(M1R[:], p_big[:], AF.Identity, scale=CN2K[:, 0:1]), "c0s", 1)
        s.wait_ge(kb.sem("c0t"), 2)
        kb.inc(s.activation(S2A[:], p_big[:], AF.Identity, scale=CN2K[:, 0:1]), "c0s", 1)
        s.wait_ge(kb.sem("c0v"), 1)
        kb.inc(s.activation(RSTD[:], S1A[:], AF.Sqrt), "c0s", 1)

    @block.vector
    def _(v):
        v.wait_ge(kb.sem("c0s"), 2)
        kb.vop(v.tensor_tensor(S1A[:], M1R[:], M1R[:], AL.mult))
        kb.vwait(v)
        kb.vop(v.tensor_tensor(S2A[:], S2A[:], S1A[:], AL.subtract))
        kb.vwait(v)
        kb.vop(v.tensor_scalar(S2A[:], S2A[:], 1e-5, None, AL.add, AL.bypass))
        kb.vwait(v)
        kb.inc(v.reciprocal(S1A[:], S2A[:]), "c0v", 1)

    @block.sync
    def _(sync):
        for hc in range(32):
            if hc < 2:
                sync.wait_ge(kb.sem("f0"), 64)
                sync.wait_ge(kb.sem("sq0"), 32)
            else:
                kb.w(sync, "a0", hc - 1)
            kb.dinc(sync.dma_start(out=XB[hc % 2][:], in_=xsrc(hc)), "xl0b", hc)
        dq.pump_sync(sync, 4)

    @block.vector
    def _(v):
        v.wait_ge(kb.sem("c0s"), 3)
        for hc in range(32):
            c, th = hc // 2, hc % 2
            kb.dw(v, "xl0b", hc)
            kb.w(v, "a0", hc)
            kb.vop(v.tensor_tensor(TR0[:], XB[hc % 2][:], M1R[:, th * TH:(th + 1) * TH], AL.subtract))
            kb.vwait(v)
            i = v.tensor_tensor(XN0[:, c * T + th * TH:c * T + (th + 1) * TH], TR0[:],
                                RSTD[:, th * TH:(th + 1) * TH], AL.mult)
            kb.inc(i, "a0", 1)
        dq.vec(v, 4)

    free("S1A", "S2A", "SQT", "TR0", "M1R", "RSTD", "XB0", "XB1")

    if upto >= 1:
        OA = sb("OA", [128, NH * T], BF16)
        QT = [sb(f"QT{i}", [128, T], BF16) for i in range(2)]
        KT = [sb(f"KT{i}", [128, T], BF16) for i in range(2)]
        VTF = [sb(f"VTF{i}", [128, T], BF16) for i in range(2)]
        EB = [sb(f"EB{i}", [128, 1024], BF16) for i in range(2)]
        RD = sb("RD", [128, 512], F32)

        @block.gpsimd
        def _(g):
            for h in range(NH):
                kb.w(g, "wqc", h - 1)
                for half in range(2):
                    src = ins["wqkv"][h, half * 8:(half + 1) * 8, :, :].transpose([1, 0, 2])
                    dst = WQH[h % 2][:].rearrange("p (c o) -> p c o", o=384)[:, half * 8:(half + 1) * 8]
                    kb.dinc(g.dma_start(out=dst, in_=src), "wq", h, 16)
                if h >= 1:
                    dq.sc_loads(g, 8 if h == 1 else 5)
            dq.sc_loads(g, 6)

        @block.tensor
        def _(t):
            t.wait_ge(kb.sem("ini"), INI)
            t.wait_ge(kb.sem("inim"), INIM)

            def qkv_piece(hh, j):
                pi, s4 = j // 4, j % 4
                q = hh * 12 + j
                if j == 0:
                    kb.dw(t, "wq", hh, 32)
                kb.w(t, "qe", q)
                for c in range(NCH):
                    if hh == 0 and j == 0:
                        t.wait_ge(kb.sem("a0"), 2 * c + 2)
                    i = t.matmul(p_st[:, 0:512],
                                 WQH[hh % 2][:, c * 384 + pi * 128:c * 384 + pi * 128 + 128],
                                 XN0[:, c * T + s4 * 512:c * T + (s4 + 1) * 512],
                                 start=(c == 0), stop=(c == NCH - 1))
                if j == 11:
                    kb.inc(i, "wqc", 1)
                else:
                    kb.inc(i, "qp", 1)

            for j in range(12):
                qkv_piece(0, j)

            for h in range(NH):
                t.wait_ge(kb.sem("qe"), h * 12 + 12)
                for bt in range(2):
                    if 2 * h + bt > 0:
                        t.wait_ge(kb.sem("vtk"), 2 * h + bt)
                    for j in range(8):
                        kc = bt * 8 + j
                        i = t.transpose(p_tr[:, j * 128:(j + 1) * 128],
                                        VTF[h % 2][:, kc * 128:(kc + 1) * 128], IDENT[:])
                    kb.inc(i, "ttr", 1)

                base_exp = h * ATT_TOTEXP
                base_msk = h * ATT_TOTMASK
                base_grp = h * ATT_NG
                base_oan = h * 4

                def s_group(gi):
                    qq, rec = ATT_GROUPS[gi]
                    if gi >= 2:
                        t.wait_ge(kb.sem("exf"), base_exp + ATT_CUMEXP[gi - 2])
                    elif h > 0:
                        t.wait_ge(kb.sem("exf"), base_exp)
                    half = (gi % 2) * 1024
                    for kti, (kt, qs, dg) in enumerate(rec):
                        i = t.matmul(p_big[:, half + kti * 512 + qs: half + (kti + 1) * 512],
                                     KT[h % 2][:, kt * 128:(kt + 1) * 128],
                                     QT[h % 2][:, qq * 512 + qs:(qq + 1) * 512],
                                     start=True, stop=True)
                    kb.inc(i, "sgr", 1)

                def do_group(gi):
                    qq, rec = ATT_GROUPS[gi]
                    t.wait_ge(kb.sem("exf"), base_exp + ATT_CUMEXP[gi])
                    if ATT_CUMMASK[gi]:
                        t.wait_ge(kb.sem("msk"), base_msk + ATT_CUMMASK[gi])
                    t.wait_ge(kb.sem("vtk"), 2 * h + 2)
                    first = (gi == 0 or ATT_GROUPS[gi - 1][0] != qq)
                    if first:
                        kb.w(t, "oan", base_oan + qq)
                    for kti, (kt, qs, dg) in enumerate(rec):
                        st = first and kti == 0
                        last = (ATT_QLAST[qq] == gi) and (kti == len(rec) - 1)
                        t.matmul(p_m2[:, qs:512], ONESB[:],
                                 EB[gi % 2][:, kti * 512 + qs:(kti + 1) * 512],
                                 start=st, stop=last)
                        i = t.matmul(p_m1[:, qs:512], VTF[h % 2][:, kt * 128:(kt + 1) * 128],
                                     EB[gi % 2][:, kti * 512 + qs:(kti + 1) * 512],
                                     start=st, stop=last)
                    kb.inc(i, "oco", 1)

                for gi in range(ATT_NG):
                    s_group(gi)
                    if gi >= 1:
                        do_group(gi - 1)
                    if h + 1 < NH and gi < 12:
                        qkv_piece(h + 1, gi)
                do_group(ATT_NG - 1)

        @block.scalar
        def _(s):
            s.wait_ge(kb.sem("ini"), INI)
            s.wait_ge(kb.sem("inim"), INIM)

            def piece_evac(hh, j):
                q = hh * 12 + j
                pi, s4 = j // 4, j % 4
                dst = [QT, KT, VTF][pi][hh % 2]
                if j == 11:
                    kb.w(s, "wqc", hh + 1)
                else:
                    s.wait_ge(kb.sem("qp"), q + 1 - hh)
                if hh >= 2 and j == 0:
                    kb.w(s, "oco", (hh - 1) * ATT_NG)
                kb.inc(s.activation(dst[:, s4 * 512:(s4 + 1) * 512], p_st[:, 0:512], AF.Copy),
                       "qe", 1)

            for j in range(12):
                piece_evac(0, j)
            for h in range(NH):
                for bt in range(2):
                    s.wait_ge(kb.sem("ttr"), 2 * h + bt + 1)
                    i = s.activation(VTF[h % 2][:, bt * 1024:(bt + 1) * 1024], p_tr[:], AF.Copy)
                    kb.inc(i, "vtk", 1)
                base_exp = h * ATT_TOTEXP
                base_grp = h * ATT_NG
                for gi, (qq, rec) in enumerate(ATT_GROUPS):
                    s.wait_ge(kb.sem("sgr"), base_grp + gi + 1)
                    kb.w(s, "oco", base_grp + gi - 1)
                    half = (gi % 2) * 1024
                    if ATT_NEXP[gi] == 1:
                        n = len(rec) * 512
                        i = s.activation(EB[gi % 2][:, 0:n], p_big[:, half:half + n], AF.Exp,
                                         bias=CEXPB[:, 0:1], scale=CEXPS[:, 0:1])
                        kb.inc(i, "exf", 1)
                    else:
                        for kti, (kt, qs, dg) in enumerate(rec):
                            i = s.activation(EB[gi % 2][:, kti * 512 + qs:(kti + 1) * 512],
                                             p_big[:, half + kti * 512 + qs:half + (kti + 1) * 512],
                                             AF.Exp, bias=CEXPB[:, 0:1], scale=CEXPS[:, 0:1])
                            kb.inc(i, "exf", 1)
                    if h + 1 < NH and gi < 12:
                        piece_evac(h + 1, gi)

        @block.vector
        def _(v):
            for h in range(NH):
                base_exp = h * ATT_TOTEXP
                base_msk = h * ATT_TOTMASK
                base_grp = h * ATT_NG
                base_oan = h * 4
                for gi, (qq, rec) in enumerate(ATT_GROUPS):
                    if ATT_NMASK[gi]:
                        v.wait_ge(kb.sem("exf"), base_exp + ATT_CUMEXP[gi])
                        for kti, (kt, qs, dg) in enumerate(rec):
                            if dg is None:
                                continue
                            sl = EB[gi % 2][:, kti * 512 + 128 * dg:kti * 512 + 128 * dg + 128]
                            kb.inc(v.tensor_tensor(sl, sl, TRIM[:], AL.mult), "msk", 1)
                    if ATT_QLAST[qq] == gi:
                        v.wait_ge(kb.sem("oco"), base_grp + gi + 1)
                        kb.w(v, "oan", base_oan + qq)
                        kb.vop(v.reciprocal(RD[:], p_m2[:]))
                        kb.vwait(v)
                        i = v.scalar_tensor_tensor(OA[:, h * T + qq * 512:h * T + (qq + 1) * 512],
                                                   p_m1[:], 1.0, RD[:], AL.mult, AL.mult)
                        kb.inc(i, "oan", 1)
                        dq.vec(v, 1)
                dq.vec(v, 1)

        @block.sync
        def _(sync):
            dq.pump_sync(sync, 78)

        OAN_ALL = 4 * NH

    if upto >= 2:
        free("QT0", "QT1", "KT0", "KT1", "EB0", "EB1", "RD", "VTF0", "VTF1", "WQH0", "WQH1")
        WPA = [sb(f"WPA{i}", [128, NCH * 128], BF16) for i in range(2)]
        WPB = [sb(f"WPB{i}", [128, NCH * 128], BF16) for i in range(2)]
        X1B = [sb(f"X1B{i}", [128, TH], F32) for i in range(2)]
        XB = [sb(f"XB{i}", [128, TH], F32) for i in range(2)]
        LG8 = sb("LG8", [EE, T], F32)
        GX = sb("GX", [128, T], BF16)
        MIDG = sb("MIDG", [128, T], BF16)
        GT8 = sb("GT8", [EE, T], BF16)
        LGD = sb("LGD", [128, 128], F32)
        EL8 = sb("EL8", [128, 128], F32)
        G2D = sb("G2D", [128, 128], BF16)
        TMP8 = sb("TMP8", [128, 8], F32)
        COLS = sb("COLS", [128, 8], F32)
        MRK = sb("MRK", [128, 4], F32)
        NM1A = sb("NM1A", [128, NCH], F32)

        @block.gpsimd
        def _(g):
            for oc in range(NCH):
                if oc < 2:
                    g.wait_ge(kb.sem("oco"), NH * ATT_NG)
                else:
                    kb.w(g, "ept", 2 * (oc - 1))
                for hf in range(2):
                    kb.inc(g.dma_start(
                        out=WPA[oc % 2][:].rearrange("p (c o) -> p c o", o=128)[:, hf * 8:(hf + 1) * 8],
                        in_=ins["baseT"][oc, hf * 8:(hf + 1) * 8].transpose([1, 0, 2])), f"wp2_{oc % 2}", 16)
                    kb.inc(g.dma_start(
                        out=WPB[oc % 2][:].rearrange("p (c o) -> p c o", o=128)[:, hf * 8:(hf + 1) * 8],
                        in_=ins["woT"][oc, hf * 8:(hf + 1) * 8].transpose([1, 0, 2])), f"wp2_{oc % 2}", 16)

        @block.tensor
        def _(t):
            t.wait_ge(kb.sem("exf"), NH * ATT_TOTEXP)
            for c in range(NCH):
                for s4 in range(4):
                    i = t.matmul(p_big[0:EE, s4 * 512:(s4 + 1) * 512],
                                 RTW[:, c * EE:(c + 1) * EE],
                                 XN0[:, c * T + s4 * 512:c * T + (s4 + 1) * 512],
                                 start=(c == 0), stop=(c == NCH - 1))
            kb.inc(i, "c2t", 1)
            t.wait_ge(kb.sem("c2s"), 1)
            t.wait_ge(kb.sem("oan"), OAN_ALL)
            for tc in range(NCH):
                i = t.transpose(p_m1[:, tc * 8:(tc + 1) * 8],
                                LG8[0:EE, tc * 128:(tc + 1) * 128], IDENTF[0:EE, 0:EE])
            kb.inc(i, "c2t", 1)
            t.wait_ge(kb.sem("vtk"), 2 * NH)
            for half in range(2):
                t.wait_ge(kb.sem("c2v"), half + 1)
                if half == 1:
                    t.wait_ge(kb.sem("c2v"), 3)
                for j in range(8):
                    tc = half * 8 + j
                    i = t.transpose(p_tr[0:8, j * 128:(j + 1) * 128],
                                    G2D[:, tc * 8:(tc + 1) * 8], IDENT[:])
                kb.inc(i, "c2t", 1)
            t.wait_ge(kb.sem("c2v"), 4)
            for s4 in range(4):
                i = t.matmul(p_big[:, s4 * 512:(s4 + 1) * 512], EMAP[0:EE, :],
                             GT8[0:EE, s4 * 512:(s4 + 1) * 512], start=True, stop=True)
            kb.inc(i, "c2t", 1)
            t.wait_ge(kb.sem("c2v"), 5)
            for c in range(NCH):
                for s4 in range(4):
                    i = t.matmul(p_big[:, s4 * 512:(s4 + 1) * 512], AAL[:, c * ER:(c + 1) * ER],
                                 XN0[:, c * T + s4 * 512:c * T + (s4 + 1) * 512],
                                 start=(c == 0), stop=(c == NCH - 1))
            kb.inc(i, "c2t", 1)
            t.wait_ge(kb.sem("c2v"), 6)
            for oc in range(NCH):
                kb.dw(t, "wp2", oc, 64)
                for th in range(2):
                    fi = 2 * oc + th
                    kb.w(t, "epv", fi - 1)
                    half = (fi % 2) * 1024
                    tb = th * TH
                    for c in range(NCH):
                        for s2 in range(2):
                            t.matmul(p_big[:, half + s2 * 512:half + (s2 + 1) * 512],
                                     WPA[oc % 2][:, c * 128:(c + 1) * 128],
                                     XN0[:, c * T + tb + s2 * 512:c * T + tb + (s2 + 1) * 512],
                                     start=(c == 0), stop=False)
                    for c in range(NCH):
                        for s2 in range(2):
                            t.matmul(p_big[:, half + s2 * 512:half + (s2 + 1) * 512],
                                     WPB[oc % 2][:, c * 128:(c + 1) * 128],
                                     OA[:, c * T + tb + s2 * 512:c * T + tb + (s2 + 1) * 512],
                                     start=False, stop=False)
                    for s2 in range(2):
                        i = t.matmul(p_big[:, half + s2 * 512:half + (s2 + 1) * 512],
                                     BAL[:, oc * 128:(oc + 1) * 128],
                                     MIDG[:, tb + s2 * 512:tb + (s2 + 1) * 512],
                                     start=False, stop=True)
                    kb.inc(i, "ept", 1)

        @block.scalar
        def _(s):
            s.wait_ge(kb.sem("c2t"), 1)
            kb.inc(s.activation(LG8[0:EE, :], p_big[0:EE, :], AF.Identity, bias=RBH[0:EE, 0:1]), "c2s", 1)
            s.wait_ge(kb.sem("c2v0"), 1)
            for tc in range(NCH):
                i = s.activation(EL8[:, tc * 8:(tc + 1) * 8], LGD[:, tc * 8:(tc + 1) * 8],
                                 AF.Exp, bias=NM1A[:, tc:tc + 1], scale=1.0)
            kb.inc(i, "c2s", 1)

        @block.sync
        def _(sync):
            for hc in range(32):
                if hc < 2:
                    sync.wait_ge(kb.sem("a0"), 32)
                else:
                    kb.w(sync, "epv", hc - 1)
                kb.dinc(sync.dma_start(out=XB[hc % 2][:], in_=xsrc(hc)), "xl2", hc)

        @block.scalar
        def _(s):
            for hc in range(32):
                c, th = hc // 2, hc % 2
                s.wait_ge(kb.sem("epv"), hc + 1)
                kb.dinc(s.dma_start(out=scr["x1"].ap()[c * 128:(c + 1) * 128, th * TH:(th + 1) * TH],
                                   in_=X1B[hc % 2][:]), "x1s", hc)

        @block.vector
        def _(v):
            v.wait_ge(kb.sem("c2t"), 2)
            kb.vop(v.tensor_copy(LGD[:], p_m1[:, 0:128]))
            for tc in range(NCH):
                sl = LGD[:, tc * 8:(tc + 1) * 8]
                kb.vwait(v)
                kb.vop(v.reduce_max(COLS[:, 0:1], sl, axis=AX.X))
                kb.vwait(v)
                i = v.tensor_scalar(NM1A[:, tc:tc + 1], COLS[:, 0:1], -1.0, None, AL.mult, AL.bypass)
                kb.vop(i)
            kb.vwait(v)
            kb.inc(v.tensor_copy(MRK[:, 0:1], CN2K[:, 0:1]), "c2v0", 1)
            v.wait_ge(kb.sem("c2s"), 2)
            for half in range(2):
                for j in range(8):
                    tc = half * 8 + j
                    sl = LGD[:, tc * 8:(tc + 1) * 8]
                    kb.vwait(v)
                    kb.vop(v.reduce_max(COLS[:, 0:1], sl, axis=AX.X))
                    kb.vwait(v)
                    kb.vop(v.tensor_single_scalar(TMP8[:], sl, COLS[:, 0:1], AL.is_equal))
                    kb.vwait(v)
                    kb.vop(v.scalar_tensor_tensor(TMP8[:], TMP8[:], -1e9, sl, AL.mult, AL.add))
                    kb.vwait(v)
                    kb.vop(v.reduce_max(COLS[:, 1:2], TMP8[:], axis=AX.X))
                    kb.vwait(v)
                    kb.vop(v.tensor_single_scalar(TMP8[:], sl, COLS[:, 1:2], AL.is_ge))
                    kb.vwait(v)
                    kb.vop(v.tensor_tensor(TMP8[:], TMP8[:], EL8[:, tc * 8:(tc + 1) * 8], AL.mult))
                    kb.vwait(v)
                    kb.vop(v.reduce_sum(COLS[:, 3:4], TMP8[:], axis=AX.X))
                    kb.vwait(v)
                    kb.vop(v.reciprocal(COLS[:, 4:5], COLS[:, 3:4]))
                    kb.vwait(v)
                    i = v.tensor_single_scalar(G2D[:, tc * 8:(tc + 1) * 8], TMP8[:], COLS[:, 4:5], AL.mult)
                    kb.vop(i)
                kb.vwait(v)
                kb.inc(v.tensor_copy(MRK[:, 1 + half:2 + half], CN2K[:, 0:1]), "c2v", 1)
            for half in range(2):
                v.wait_ge(kb.sem("c2t"), 3 + half)
                i = v.tensor_copy(GT8[0:EE, half * 1024:(half + 1) * 1024], p_tr[0:8, :])
                kb.inc(i, "c2v", 1)
            v.wait_ge(kb.sem("c2t"), 5)
            kb.inc(v.tensor_copy(GX[:], p_big[:]), "c2v", 1)
            v.wait_ge(kb.sem("c2t"), 6)
            kb.inc(v.scalar_tensor_tensor(MIDG[:], p_big[:], 2.0, GX[:], AL.mult, AL.mult), "c2v", 1)
            for hc in range(32):
                v.wait_ge(kb.sem("ept"), hc + 1)
                kb.dw(v, "xl2", hc)
                if hc >= 2:
                    kb.dw(v, "x1s", hc - 2)
                half = (hc % 2) * 1024
                oc = hc // 2
                i = v.scalar_tensor_tensor(X1B[hc % 2][:], p_big[:, half:half + 1024],
                                           BBH[:, oc:oc + 1], XB[hc % 2][:], AL.add, AL.add)
                kb.inc(i, "epv", 1)

    if upto >= 3:
        free("LG8", "LGD", "TMP8", "COLS", "MRK", "NM1A", "EL8", "G2D", "GT8", "GX", "MIDG")
        free("OA", "XB0", "XB1")
        S1A = sb("S1A3", [128, T], F32)
        S2A = sb("S2A3", [128, T], F32)
        SQT = sb("SQT3", [128, TH], F32)
        M1R = sb("M1R3", [128, T], F32)
        RSTD = sb("RSTD3", [128, T], F32)
        GZ = sb("GZ3", [128, T], F32)
        TR0 = sb("TR03", [128, TH], F32)
        TR1 = sb("TR13", [128, T], F32)
        XN1S = [sb(f"XN1S{i}", [128, TH], BF16) for i in range(2)]
        Z = XN0

        def x1src(hc):
            c, th = hc // 2, hc % 2
            return scr["x1"].ap()[c * 128:(c + 1) * 128, th * TH:(th + 1) * TH]

        @block.scalar
        def _(s):
            kb.dwfull(s, "x1s", 256, 256)
            kb.dinc(s.dma_start(out=X1B[0][:], in_=x1src(0)), "x1l3", 0)
            kb.dinc(s.dma_start(out=X1B[1][:], in_=x1src(1)), "x1l3", 1)
            for hc in range(32):
                kb.dw(s, "x1l3", hc)
                kb.w(s, "f3", 2 * hc)
                kb.inc(s.activation(SQT[:], X1B[hc % 2][:], AF.Square), "sq3", 1)
                if hc + 2 < 32:
                    kb.w(s, "f3", 2 * (hc + 1))
                    kb.dinc(s.dma_start(out=X1B[hc % 2][:], in_=x1src(hc + 2)), "x1l3", hc + 2)

        @block.vector
        def _(v):
            for hc in range(32):
                th = hc % 2
                kb.dw(v, "x1l3", hc)
                v.wait_ge(kb.sem("sq3"), hc + 1)
                if hc < 2:
                    i1 = v.tensor_copy(S1A[:, th * TH:(th + 1) * TH], X1B[hc % 2][:])
                    i2 = v.tensor_copy(S2A[:, th * TH:(th + 1) * TH], SQT[:])
                else:
                    i1 = v.tensor_add(S1A[:, th * TH:(th + 1) * TH], S1A[:, th * TH:(th + 1) * TH], X1B[hc % 2][:])
                    i2 = v.tensor_add(S2A[:, th * TH:(th + 1) * TH], S2A[:, th * TH:(th + 1) * TH], SQT[:])
                kb.inc(i1, "f3", 1)
                kb.inc(i2, "f3", 1)

        @block.gpsimd
        def _(g):
            dq.sc_loads(g, 3)

        @block.tensor
        def _(t):
            t.wait_ge(kb.sem("f3"), 64)
            t.wait_ge(kb.sem("epv"), 32)
            for s4 in range(4):
                i = t.matmul(p_big[:, s4 * 512:(s4 + 1) * 512], ONESF[:],
                             S1A[:, s4 * 512:(s4 + 1) * 512], start=True, stop=True)
            kb.inc(i, "c3t", 1)
            t.wait_ge(kb.sem("c3s"), 1)
            for s4 in range(4):
                i = t.matmul(p_big[:, s4 * 512:(s4 + 1) * 512], ONESF[:],
                             S2A[:, s4 * 512:(s4 + 1) * 512], start=True, stop=True)
            kb.inc(i, "c3t", 1)

        @block.scalar
        def _(s):
            s.wait_ge(kb.sem("c3t"), 1)
            kb.inc(s.activation(M1R[:], p_big[:], AF.Identity, scale=CN2K[:, 0:1]), "c3s", 1)
            s.wait_ge(kb.sem("c3t"), 2)
            kb.inc(s.activation(S2A[:], p_big[:], AF.Identity, scale=CN2K[:, 0:1]), "c3s", 1)
            s.wait_ge(kb.sem("c3v"), 1)
            kb.inc(s.activation(RSTD[:], TR1[:], AF.Sqrt), "c3s", 1)
            s.wait_ge(kb.sem("c3v"), 2)
            kb.inc(s.activation(GZ[:], TR1[:], AF.Sqrt), "c3s", 1)

        @block.vector
        def _(v):
            v.wait_ge(kb.sem("c3s"), 2)
            kb.vop(v.tensor_tensor(S1A[:], M1R[:], M1R[:], AL.mult))
            kb.vwait(v)
            kb.vop(v.tensor_tensor(S2A[:], S2A[:], S1A[:], AL.subtract))
            kb.vwait(v)
            kb.vop(v.tensor_scalar(S1A[:], S2A[:], 1e-5, None, AL.add, AL.bypass))
            kb.vwait(v)
            kb.inc(v.reciprocal(TR1[:], S1A[:]), "c3v", 1)
            v.wait_ge(kb.sem("c3s"), 3)
            kb.vwait(v)
            kb.w(v, "c3v", 1)
            kb.vop(v.tensor_tensor(S2A[:], S2A[:], TR1[:], AL.mult))
            kb.vwait(v)
            kb.vop(v.tensor_scalar(S2A[:], S2A[:], 1e-5, None, AL.add, AL.bypass))
            kb.vwait(v)
            kb.inc(v.reciprocal(TR1[:], S2A[:]), "c3v", 1)
            v.wait_ge(kb.sem("c3s"), 4)
            kb.vwait(v)
            kb.inc(v.tensor_tensor(GZ[:], GZ[:], RSTD[:], AL.mult), "c3v", 1)
            for hc in range(32):
                c, th = hc // 2, hc % 2
                kb.dw(v, "x1l3b", hc)
                kb.w(v, "p3z", hc)
                kb.vop(v.tensor_tensor(TR0[:], X1B[hc % 2][:], M1R[:, th * TH:(th + 1) * TH], AL.subtract))
                if hc >= 2:
                    kb.dw(v, "xn1s", hc - 2)
                kb.vwait(v)
                i = v.tensor_tensor(XN1S[hc % 2][:], TR0[:], RSTD[:, th * TH:(th + 1) * TH], AL.mult)
                kb.inc(i, "p3x", 1)
                i = v.scalar_tensor_tensor(Z[:, c * T + th * TH:c * T + (th + 1) * TH], TR0[:],
                                           RWH[:, c:c + 1], GZ[:, th * TH:(th + 1) * TH],
                                           AL.mult, AL.mult)
                kb.inc(i, "p3z", 1)
            dq.vec(v, 2)

        @block.scalar
        def _(s):
            for hc in range(32):
                if hc < 2:
                    s.wait_ge(kb.sem("f3"), 64)
                    s.wait_ge(kb.sem("sq3"), 32)
                else:
                    kb.w(s, "p3z", hc - 1)
                kb.dinc(s.dma_start(out=X1B[hc % 2][:], in_=x1src(hc)), "x1l3b", hc)

        @block.sync
        def _(sync):
            for hc in range(32):
                c, th = hc // 2, hc % 2
                sync.wait_ge(kb.sem("p3x"), hc + 1)
                kb.dinc(sync.dma_start(out=scr["xn1"].ap()[c * 128:(c + 1) * 128, th * TH:(th + 1) * TH],
                                      in_=XN1S[hc % 2][:]), "xn1s", hc)
            dq.pump_sync(sync, 24)

        free("S1A3", "S2A3", "SQT3", "M1R3", "TR03", "TR13", "WPB0", "WPB1", "RSTD3", "GZ3")

    if upto >= 4:
        WM = [[sb(f"WM{w}_{i}", [128, NCH * 128], BF16) for i in range(2)] for w in range(5)]
        AAc = sb("AAc", [128, T], F32)
        UUc = sb("UUc", [128, T], F32)
        HHc = sb("HHc", [128, T], F32)
        CAc = sb("CAc", [128, T], BF16)
        CCc = sb("CCc", [128, T], BF16)
        DDc = sb("DDc", [128, T], BF16)
        YB = [sb(f"YB{i}", [128, T], BF16) for i in range(2)]
        WKEYS = ["CawT", "CbwT", "AwT", "DwT", "BwT"]
        EVENG = ["s", "v", "s", "s", "v"]

        @block.gpsimd
        def _(g):
            for nn in range(NCH):
                if nn < 2:
                    g.wait_ge(kb.sem("ept"), 32)
                    g.wait_ge(kb.sem("p3z"), 32)
                else:
                    kb.w(g, "p4s", 6 * (nn - 1))
                    kb.w(g, "p4v", 4 * (nn - 1))
                for w in range(5):
                    for hf in range(2):
                        kb.inc(g.dma_start(
                            out=WM[w][nn % 2][:].rearrange("p (c o) -> p c o", o=128)[:, hf * 8:(hf + 1) * 8],
                            in_=ins[WKEYS[w]][nn, hf * 8:(hf + 1) * 8].transpose([1, 0, 2])), f"wm_{nn % 2}", 16)
                if nn >= 1:
                    dq.sc_loads(g, 3)

        @block.tensor
        def _(t):
            t.wait_ge(kb.sem("c3v"), 3)
            for nn in range(NCH):
                kb.dw(t, "wm", nn, 160)
                for w in range(5):
                    for th in range(2):
                        fi = 10 * nn + 2 * w + th
                        pf = fi - 2
                        if pf >= 0:
                            peng = EVENG[(pf % 10) // 2]
                            cnt = sum(1 for q in range(pf + 1) if EVENG[(q % 10) // 2] == peng)
                            kb.w(t, "p4" + peng, cnt)
                        half = (fi % 2) * 1024
                        for c in range(NCH):
                            if nn == 0 and w == 0 and th == 0:
                                t.wait_ge(kb.sem("p3z"), 2 * c + 2)
                            for s2 in range(2):
                                i = t.matmul(p_big[:, half + s2 * 512:half + (s2 + 1) * 512],
                                             WM[w][nn % 2][:, c * 128:(c + 1) * 128],
                                             Z[:, c * T + th * TH + s2 * 512:c * T + th * TH + (s2 + 1) * 512],
                                             start=(c == 0), stop=(c == NCH - 1))
                        kb.inc(i, "p4t", 1)

        @block.scalar
        def _(s):
            for nn in range(NCH):
                for w, th, dst, func, bias in [
                        (0, 0, CAc, AF.Silu, None), (0, 1, CAc, AF.Silu, None),
                        (2, 0, AAc, AF.Sigmoid, ABH), (2, 1, AAc, AF.Sigmoid, ABH),
                        (3, 0, DDc, AF.Sigmoid, None), (3, 1, DDc, AF.Sigmoid, None)]:
                    fi = 10 * nn + 2 * w + th
                    s.wait_ge(kb.sem("p4t"), fi + 1)
                    if nn > 0:
                        kb.w(s, "scn", nn)
                        kb.w(s, "p4y", nn)
                    half = (fi % 2) * 1024
                    kw = {}
                    if bias is not None:
                        kw["bias"] = bias[:, nn:nn + 1]
                    i = s.activation(dst[:, th * 1024:(th + 1) * 1024], p_big[:, half:half + 1024],
                                     func, **kw)
                    kb.inc(i, "p4s", 1)
                if nn >= 1:
                    s.wait_ge(kb.sem("p4y"), nn)
                    kb.dinc(s.dma_start(out=scr["y"].ap()[(nn - 1) * 128:nn * 128, :],
                                        in_=YB[(nn - 1) % 2][:]), "ys", nn - 1)

        @block.vector
        def _(v):
            for nn in range(NCH):
                if nn > 0:
                    kb.w(v, "scn", nn)
                for w, th in [(1, 0), (1, 1), (4, 0), (4, 1)]:
                    fi = 10 * nn + 2 * w + th
                    v.wait_ge(kb.sem("p4t"), fi + 1)
                    half = (fi % 2) * 1024
                    if w == 1:
                        v.wait_ge(kb.sem("p4s"), 6 * nn + th + 1)
                        i = v.scalar_tensor_tensor(CCc[:, th * 1024:(th + 1) * 1024],
                                                   p_big[:, half:half + 1024], 1.0,
                                                   CAc[:, th * 1024:(th + 1) * 1024], AL.mult, AL.mult)
                    else:
                        v.wait_ge(kb.sem("p4s"), 6 * nn + 4 + th + 1)
                        i = v.scalar_tensor_tensor(UUc[:, th * 1024:(th + 1) * 1024],
                                                   p_big[:, half:half + 1024], 1.0,
                                                   DDc[:, th * 1024:(th + 1) * 1024], AL.mult, AL.mult)
                    kb.inc(i, "p4v", 1)
                v.wait_ge(kb.sem("p4s"), 6 * nn + 4)
                kb.w(v, "p4v", 4 * nn + 4)
                i = v.tensor_tensor_scan(HHc[:], AAc[:], UUc[:], 0.0, AL.mult, AL.add)
                kb.inc(i, "scn", 1)
                if nn >= 2:
                    kb.dw(v, "ys", nn - 2)
                kb.w(v, "scn", nn + 1)
                i = v.scalar_tensor_tensor(YB[nn % 2][:], HHc[:], 1.0, CCc[:], AL.mult, AL.mult)
                kb.inc(i, "p4y", 1)
                dq.vec(v, 3)

        @block.scalar
        def _(s):
            s.wait_ge(kb.sem("p4y"), NCH)
            kb.dinc(s.dma_start(out=scr["y"].ap()[(NCH - 1) * 128:NCH * 128, :],
                                in_=YB[(NCH - 1) % 2][:]), "ys", NCH - 1)

        @block.sync
        def _(sync):
            dq.pump_sync(sync, 30)

    if upto >= 5:
        free("AAc", "UUc", "HHc", "CAc", "CCc", "DDc",
             *[f"WM{w}_{i}" for w in range(5) for i in range(2)])
        YTl = [sb(f"YT{c}", [128, T], BF16) for c in range(NCH)]
        P1S = [sb(f"P1S{i}", [128, TH], F32) for i in range(2)]

        @block.sync
        def _(sync):
            sync.wait_ge(kb.sem("p4y"), NCH)
            for c in range(NCH):
                kb.dw(sync, "ys", c)
                kb.inc(sync.dma_start(out=YTl[c][:],
                                      in_=scr["y"].ap()[c * 128:(c + 1) * 128, :]), "yl", 16)
            sync.wait_ge(kb.sem("p4t"), 160)
            kb.dwfull(sync, "xn1s", 256, 256)
            for c in range(NCH):
                kb.inc(sync.dma_start(out=XN0[:, c * T:(c + 1) * T],
                                      in_=scr["xn1"].ap()[c * 128:(c + 1) * 128, :]), "xn1l", 16)

        @block.gpsimd
        def _(g):
            for oc in range(NCH):
                if oc < 2:
                    g.wait_ge(kb.sem("ept"), 2 * NCH)
                else:
                    kb.w(g, "p5t", 2 * (oc - 1))
                for hf in range(2):
                    kb.inc(g.dma_start(
                        out=WPA[oc % 2][:].rearrange("p (c o) -> p c o", o=128)[:, hf * 8:(hf + 1) * 8],
                        in_=ins["OwT"][oc, hf * 8:(hf + 1) * 8].transpose([1, 0, 2])), f"wo5_{oc % 2}", 16)
                dq.sc_loads(g, 2)

        @block.tensor
        def _(t):
            t.wait_ge(kb.sem("p4s"), 96)
            t.wait_ge(kb.sem("p4v"), 64)
            t.wait_ge(kb.sem("yl"), 16 * NCH)
            for oc in range(NCH):
                kb.dw(t, "wo5", oc, 32)
                for th in range(2):
                    fi = 2 * oc + th
                    kb.w(t, "p5v", fi - 1)
                    half = (fi % 2) * 1024
                    tb = th * TH
                    for c in range(NCH):
                        for s2 in range(2):
                            i = t.matmul(p_big[:, half + s2 * 512:half + (s2 + 1) * 512],
                                         WPA[oc % 2][:, c * 128:(c + 1) * 128],
                                         YTl[c][:, tb + s2 * 512:tb + (s2 + 1) * 512],
                                         start=(c == 0), stop=(c == NCH - 1))
                    kb.inc(i, "p5t", 1)

        @block.scalar
        def _(s):
            for hc in range(32):
                c, th = hc // 2, hc % 2
                if hc < 2:
                    kb.dwfull(s, "xn1s", 256, 256)
                    s.wait_ge(kb.sem("p3z"), 32)
                else:
                    kb.w(s, "p5v", hc - 1)
                kb.dinc(s.dma_start(out=XN1S[hc % 2][:],
                                    in_=scr["xn1"].ap()[c * 128:(c + 1) * 128, th * TH:(th + 1) * TH]),
                        "p5l", hc)
                kb.dinc(s.dma_start(out=X1B[hc % 2][:], in_=x1src(hc)), "p5l", hc)
                if hc >= 2:
                    ph, pc, pt = hc - 2, (hc - 2) // 2, (hc - 2) % 2
                    kb.dinc(s.dma_start(out=scr["p1"].ap()[pc * 128:(pc + 1) * 128, pt * TH:(pt + 1) * TH],
                                        in_=P1S[ph % 2][:]), "p1st", ph)

        @block.vector
        def _(v):
            for hc in range(32):
                oc = hc // 2
                v.wait_ge(kb.sem("p5t"), hc + 1)
                kb.dw(v, "p5l", hc, 32)
                if hc >= 2:
                    kb.dw(v, "p1st", hc - 2)
                half = (hc % 2) * 1024
                kb.vop(v.scalar_tensor_tensor(P1S[hc % 2][:], p_big[:, half:half + 1024],
                                              OBH[:, oc:oc + 1], XN1S[hc % 2][:], AL.add, AL.add))
                kb.vwait(v)
                i = v.tensor_add(P1S[hc % 2][:], P1S[hc % 2][:], X1B[hc % 2][:])
                kb.inc(i, "p5v", 1)
                dq.vec(v, 1)

        @block.scalar
        def _(s):
            for hc in range(30, 32):
                c, th = hc // 2, hc % 2
                s.wait_ge(kb.sem("p5v"), hc + 1)
                kb.dinc(s.dma_start(out=scr["p1"].ap()[c * 128:(c + 1) * 128, th * TH:(th + 1) * TH],
                                    in_=P1S[hc % 2][:]), "p1st", hc)


        DQST = dq.nu * 16

    if upto >= 6:
        free("X1B0", "X1B1", "XN1S0", "XN1S1", "P1S0", "P1S1", "YB0", "YB1",
             "dq_pk0", "dq_pk1", "dq_sc0", "dq_sc1", "dq_out",
             "dq_t0", "dq_t1", "dq_t2", "dq_t3",
             "WPA0", "WPA1", *[f"YT{c}" for c in range(NCH)])
        XN1R = XN0
        HQ = sb("HQ", [128, NGC * 512], BF16)
        WG6 = [sb(f"WG6{i}", [128, NCH * 256], BF16) for i in range(2)]
        WGU6 = [sb(f"WGU6{i}", [128, NCH * 256], BF16) for i in range(2)]
        WD6 = [sb(f"WD6{i}", [128, 11 * 512], BF16) for i in range(2)]
        SG6 = sb("SG6", [128, 512], BF16)
        YF = sb("YF", [128, NCH * 512], BF16)
        SQ6 = sb("SQ6", [128, 512], F32)
        S2F = sb("S2F", [128, 512], F32)
        GF = sb("GF", [128, 512], F32)
        P1B = [sb(f"P1B{i}", [128, 512], F32) for i in range(2)]
        X2B = [sb(f"X2B{i}", [128, 512], F32) for i in range(2)]

        @block.sync
        def _(sync):
            sync.wait_ge(kb.sem("p5t"), 2 * NCH)
            kb.dwfull(sync, "p1st", 256, 256)
            sync.wait_ge(kb.sem("dq_st"), DQST)
            for tq in range(4):
                for g2 in range(22):
                    li = tq * 22 + g2
                    if tq > 0 and 2 <= g2 <= 17:
                        sdc = g2 - 2
                        sli = (tq - 1) * 16 + sdc
                        sync.wait_ge(kb.sem("p6x2"), sli + 1)
                        kb.dinc(sync.dma_start(out=out[sdc * 128:(sdc + 1) * 128, (tq - 1) * 512:tq * 512],
                                               in_=X2B[sdc % 2][:]), "outs", sli)
                    kb.w(sync, "hqv", (li - 1) * 2)
                    for hf in range(2):
                        kb.inc(sync.dma_start(
                            out=WG6[li % 2][:].rearrange("p (c o) -> p c o", o=256)[:, hf * 8:(hf + 1) * 8],
                            in_=scr["wgu"].ap().rearrange("(c p) o -> p c o", p=128)[
                                :, hf * 8:(hf + 1) * 8, g2 * 256:(g2 + 1) * 256]), f"wg6_{li % 2}", 16)
                        kb.inc(sync.dma_start(
                            out=WGU6[li % 2][:].rearrange("p (c o) -> p c o", o=256)[:, hf * 8:(hf + 1) * 8],
                            in_=scr["wgu"].ap().rearrange("(c p) o -> p c o", p=128)[
                                :, hf * 8:(hf + 1) * 8, IH + g2 * 256:IH + (g2 + 1) * 256]), f"wg6_{li % 2}", 16)
                for dg in range(4):
                    for kq in range(4):
                        di = tq * 16 + dg * 4 + kq
                        kb.w(sync, "p6dk", di - 1)
                        kb.inc(sync.dma_start(
                            out=WD6[di % 2][:].rearrange("p (c o) -> p c o", o=512),
                            in_=scr["wdn"].ap().rearrange("(c p) o -> p c o", p=128)[
                                :, kq * 11:(kq + 1) * 11, dg * 512:(dg + 1) * 512]), f"wd6_{di % 2}", 16)

        @block.tensor
        def _(t):
            t.wait_ge(kb.sem("xn1l"), 16 * NCH)
            for tq in range(4):
                tb = tq * 512
                for g2 in range(22):
                    li = tq * 22 + g2
                    kb.dw(t, "wg6", li, 64)
                    for j in range(2):
                        gi6 = li * 2 + j
                        kb.w(t, "p6gs", gi6)
                        kb.w(t, "hqv", gi6)
                        for c in range(NCH):
                            i = t.matmul(p_m1[:, 0:512],
                                         WG6[li % 2][:, c * 256 + j * 128:c * 256 + (j + 1) * 128],
                                         XN1R[:, c * T + tb:c * T + tb + 512],
                                         start=(c == 0), stop=(c == NCH - 1))
                        kb.inc(i, "p6g", 1)
                        for c in range(NCH):
                            i = t.matmul(p_m2[:, 0:512],
                                         WGU6[li % 2][:, c * 256 + j * 128:c * 256 + (j + 1) * 128],
                                         XN1R[:, c * T + tb:c * T + tb + 512],
                                         start=(c == 0), stop=(c == NCH - 1))
                        kb.inc(i, "p6u", 1)
                for dg in range(4):
                    for kq in range(4):
                        di = tq * 16 + dg * 4 + kq
                        kb.dw(t, "wd6", di, 16)
                        if kq == 0:
                            t.wait_ge(kb.sem("hqv"), 44 * (tq + 1))
                            for dc4 in range(4):
                                fli = tq * 16 + dg * 4 + dc4
                                kb.w(t, "p6sq", fli - 3)
                        for ch in range(11):
                            for dc4 in range(4):
                                i = t.matmul(p_big[:, dc4 * 512:(dc4 + 1) * 512],
                                             WD6[di % 2][:, ch * 512 + dc4 * 128:ch * 512 + dc4 * 128 + 128],
                                             HQ[:, (kq * 11 + ch) * 512:(kq * 11 + ch + 1) * 512],
                                             start=(kq == 0 and ch == 0), stop=(kq == 3 and ch == 10))
                        kb.inc(i, "p6dk", 1)
                t.wait_ge(kb.sem("p6sf"), 16 * (tq + 1))
                kb.inc(t.matmul(p_st[:, 0:512], ONESF[:], S2F[:], start=True, stop=True),
                       "p6st", 1)

        @block.scalar
        def _(s):
            kb.dwfull(s, "p1st", 256, 256)
            kb.dinc(s.dma_start(out=P1B[0][:], in_=scr["p1"].ap()[0:128, 0:512]), "p1l6", 0)
            kb.dinc(s.dma_start(out=P1B[1][:], in_=scr["p1"].ap()[128:256, 0:512]), "p1l6", 1)
            for tq in range(4):
                for og in range(44):
                    gi6 = tq * 44 + og
                    if tq > 0 and og % 2 == 0 and 4 <= og <= 34:
                        pli = (tq - 1) * 16 + og // 2
                        pdc, ptq = pli % 16, pli // 16
                        kb.w(s, "p6x2", pli - 1)
                        kb.dinc(s.dma_start(out=P1B[pdc % 2][:],
                                            in_=scr["p1"].ap()[pdc * 128:(pdc + 1) * 128,
                                                               ptq * 512:(ptq + 1) * 512]),
                                "p1l6", pli)
                    s.wait_ge(kb.sem("p6g"), gi6 + 1)
                    kb.w(s, "hqv", gi6)
                    kb.inc(s.activation(SG6[:], p_m1[:, 0:512], AF.Silu), "p6gs", 1)
                for dgi in range(4):
                    s.wait_ge(kb.sem("p6dk"), tq * 16 + (dgi + 1) * 4)
                    for dc4 in range(4):
                        fli = tq * 16 + dgi * 4 + dc4
                        i = s.activation(YF[:, (dgi * 4 + dc4) * 512:(dgi * 4 + dc4 + 1) * 512],
                                         p_big[:, dc4 * 512:(dc4 + 1) * 512], AF.Copy)
                        kb.inc(i, "p6ys", 1)
                        kb.w(s, "p6sf", fli)
                        i = s.activation(SQ6[:], p_big[:, dc4 * 512:(dc4 + 1) * 512], AF.Square)
                        kb.inc(i, "p6sq", 1)
                s.wait_ge(kb.sem("p6v2"), tq + 1)
                kb.w(s, "p6x2", 16 * tq)
                kb.inc(s.activation(GF[:], S2F[:], AF.Sqrt), "p6s2", 1)
            for li in range(3 * 16 + 2, 64):
                dc = li % 16
                kb.w(s, "p6x2", li - 1)
                kb.dinc(s.dma_start(out=P1B[dc % 2][:],
                                    in_=scr["p1"].ap()[dc * 128:(dc + 1) * 128, 3 * 512:4 * 512]),
                        "p1l6", li)

        @block.vector
        def _(v):
            def x2_chunk(li):
                dc, ltq = li % 16, li // 16
                kb.dw(v, "p1l6", li)
                if li >= 2:
                    kb.dw(v, "outs", li - 2)
                v.wait_ge(kb.sem("p6s2"), ltq + 1)
                kb.vwait(v)
                kb.vop(v.scalar_tensor_tensor(X2B[dc % 2][:], YF[:, dc * 512:(dc + 1) * 512], 1.0,
                                              GF[:], AL.mult, AL.mult))
                kb.vwait(v)
                i = v.tensor_add(X2B[dc % 2][:], X2B[dc % 2][:], P1B[dc % 2][:])
                kb.inc(i, "p6x2", 1)

            for tq in range(4):
                for og in range(44):
                    gi6 = tq * 44 + og
                    if og == 0 and tq > 0:
                        v.wait_ge(kb.sem("p6dk"), 16 * tq)
                    v.wait_ge(kb.sem("p6gs"), gi6 + 1)
                    v.wait_ge(kb.sem("p6u"), gi6 + 1)
                    i = v.scalar_tensor_tensor(HQ[:, og * 512:(og + 1) * 512],
                                               p_m2[:, 0:512], 1.0, SG6[:], AL.mult, AL.mult)
                    kb.inc(i, "hqv", 1)
                    if tq > 0 and og % 2 == 1 and og < 32:
                        x2_chunk((tq - 1) * 16 + og // 2)
                for fi16 in range(16):
                    v.wait_ge(kb.sem("p6sq"), tq * 16 + fi16 + 1)
                    if fi16 == 0:
                        kb.w(v, "p6s2", tq)
                        i = v.tensor_copy(S2F[:], SQ6[:])
                    else:
                        i = v.tensor_add(S2F[:], S2F[:], SQ6[:])
                    kb.inc(i, "p6sf", 1)
                v.wait_ge(kb.sem("p6st"), tq + 1)
                kb.vwait(v)
                kb.vop(v.tensor_scalar(S2F[:], p_st[:, 0:512], 1.0 / D, 1e-6, AL.mult, AL.add))
                kb.vwait(v)
                kb.inc(v.reciprocal(S2F[:], S2F[:]), "p6v2", 1)
            for li in range(3 * 16, 64):
                x2_chunk(li)


        @block.sync
        def _(sync):
            for dc in range(NCH):
                li = 3 * 16 + dc
                sync.wait_ge(kb.sem("p6x2"), li + 1)
                kb.dinc(sync.dma_start(out=out[dc * 128:(dc + 1) * 128, 3 * 512:4 * 512],
                                       in_=X2B[dc % 2][:]), "outs", li)
            kb.dwfull(sync, "outs", 16 * 32, 16 * 32)

    while dq.st_i < dq.nu or dq.vec_i < dq.nu or dq.sc_i < dq.nu or dq.ld_i < dq.nu:
        @block.gpsimd
        def _(g):
            dq.sc_loads(g, 6)

        @block.vector
        def _(v):
            dq.vec(v, 6)

        @block.sync
        def _(sync):
            dq.pump_sync(sync, 6)

    @block.sync
    def _(sync):
        for nm in sorted(kb.dma_sems):
            sync.wait_ge(kb.sem(nm), kb.cnt[nm])

    blockg.__exit__(None, None, None)
    for g in kb._semctx[::-1]:
        g.__exit__(None, None, None)
    return nc


_CACHE = {}


def kernel(**inputs):
    if "nc" not in _CACHE:
        _CACHE["nc"] = build_program()
    nc = _CACHE["nc"]
    shared = stage_shared(inputs)
    x = np.asarray(inputs["L0_attn_x"])
    in_maps = []
    for c in range(B):
        m = dict(shared)
        m["xT"] = np.ascontiguousarray(x[c].T[SIGD])
        in_maps.append(m)
    res = run_bass_kernel_spmd(nc, in_maps, core_ids=list(range(B)))
    outp = np.empty((B, T, D), np.float32)
    for c in range(B):
        x2T = res.results[c]["x2T"]
        fm = np.empty((D, T), np.float32)
        fm[SIGD] = x2T
        outp[c] = fm.T
    return outp


# version 05
# speedup vs baseline: 1.3038x; 1.0011x over previous
import numpy as np
import ml_dtypes
import concourse.bass as bass
import concourse.mybir as mybir
from concourse.bass_utils import run_bass_kernel_spmd

F32 = mybir.dt.float32
BF16 = mybir.dt.bfloat16
F16 = mybir.dt.float16
I16 = mybir.dt.int16
U16 = mybir.dt.uint16
AL = mybir.AluOpType
AF = mybir.ActivationFunctionType
AX = mybir.AxisListType

B = 8
D = 2048
T = 2048
TH = 1024
NCH = 16
NH = 16
HD = 128
EE = 8
ER = 128
IH = 5632
GUO = 11264
NGC = 44
RSQ_HD = float(1.0 / np.sqrt(128.0))
EXP_BIAS = -4.0
BF = np.dtype(ml_dtypes.bfloat16)

_p = np.arange(D)
SIGD = 4 * (_p % 512) + _p // 512
_q = np.arange(IH)
SIGH = 4 * (_q % 1408) + _q // 1408



def _tile_chunks(wT):
    r = np.empty((NCH, NCH, 128, 128), np.float32)
    for oc in range(NCH):
        r[oc] = wT[:, oc * 128:(oc + 1) * 128].reshape(NCH, 128, 128)
    return np.ascontiguousarray(r)


def stage_shared(inputs):
    g = {}
    sd, sh = SIGD, SIGH
    wq = np.ascontiguousarray(inputs["L0_attn_w_project"]).T[sd]
    t = np.empty((NH, NCH, 128, 384), np.float32)
    for h in range(NH):
        cols = np.concatenate([wq[:, k * D + h * HD:k * D + (h + 1) * HD] for k in range(3)], axis=1)
        t[h] = cols.reshape(NCH, 128, 384)
    g["wqkv"] = np.ascontiguousarray(t)
    g["woT"] = _tile_chunks(np.ascontiguousarray(inputs["L0_attn_w_out"]).T[:, sd])
    g["baseT"] = _tile_chunks(np.ascontiguousarray(inputs["L0_ffn_base_w"]).T[sd][:, sd])
    g["base_b"] = np.ascontiguousarray(np.asarray(inputs["L0_ffn_base_b"])[sd].reshape(NCH, 128).T)
    g["routerT"] = np.ascontiguousarray(
        np.asarray(inputs["L0_ffn_router_w"]).T[sd].reshape(NCH, 128, EE).transpose(1, 0, 2).reshape(128, NCH * EE))
    g["router_b"] = np.ascontiguousarray(np.asarray(inputs["L0_ffn_router_b"]).reshape(EE, 1))
    g["aallT"] = np.ascontiguousarray(
        np.asarray(inputs["L0_ffn_lora_A"]).transpose(1, 0, 2).reshape(D, ER)[sd]
        .reshape(NCH, 128, ER).transpose(1, 0, 2).reshape(128, NCH * ER))
    g["ballT"] = np.ascontiguousarray(np.asarray(inputs["L0_ffn_lora_B"]).reshape(ER, D)[:, sd])
    g["rms_w"] = np.ascontiguousarray(np.asarray(inputs["L1_attn_rms_w"])[sd].reshape(NCH, 128).T)
    for nm, key in [("AwT", "L1_attn_A_w"), ("BwT", "L1_attn_B_w"), ("CawT", "L1_attn_Ca_w"),
                    ("CbwT", "L1_attn_Cb_w"), ("DwT", "L1_attn_D_w")]:
        g[nm] = _tile_chunks(np.ascontiguousarray(inputs[key]).T[sd])
    g["A_b"] = np.ascontiguousarray(np.asarray(inputs["L1_attn_A_b"]).reshape(NCH, 128).T)
    g["OwT"] = _tile_chunks(np.ascontiguousarray(inputs["L1_attn_out_w"]).T[:, sd])
    g["out_b"] = np.ascontiguousarray(np.asarray(inputs["L1_attn_out_b"])[sd].reshape(NCH, 128).T)
    rowp = np.concatenate([sh, IH + sh])
    gu16 = np.ascontiguousarray(np.asarray(inputs["L1_ffn_gate_up_packed"]).view(np.int16)[rowp].T)
    g["gu16"] = np.ascontiguousarray(gu16.reshape(4, 128, GUO))
    gsc = np.asarray(inputs["L1_ffn_gate_up_scales"])[:, rowp]
    g["gusc"] = np.ascontiguousarray(np.repeat(gsc, 32, axis=0).reshape(4, 128, GUO))
    dn16 = np.ascontiguousarray(np.asarray(inputs["L1_ffn_down_packed"])[sd].view(np.int16).T)
    g["dn16"] = np.ascontiguousarray(dn16.reshape(11, 128, D))
    dsc = np.asarray(inputs["L1_ffn_down_scales"])[:, sd]
    g["dnsc"] = np.ascontiguousarray(np.repeat(dsc, 32, axis=0).reshape(11, 128, D))
    return g



_DTSIZE = {F32: 4, BF16: 2, F16: 2, I16: 2, U16: 2}


class KB:

    def __init__(self, nc):
        self.nc = nc
        self.sems = {}
        self.cnt = {}
        self.guards = {}
        self._semctx = []
        self.dma_sems = set()
        base = (nc.sbuf_base + 31) // 32 * 32
        self.freelist = [(base, nc.sbuf_top)]
        self.live = {}
        self.hwm = 0

    def sem(self, name):
        if name not in self.sems:
            g = self.nc.semaphore(name)
            h = g.__enter__()
            self._semctx.append(g)
            self.sems[name] = h
            self.cnt[name] = 0
        return self.sems[name]

    def inc(self, instr, name, k=1):
        instr.then_inc(self.sem(name), k)
        self.cnt[name] += k
        if k >= 16 and k % 16 == 0:
            self.dma_sems.add(name)
        return self.cnt[name]

    def n(self, name):
        self.sem(name)
        return self.cnt[name]

    def w(self, eng, name, val):
        if val > 0:
            eng.wait_ge(self.sem(name), val)

    def dinc(self, instr, name, idx, k=16):
        self.inc(instr, f"{name}_{idx % 2}", k)

    def dw(self, eng, name, idx, k=16):
        eng.wait_ge(self.sem(f"{name}_{idx % 2}"), k * (idx // 2 + 1))

    def dwfull(self, eng, name, total0, total1):
        self.w(eng, f"{name}_0", total0)
        self.w(eng, f"{name}_1", total1)

    def vop(self, instr, name="vch"):
        return self.inc(instr, name, 1)

    def vwait(self, eng, name="vch"):
        self.w(eng, name, self.n(name))

    def sb(self, name, shape, dt):
        nb = 1
        for s in shape[1:]:
            nb *= s
        nb *= _DTSIZE[dt]
        nb = (nb + 31) // 32 * 32
        for idx, (lo, hi) in enumerate(self.freelist):
            if hi - lo >= nb:
                self.freelist[idx] = (lo + nb, hi)
                if lo + nb == hi:
                    self.freelist.pop(idx)
                self.live[name] = (lo, lo + nb)
                used = sum(b - a for (a, b) in (v for v in self.live.values()))
                self.hwm = max(self.hwm, used)
                return self.nc.alloc_sbuf_tensor_at(name, shape, dt, offset=lo)
        raise MemoryError(f"arena OOM for {name} need {nb} free={self.freelist}")

    def free(self, *names):
        for n in names:
            lo, hi = self.live.pop(n)
            self.freelist.append((lo, hi))
        self.freelist.sort()
        merged = []
        for lo, hi in self.freelist:
            if merged and merged[-1][1] >= lo:
                merged[-1] = (merged[-1][0], max(merged[-1][1], hi))
            else:
                merged.append((lo, hi))
        self.freelist = [tuple(x) for x in merged]

    def ps(self, name, shape, dt=F32):
        g = self.nc.psum_tensor(name, shape, dt)
        t = g.__enter__()
        self.guards[name] = g
        return t


class DQ:

    def __init__(self, kb, ins, scr):
        self.kb = kb
        self.units = [("gu", wb, sl) for wb in range(4) for sl in range(22)] + \
                     [("dn", wb, sl) for wb in range(11) for sl in range(4)]
        self.split = None
        self.nu = len(self.units)
        self.ld_i = self.sc_i = self.vec_i = self.st_i = 0
        self.ins = ins
        self.scr = scr
        self.pk = [kb.sb(f"dq_pk{i}", [128, 512], I16) for i in range(2)]
        self.sc = [kb.sb(f"dq_sc{i}", [128, 512], F32) for i in range(2)]
        self.out = kb.sb("dq_out", [128, 4 * 512], BF16)
        self.t0 = kb.sb("dq_t0", [128, 512], I16)
        self.t1 = kb.sb("dq_t1", [128, 512], I16)
        self.t2 = kb.sb("dq_t2", [128, 512], I16)
        self.t3 = kb.sb("dq_t3", [128, 512], I16)

    def _src(self, u):
        kind, wb, sl = u
        if kind == "gu":
            return (self.ins["gu16"][wb, :, sl * 512:(sl + 1) * 512],
                    self.ins["gusc"][wb, :, sl * 512:(sl + 1) * 512])
        return (self.ins["dn16"][wb, :, sl * 512:(sl + 1) * 512],
                self.ins["dnsc"][wb, :, sl * 512:(sl + 1) * 512])

    def pump_sync(self, eng, n, lag=1):
        kb = self.kb
        for _ in range(n):
            progressed = False
            while self.st_i < self.nu and (self.st_i <= self.ld_i - lag or self.ld_i >= self.nu):
                self._store(eng)
                progressed = True
            if self.ld_i < self.nu:
                i = self.ld_i
                kb.w(eng, "dq_v", (i - 1) * 4)
                pk_src, _ = self._src(self.units[i])
                kb.dinc(eng.dma_start(out=self.pk[i % 2][:], in_=pk_src), "dq_pk", i)
                self.ld_i += 1
                progressed = True
            if not progressed:
                break

    def _store(self, eng):
        kb = self.kb
        i = self.st_i
        eng.wait_ge(kb.sem("dq_v"), (i + 1) * 4)
        kind, wb, sl = self.units[i]
        scrt = self.scr["wgu"] if kind == "gu" else self.scr["wdn"]
        ncols = GUO if kind == "gu" else D
        R = 4 if kind == "gu" else 11
        off = wb * 128 * ncols + sl * 512
        dst = bass.AP(scrt, off, [[ncols, 128], [R * 128 * ncols, 4], [1, 512]])
        srcap = self.out[:].rearrange("p (j o) -> p j o", j=4)
        sem = "dq_st" if (self.split is None or i < self.split) else "dq_st2"
        kb.inc(eng.dma_start(out=dst, in_=srcap), sem, 16)
        self.st_i += 1

    def sc_loads(self, eng, n):
        kb = self.kb
        for _ in range(n):
            if self.sc_i >= self.nu:
                return
            i = self.sc_i
            kb.w(eng, "dq_v", (i - 1) * 4)
            _, sc_src = self._src(self.units[i])
            kb.dinc(eng.dma_start(out=self.sc[i % 2][:], in_=sc_src), "dq_sc", i)
            self.sc_i += 1

    def vec(self, v, n):
        kb = self.kb
        for _ in range(n):
            if self.vec_i >= self.nu:
                return
            i = self.vec_i
            kb.dw(v, "dq_pk", i)
            kb.dw(v, "dq_sc", i)
            if self.split is None or i <= self.split:
                kb.w(v, "dq_st", 16 * i)
            else:
                kb.w(v, "dq_st", 16 * self.split)
                kb.w(v, "dq_st2", 16 * (i - self.split))
            W16, S = self.pk[i % 2], self.sc[i % 2]
            O = self.out
            for jp in range(4):
                sh = 4 * jp - 6
                kb.w(v, "dq_v", 4 * i + jp)
                kb.vwait(v)
                if sh < 0:
                    kb.vop(v.tensor_scalar(self.t0[:], W16[:], -sh, 0x1C0, AL.logical_shift_left, AL.bitwise_and))
                else:
                    kb.vop(v.tensor_scalar(self.t0[:], W16[:], sh, 0x1C0, AL.logical_shift_right, AL.bitwise_and))
                kb.vop(v.tensor_scalar(self.t1[:], W16[:], 4 * jp + 3, 15, AL.logical_shift_right, AL.logical_shift_left))
                kb.vwait(v)
                kb.vop(v.tensor_single_scalar(self.t2[:], self.t0[:], 64, AL.is_ge))
                kb.vwait(v)
                kb.vop(v.scalar_tensor_tensor(self.t2[:], self.t2[:], 16064, self.t0[:], AL.mult, AL.add))
                kb.vop(v.tensor_single_scalar(self.t3[:], self.t0[:], 128, AL.is_ge))
                kb.vwait(v)
                kb.vop(v.scalar_tensor_tensor(self.t2[:], self.t3[:], 64, self.t2[:], AL.mult, AL.add))
                kb.vwait(v)
                kb.vop(v.tensor_tensor(self.t3[:], self.t2[:], self.t1[:], AL.bitwise_or))
                kb.vwait(v)
                i2 = v.tensor_tensor(O[:, jp * 512:(jp + 1) * 512], self.t3[:].bitcast(BF16), S[:], AL.mult)
                kb.inc(i2, "dq_v", 1)
            self.vec_i += 1




def _att_sched():
    groups = []
    for qq in range(4):
        kts = list(range(4 * qq + 4))
        for i in range(0, len(kts), 2):
            rec = []
            for kt in kts[i:i + 2]:
                dt = kt - 4 * qq
                qs = max(0, 128 * dt)
                rec.append((kt, qs, dt if 0 <= dt <= 3 else None))
            groups.append((qq, rec))
    return groups


ATT_GROUPS = _att_sched()
ATT_NG = len(ATT_GROUPS)
ATT_NEXP = [1 if all(qs == 0 for (_, qs, _) in rec) else len(rec) for (_, rec) in ATT_GROUPS]
ATT_CUMEXP = np.cumsum(ATT_NEXP).tolist()
ATT_NMASK = [sum(1 for (_, _, dg) in rec if dg is not None) for (_, rec) in ATT_GROUPS]
ATT_CUMMASK = np.cumsum(ATT_NMASK).tolist()
ATT_QLAST = {}
for _gi, (_qq, _) in enumerate(ATT_GROUPS):
    ATT_QLAST[_qq] = _gi
ATT_TOTEXP = ATT_CUMEXP[-1]
ATT_TOTMASK = ATT_CUMMASK[-1]


def build_program(upto=99):
    nc = bass.Bass("TRN2", target_bir_lowering=False)
    ins = {}
    ins["xT"] = nc.dram_tensor("xT", [D, T], F32, kind="ExternalInput").ap()
    ins["wqkv"] = nc.dram_tensor("wqkv", [NH, NCH, 128, 384], F32, kind="ExternalInput").ap()
    for nm in ["woT", "baseT", "OwT", "AwT", "BwT", "CawT", "CbwT", "DwT"]:
        ins[nm] = nc.dram_tensor(nm, [NCH, NCH, 128, 128], F32, kind="ExternalInput").ap()
    for nm, shape in [("base_b", [128, NCH]), ("A_b", [128, NCH]), ("out_b", [128, NCH]),
                      ("rms_w", [128, NCH]), ("routerT", [128, NCH * EE]),
                      ("router_b", [EE, 1]), ("aallT", [128, NCH * ER]), ("ballT", [ER, D])]:
        ins[nm] = nc.dram_tensor(nm, shape, F32, kind="ExternalInput").ap()
    ins["gu16"] = nc.dram_tensor("gu16", [4, 128, GUO], I16, kind="ExternalInput").ap()
    ins["gusc"] = nc.dram_tensor("gusc", [4, 128, GUO], F16, kind="ExternalInput").ap()
    ins["dn16"] = nc.dram_tensor("dn16", [11, 128, D], I16, kind="ExternalInput").ap()
    ins["dnsc"] = nc.dram_tensor("dnsc", [11, 128, D], F16, kind="ExternalInput").ap()

    out = nc.dram_tensor("x2T", [D, T], F32, kind="ExternalOutput").ap()

    scr = {
        "x1": nc.dram_tensor("s_x1", [D, T], F32, kind="Internal"),
        "xn1": nc.dram_tensor("s_xn1", [D, T], BF16, kind="Internal"),
        "y": nc.dram_tensor("s_y", [D, T], BF16, kind="Internal"),
        "p1": nc.dram_tensor("s_p1", [D, T], F32, kind="Internal"),
        "wgu": nc.dram_tensor("s_wgu", [D, GUO], BF16, kind="Internal"),
        "wdn": nc.dram_tensor("s_wdn", [IH, D], BF16, kind="Internal"),
    }

    ident_np = np.eye(128, dtype=BF).view(np.uint16)
    identf_np = np.eye(128, dtype=np.float32)
    trimask_np = (np.arange(128)[:, None] <= np.arange(128)[None, :]).astype(BF).view(np.uint16)
    emap_np = np.zeros((EE, ER), BF)
    for e in range(EE):
        emap_np[e, e * 16:(e + 1) * 16] = 1.0
    c_ident = nc.inline_tensor(ident_np, "c_ident")
    c_identf = nc.inline_tensor(identf_np, "c_identf")
    c_tri = nc.inline_tensor(trimask_np, "c_tri")
    c_emap = nc.inline_tensor(emap_np.view(np.uint16), "c_emap")

    kb = KB(nc)
    sb, free = kb.sb, kb.free

    ONESF = sb("ONESF", [128, 128], F32)
    ONESB = sb("ONESB", [128, 128], BF16)
    IDENT = sb("IDENT", [128, 128], BF16)
    IDENTF = sb("IDENTF", [128, 128], F32)
    TRIM = sb("TRIM", [128, 128], BF16)
    EMAP = sb("EMAP", [EE, ER], BF16)
    CEXPS = sb("CEXPS", [128, 1], F32)
    CEXPB = sb("CEXPB", [128, 1], F32)
    CN2K = sb("CN2K", [128, 1], F32)
    BBH = sb("BBH", [128, NCH], F32)
    OBH = sb("OBH", [128, NCH], F32)
    ABH = sb("ABH", [128, NCH], F32)
    RWH = sb("RWH", [128, NCH], F32)
    RBH = sb("RBH", [EE, 1], F32)
    BAL = sb("BAL", [ER, D], BF16)
    AAL = sb("AAL", [128, NCH * ER], BF16)
    RTW = sb("RTW", [128, NCH * EE], BF16)
    XN0 = sb("XN0", [128, NCH * T], BF16)
    WQH = [sb(f"WQH{i}", [128, NCH * 384], BF16) for i in range(2)]
    XB = [sb(f"XB{i}", [128, TH], F32) for i in range(2)]
    dq = DQ(kb, ins, scr)

    blockg = nc.Block()
    block = blockg.__enter__()

    @block.gpsimd
    def _(g):
        for tile, val in [(ONESF, 1.0), (ONESB, 1.0), (CEXPS, RSQ_HD),
                          (CEXPB, EXP_BIAS), (CN2K, 1.0 / D)]:
            kb.inc(g.memset(tile[:], val), "inim", 1)
        for dst, src in [(BBH[:], ins["base_b"]), (OBH[:], ins["out_b"]),
                         (ABH[:], ins["A_b"]), (RWH[:], ins["rms_w"]),
                         (RBH[:], ins["router_b"]), (BAL[:], ins["ballT"]),
                         (AAL[:], ins["aallT"]), (RTW[:], ins["routerT"])]:
            kb.inc(g.dma_start(out=dst, in_=src), "ini", 16)
        kb.inc(g.dma_start(out=IDENT[:].bitcast(U16), in_=c_ident.ap()), "ini", 16)
        kb.inc(g.dma_start(out=IDENTF[:], in_=c_identf.ap()), "ini", 16)
        kb.inc(g.dma_start(out=TRIM[:].bitcast(U16), in_=c_tri.ap()), "ini", 16)
        kb.inc(g.dma_start(out=EMAP[:].bitcast(U16), in_=c_emap.ap()), "ini", 16)
    INI = kb.n("ini")
    INIM = kb.n("inim")

    M1R = sb("M1R", [128, T], F32)
    RSTD = sb("RSTD", [128, T], F32)
    S1A = sb("S1A", [128, T], F32)
    S2A = sb("S2A", [128, T], F32)
    SQT = sb("SQT", [128, TH], F32)
    TR0 = sb("TR0", [128, TH], F32)
    p_big = kb.ps("p_big", [128, 2048], F32)
    p_m1 = kb.ps("p_m1", [128, 512], F32)
    p_m2 = kb.ps("p_m2", [128, 512], F32)
    p_tr = kb.ps("p_tr", [128, 1024], BF16)
    p_st = kb.ps("p_st", [128, 512], F32)

    def xsrc(hc):
        c, th = hc // 2, hc % 2
        return ins["xT"][c * 128:(c + 1) * 128, th * TH:(th + 1) * TH]

    @block.sync
    def _(sync):
        for hc in range(32):
            kb.w(sync, "f0", 2 * (hc - 1))
            kb.dinc(sync.dma_start(out=XB[hc % 2][:], in_=xsrc(hc)), "xl0", hc)
        dq.pump_sync(sync, 2, lag=99)

    @block.scalar
    def _(s):
        for hc in range(32):
            kb.dw(s, "xl0", hc)
            kb.w(s, "f0", 2 * hc)
            kb.inc(s.activation(SQT[:], XB[hc % 2][:], AF.Square), "sq0", 1)

    @block.vector
    def _(v):
        for hc in range(32):
            th = hc % 2
            kb.dw(v, "xl0", hc)
            v.wait_ge(kb.sem("sq0"), hc + 1)
            if hc < 2:
                i1 = v.tensor_copy(S1A[:, th * TH:(th + 1) * TH], XB[hc % 2][:])
                i2 = v.tensor_copy(S2A[:, th * TH:(th + 1) * TH], SQT[:])
            else:
                i1 = v.tensor_add(S1A[:, th * TH:(th + 1) * TH], S1A[:, th * TH:(th + 1) * TH], XB[hc % 2][:])
                i2 = v.tensor_add(S2A[:, th * TH:(th + 1) * TH], S2A[:, th * TH:(th + 1) * TH], SQT[:])
            kb.inc(i1, "f0", 1)
            kb.inc(i2, "f0", 1)

    @block.tensor
    def _(t):
        t.wait_ge(kb.sem("ini"), INI)
        t.wait_ge(kb.sem("inim"), INIM)
        t.wait_ge(kb.sem("f0"), 64)
        for s4 in range(4):
            i = t.matmul(p_big[:, s4 * 512:(s4 + 1) * 512], ONESF[:],
                         S1A[:, s4 * 512:(s4 + 1) * 512], start=True, stop=True)
        kb.inc(i, "c0t", 1)
        t.wait_ge(kb.sem("c0s"), 1)
        for s4 in range(4):
            i = t.matmul(p_big[:, s4 * 512:(s4 + 1) * 512], ONESF[:],
                         S2A[:, s4 * 512:(s4 + 1) * 512], start=True, stop=True)
        kb.inc(i, "c0t", 1)

    @block.scalar
    def _(s):
        s.wait_ge(kb.sem("c0t"), 1)
        kb.inc(s.activation(M1R[:], p_big[:], AF.Identity, scale=CN2K[:, 0:1]), "c0s", 1)
        s.wait_ge(kb.sem("c0t"), 2)
        kb.inc(s.activation(S2A[:], p_big[:], AF.Identity, scale=CN2K[:, 0:1]), "c0s", 1)
        s.wait_ge(kb.sem("c0v"), 1)
        kb.inc(s.activation(RSTD[:], S1A[:], AF.Sqrt), "c0s", 1)

    @block.vector
    def _(v):
        v.wait_ge(kb.sem("c0s"), 2)
        kb.vop(v.tensor_tensor(S1A[:], M1R[:], M1R[:], AL.mult))
        kb.vwait(v)
        kb.vop(v.tensor_tensor(S2A[:], S2A[:], S1A[:], AL.subtract))
        kb.vwait(v)
        kb.vop(v.tensor_scalar(S2A[:], S2A[:], 1e-5, None, AL.add, AL.bypass))
        kb.vwait(v)
        kb.inc(v.reciprocal(S1A[:], S2A[:]), "c0v", 1)

    @block.sync
    def _(sync):
        for hc in range(32):
            if hc < 2:
                sync.wait_ge(kb.sem("f0"), 64)
                sync.wait_ge(kb.sem("sq0"), 32)
            else:
                kb.w(sync, "a0", hc - 1)
            kb.dinc(sync.dma_start(out=XB[hc % 2][:], in_=xsrc(hc)), "xl0b", hc)
        dq.pump_sync(sync, 4)

    @block.vector
    def _(v):
        v.wait_ge(kb.sem("c0s"), 3)
        for hc in range(32):
            c, th = hc // 2, hc % 2
            kb.dw(v, "xl0b", hc)
            kb.w(v, "a0", hc)
            kb.vop(v.tensor_tensor(TR0[:], XB[hc % 2][:], M1R[:, th * TH:(th + 1) * TH], AL.subtract))
            kb.vwait(v)
            i = v.tensor_tensor(XN0[:, c * T + th * TH:c * T + (th + 1) * TH], TR0[:],
                                RSTD[:, th * TH:(th + 1) * TH], AL.mult)
            kb.inc(i, "a0", 1)
        dq.vec(v, 4)

    free("S1A", "S2A", "SQT", "TR0", "M1R", "RSTD", "XB0", "XB1")

    if upto >= 1:
        OA = sb("OA", [128, NH * T], BF16)
        QT = [sb(f"QT{i}", [128, T], BF16) for i in range(2)]
        KT = [sb(f"KT{i}", [128, T], BF16) for i in range(2)]
        VTF = [sb(f"VTF{i}", [128, T], BF16) for i in range(2)]
        EB = [sb(f"EB{i}", [128, 1024], BF16) for i in range(2)]
        RD = sb("RD", [128, 512], F32)

        @block.gpsimd
        def _(g):
            for h in range(NH):
                kb.w(g, "wqc", h - 1)
                for half in range(2):
                    src = ins["wqkv"][h, half * 8:(half + 1) * 8, :, :].transpose([1, 0, 2])
                    dst = WQH[h % 2][:].rearrange("p (c o) -> p c o", o=384)[:, half * 8:(half + 1) * 8]
                    kb.dinc(g.dma_start(out=dst, in_=src), "wq", h, 16)
                if h >= 1:
                    dq.sc_loads(g, 8 if h == 1 else 5)
            dq.sc_loads(g, 6)

        @block.tensor
        def _(t):
            t.wait_ge(kb.sem("ini"), INI)
            t.wait_ge(kb.sem("inim"), INIM)

            def qkv_piece(hh, j):
                pi, s4 = j // 4, j % 4
                q = hh * 12 + j
                if j == 0:
                    kb.dw(t, "wq", hh, 32)
                kb.w(t, "qe", q)
                for c in range(NCH):
                    if hh == 0 and j == 0:
                        t.wait_ge(kb.sem("a0"), 2 * c + 2)
                    i = t.matmul(p_st[:, 0:512],
                                 WQH[hh % 2][:, c * 384 + pi * 128:c * 384 + pi * 128 + 128],
                                 XN0[:, c * T + s4 * 512:c * T + (s4 + 1) * 512],
                                 start=(c == 0), stop=(c == NCH - 1))
                if j == 11:
                    kb.inc(i, "wqc", 1)
                else:
                    kb.inc(i, "qp", 1)

            for j in range(12):
                qkv_piece(0, j)

            for h in range(NH):
                t.wait_ge(kb.sem("qe"), h * 12 + 12)
                for bt in range(2):
                    if 2 * h + bt > 0:
                        t.wait_ge(kb.sem("vtk"), 2 * h + bt)
                    for j in range(8):
                        kc = bt * 8 + j
                        i = t.transpose(p_tr[:, j * 128:(j + 1) * 128],
                                        VTF[h % 2][:, kc * 128:(kc + 1) * 128], IDENT[:])
                    kb.inc(i, "ttr", 1)

                base_exp = h * ATT_TOTEXP
                base_msk = h * ATT_TOTMASK
                base_grp = h * ATT_NG
                base_oan = h * 4

                def s_group(gi):
                    qq, rec = ATT_GROUPS[gi]
                    if gi >= 2:
                        t.wait_ge(kb.sem("exf"), base_exp + ATT_CUMEXP[gi - 2])
                    elif h > 0:
                        t.wait_ge(kb.sem("exf"), base_exp)
                    half = (gi % 2) * 1024
                    for kti, (kt, qs, dg) in enumerate(rec):
                        i = t.matmul(p_big[:, half + kti * 512 + qs: half + (kti + 1) * 512],
                                     KT[h % 2][:, kt * 128:(kt + 1) * 128],
                                     QT[h % 2][:, qq * 512 + qs:(qq + 1) * 512],
                                     start=True, stop=True)
                    kb.inc(i, "sgr", 1)

                def do_group(gi):
                    qq, rec = ATT_GROUPS[gi]
                    t.wait_ge(kb.sem("exf"), base_exp + ATT_CUMEXP[gi])
                    if ATT_CUMMASK[gi]:
                        t.wait_ge(kb.sem("msk"), base_msk + ATT_CUMMASK[gi])
                    t.wait_ge(kb.sem("vtk"), 2 * h + 2)
                    first = (gi == 0 or ATT_GROUPS[gi - 1][0] != qq)
                    if first:
                        kb.w(t, "oan", base_oan + qq)
                    for kti, (kt, qs, dg) in enumerate(rec):
                        st = first and kti == 0
                        last = (ATT_QLAST[qq] == gi) and (kti == len(rec) - 1)
                        t.matmul(p_m2[:, qs:512], ONESB[:],
                                 EB[gi % 2][:, kti * 512 + qs:(kti + 1) * 512],
                                 start=st, stop=last)
                        i = t.matmul(p_m1[:, qs:512], VTF[h % 2][:, kt * 128:(kt + 1) * 128],
                                     EB[gi % 2][:, kti * 512 + qs:(kti + 1) * 512],
                                     start=st, stop=last)
                    kb.inc(i, "oco", 1)

                for gi in range(ATT_NG):
                    s_group(gi)
                    if gi >= 1:
                        do_group(gi - 1)
                    if h + 1 < NH and gi < 12:
                        qkv_piece(h + 1, gi)
                do_group(ATT_NG - 1)

        @block.scalar
        def _(s):
            s.wait_ge(kb.sem("ini"), INI)
            s.wait_ge(kb.sem("inim"), INIM)

            def piece_evac(hh, j):
                q = hh * 12 + j
                pi, s4 = j // 4, j % 4
                dst = [QT, KT, VTF][pi][hh % 2]
                if j == 11:
                    kb.w(s, "wqc", hh + 1)
                else:
                    s.wait_ge(kb.sem("qp"), q + 1 - hh)
                if hh >= 2 and j == 0:
                    kb.w(s, "oco", (hh - 1) * ATT_NG)
                kb.inc(s.activation(dst[:, s4 * 512:(s4 + 1) * 512], p_st[:, 0:512], AF.Copy),
                       "qe", 1)

            for j in range(12):
                piece_evac(0, j)
            for h in range(NH):
                for bt in range(2):
                    s.wait_ge(kb.sem("ttr"), 2 * h + bt + 1)
                    i = s.activation(VTF[h % 2][:, bt * 1024:(bt + 1) * 1024], p_tr[:], AF.Copy)
                    kb.inc(i, "vtk", 1)
                base_exp = h * ATT_TOTEXP
                base_grp = h * ATT_NG
                for gi, (qq, rec) in enumerate(ATT_GROUPS):
                    s.wait_ge(kb.sem("sgr"), base_grp + gi + 1)
                    kb.w(s, "oco", base_grp + gi - 1)
                    half = (gi % 2) * 1024
                    if ATT_NEXP[gi] == 1:
                        n = len(rec) * 512
                        i = s.activation(EB[gi % 2][:, 0:n], p_big[:, half:half + n], AF.Exp,
                                         bias=CEXPB[:, 0:1], scale=CEXPS[:, 0:1])
                        kb.inc(i, "exf", 1)
                    else:
                        for kti, (kt, qs, dg) in enumerate(rec):
                            i = s.activation(EB[gi % 2][:, kti * 512 + qs:(kti + 1) * 512],
                                             p_big[:, half + kti * 512 + qs:half + (kti + 1) * 512],
                                             AF.Exp, bias=CEXPB[:, 0:1], scale=CEXPS[:, 0:1])
                            kb.inc(i, "exf", 1)
                    if h + 1 < NH and gi < 12:
                        piece_evac(h + 1, gi)

        @block.vector
        def _(v):
            for h in range(NH):
                base_exp = h * ATT_TOTEXP
                base_msk = h * ATT_TOTMASK
                base_grp = h * ATT_NG
                base_oan = h * 4
                for gi, (qq, rec) in enumerate(ATT_GROUPS):
                    if ATT_NMASK[gi]:
                        v.wait_ge(kb.sem("exf"), base_exp + ATT_CUMEXP[gi])
                        for kti, (kt, qs, dg) in enumerate(rec):
                            if dg is None:
                                continue
                            sl = EB[gi % 2][:, kti * 512 + 128 * dg:kti * 512 + 128 * dg + 128]
                            kb.inc(v.tensor_tensor(sl, sl, TRIM[:], AL.mult), "msk", 1)
                    if ATT_QLAST[qq] == gi:
                        v.wait_ge(kb.sem("oco"), base_grp + gi + 1)
                        kb.w(v, "oan", base_oan + qq)
                        kb.vop(v.reciprocal(RD[:], p_m2[:]))
                        kb.vwait(v)
                        i = v.scalar_tensor_tensor(OA[:, h * T + qq * 512:h * T + (qq + 1) * 512],
                                                   p_m1[:], 1.0, RD[:], AL.mult, AL.mult)
                        kb.inc(i, "oan", 1)
                        dq.vec(v, 1)
                dq.vec(v, 1)

        @block.sync
        def _(sync):
            dq.pump_sync(sync, 78)

        OAN_ALL = 4 * NH

    if upto >= 2:
        free("QT0", "QT1", "KT0", "KT1", "EB0", "EB1", "RD", "VTF0", "VTF1", "WQH0", "WQH1")
        WPA = [sb(f"WPA{i}", [128, NCH * 128], BF16) for i in range(2)]
        WPB = [sb(f"WPB{i}", [128, NCH * 128], BF16) for i in range(2)]
        X1B = [sb(f"X1B{i}", [128, TH], F32) for i in range(2)]
        XB = [sb(f"XB{i}", [128, TH], F32) for i in range(2)]
        LG8 = sb("LG8", [EE, T], F32)
        GX = sb("GX", [128, T], BF16)
        MIDG = sb("MIDG", [128, T], BF16)
        GT8 = sb("GT8", [EE, T], BF16)
        LGD = sb("LGD", [128, 128], F32)
        EL8 = sb("EL8", [128, 128], F32)
        G2D = sb("G2D", [128, 128], BF16)
        TMP8 = sb("TMP8", [128, 8], F32)
        COLS = sb("COLS", [128, 8], F32)
        MRK = sb("MRK", [128, 4], F32)
        NM1A = sb("NM1A", [128, NCH], F32)

        @block.gpsimd
        def _(g):
            for oc in range(NCH):
                if oc < 2:
                    g.wait_ge(kb.sem("oco"), NH * ATT_NG)
                else:
                    kb.w(g, "ept", 2 * (oc - 1))
                for hf in range(2):
                    kb.inc(g.dma_start(
                        out=WPA[oc % 2][:].rearrange("p (c o) -> p c o", o=128)[:, hf * 8:(hf + 1) * 8],
                        in_=ins["baseT"][oc, hf * 8:(hf + 1) * 8].transpose([1, 0, 2])), f"wp2_{oc % 2}", 16)
                    kb.inc(g.dma_start(
                        out=WPB[oc % 2][:].rearrange("p (c o) -> p c o", o=128)[:, hf * 8:(hf + 1) * 8],
                        in_=ins["woT"][oc, hf * 8:(hf + 1) * 8].transpose([1, 0, 2])), f"wp2_{oc % 2}", 16)

        @block.tensor
        def _(t):
            t.wait_ge(kb.sem("exf"), NH * ATT_TOTEXP)
            for c in range(NCH):
                for s4 in range(4):
                    i = t.matmul(p_big[0:EE, s4 * 512:(s4 + 1) * 512],
                                 RTW[:, c * EE:(c + 1) * EE],
                                 XN0[:, c * T + s4 * 512:c * T + (s4 + 1) * 512],
                                 start=(c == 0), stop=(c == NCH - 1))
            kb.inc(i, "c2t", 1)
            t.wait_ge(kb.sem("c2s"), 1)
            t.wait_ge(kb.sem("oan"), OAN_ALL)
            for tc in range(NCH):
                i = t.transpose(p_m1[:, tc * 8:(tc + 1) * 8],
                                LG8[0:EE, tc * 128:(tc + 1) * 128], IDENTF[0:EE, 0:EE])
            kb.inc(i, "c2t", 1)
            t.wait_ge(kb.sem("vtk"), 2 * NH)
            for half in range(2):
                t.wait_ge(kb.sem("c2v"), half + 1)
                if half == 1:
                    t.wait_ge(kb.sem("c2v"), 3)
                for j in range(8):
                    tc = half * 8 + j
                    i = t.transpose(p_tr[0:8, j * 128:(j + 1) * 128],
                                    G2D[:, tc * 8:(tc + 1) * 8], IDENT[:])
                kb.inc(i, "c2t", 1)
            t.wait_ge(kb.sem("c2v"), 4)
            for s4 in range(4):
                i = t.matmul(p_big[:, s4 * 512:(s4 + 1) * 512], EMAP[0:EE, :],
                             GT8[0:EE, s4 * 512:(s4 + 1) * 512], start=True, stop=True)
            kb.inc(i, "c2t", 1)
            t.wait_ge(kb.sem("c2v"), 5)
            for c in range(NCH):
                for s4 in range(4):
                    i = t.matmul(p_big[:, s4 * 512:(s4 + 1) * 512], AAL[:, c * ER:(c + 1) * ER],
                                 XN0[:, c * T + s4 * 512:c * T + (s4 + 1) * 512],
                                 start=(c == 0), stop=(c == NCH - 1))
            kb.inc(i, "c2t", 1)
            t.wait_ge(kb.sem("c2v"), 6)
            for oc in range(NCH):
                kb.dw(t, "wp2", oc, 64)
                for th in range(2):
                    fi = 2 * oc + th
                    kb.w(t, "epv", fi - 1)
                    half = (fi % 2) * 1024
                    tb = th * TH
                    for c in range(NCH):
                        for s2 in range(2):
                            t.matmul(p_big[:, half + s2 * 512:half + (s2 + 1) * 512],
                                     WPA[oc % 2][:, c * 128:(c + 1) * 128],
                                     XN0[:, c * T + tb + s2 * 512:c * T + tb + (s2 + 1) * 512],
                                     start=(c == 0), stop=False)
                    for c in range(NCH):
                        for s2 in range(2):
                            t.matmul(p_big[:, half + s2 * 512:half + (s2 + 1) * 512],
                                     WPB[oc % 2][:, c * 128:(c + 1) * 128],
                                     OA[:, c * T + tb + s2 * 512:c * T + tb + (s2 + 1) * 512],
                                     start=False, stop=False)
                    for s2 in range(2):
                        i = t.matmul(p_big[:, half + s2 * 512:half + (s2 + 1) * 512],
                                     BAL[:, oc * 128:(oc + 1) * 128],
                                     MIDG[:, tb + s2 * 512:tb + (s2 + 1) * 512],
                                     start=False, stop=True)
                    kb.inc(i, "ept", 1)

        @block.scalar
        def _(s):
            s.wait_ge(kb.sem("c2t"), 1)
            kb.inc(s.activation(LG8[0:EE, :], p_big[0:EE, :], AF.Identity, bias=RBH[0:EE, 0:1]), "c2s", 1)
            s.wait_ge(kb.sem("c2v0"), 1)
            for tc in range(NCH):
                i = s.activation(EL8[:, tc * 8:(tc + 1) * 8], LGD[:, tc * 8:(tc + 1) * 8],
                                 AF.Exp, bias=NM1A[:, tc:tc + 1], scale=1.0)
            kb.inc(i, "c2s", 1)

        @block.sync
        def _(sync):
            for hc in range(32):
                if hc < 2:
                    sync.wait_ge(kb.sem("a0"), 32)
                else:
                    kb.w(sync, "epv", hc - 1)
                kb.dinc(sync.dma_start(out=XB[hc % 2][:], in_=xsrc(hc)), "xl2", hc)

        @block.scalar
        def _(s):
            for hc in range(32):
                c, th = hc // 2, hc % 2
                s.wait_ge(kb.sem("epv"), hc + 1)
                kb.dinc(s.dma_start(out=scr["x1"].ap()[c * 128:(c + 1) * 128, th * TH:(th + 1) * TH],
                                   in_=X1B[hc % 2][:]), "x1s", hc)

        @block.vector
        def _(v):
            v.wait_ge(kb.sem("c2t"), 2)
            kb.vop(v.tensor_copy(LGD[:], p_m1[:, 0:128]))
            for tc in range(NCH):
                sl = LGD[:, tc * 8:(tc + 1) * 8]
                kb.vwait(v)
                kb.vop(v.reduce_max(COLS[:, 0:1], sl, axis=AX.X))
                kb.vwait(v)
                i = v.tensor_scalar(NM1A[:, tc:tc + 1], COLS[:, 0:1], -1.0, None, AL.mult, AL.bypass)
                kb.vop(i)
            kb.vwait(v)
            kb.inc(v.tensor_copy(MRK[:, 0:1], CN2K[:, 0:1]), "c2v0", 1)
            v.wait_ge(kb.sem("c2s"), 2)
            for half in range(2):
                for j in range(8):
                    tc = half * 8 + j
                    sl = LGD[:, tc * 8:(tc + 1) * 8]
                    kb.vwait(v)
                    kb.vop(v.reduce_max(COLS[:, 0:1], sl, axis=AX.X))
                    kb.vwait(v)
                    kb.vop(v.tensor_single_scalar(TMP8[:], sl, COLS[:, 0:1], AL.is_equal))
                    kb.vwait(v)
                    kb.vop(v.scalar_tensor_tensor(TMP8[:], TMP8[:], -1e9, sl, AL.mult, AL.add))
                    kb.vwait(v)
                    kb.vop(v.reduce_max(COLS[:, 1:2], TMP8[:], axis=AX.X))
                    kb.vwait(v)
                    kb.vop(v.tensor_single_scalar(TMP8[:], sl, COLS[:, 1:2], AL.is_ge))
                    kb.vwait(v)
                    kb.vop(v.tensor_tensor(TMP8[:], TMP8[:], EL8[:, tc * 8:(tc + 1) * 8], AL.mult))
                    kb.vwait(v)
                    kb.vop(v.reduce_sum(COLS[:, 3:4], TMP8[:], axis=AX.X))
                    kb.vwait(v)
                    kb.vop(v.reciprocal(COLS[:, 4:5], COLS[:, 3:4]))
                    kb.vwait(v)
                    i = v.tensor_single_scalar(G2D[:, tc * 8:(tc + 1) * 8], TMP8[:], COLS[:, 4:5], AL.mult)
                    kb.vop(i)
                kb.vwait(v)
                kb.inc(v.tensor_copy(MRK[:, 1 + half:2 + half], CN2K[:, 0:1]), "c2v", 1)
            for half in range(2):
                v.wait_ge(kb.sem("c2t"), 3 + half)
                i = v.tensor_copy(GT8[0:EE, half * 1024:(half + 1) * 1024], p_tr[0:8, :])
                kb.inc(i, "c2v", 1)
            v.wait_ge(kb.sem("c2t"), 5)
            kb.inc(v.tensor_copy(GX[:], p_big[:]), "c2v", 1)
            v.wait_ge(kb.sem("c2t"), 6)
            kb.inc(v.scalar_tensor_tensor(MIDG[:], p_big[:], 2.0, GX[:], AL.mult, AL.mult), "c2v", 1)
            for hc in range(32):
                v.wait_ge(kb.sem("ept"), hc + 1)
                kb.dw(v, "xl2", hc)
                if hc >= 2:
                    kb.dw(v, "x1s", hc - 2)
                half = (hc % 2) * 1024
                oc = hc // 2
                i = v.scalar_tensor_tensor(X1B[hc % 2][:], p_big[:, half:half + 1024],
                                           BBH[:, oc:oc + 1], XB[hc % 2][:], AL.add, AL.add)
                kb.inc(i, "epv", 1)

    if upto >= 3:
        free("LG8", "LGD", "TMP8", "COLS", "MRK", "NM1A", "EL8", "G2D", "GT8", "GX", "MIDG")
        free("OA", "XB0", "XB1")
        S1A = sb("S1A3", [128, T], F32)
        S2A = sb("S2A3", [128, T], F32)
        SQT = sb("SQT3", [128, TH], F32)
        M1R = sb("M1R3", [128, T], F32)
        RSTD = sb("RSTD3", [128, T], F32)
        GZ = sb("GZ3", [128, T], F32)
        TR0 = sb("TR03", [128, TH], F32)
        TR1 = sb("TR13", [128, T], F32)
        XN1S = [sb(f"XN1S{i}", [128, TH], BF16) for i in range(2)]
        Z = XN0

        def x1src(hc):
            c, th = hc // 2, hc % 2
            return scr["x1"].ap()[c * 128:(c + 1) * 128, th * TH:(th + 1) * TH]

        @block.scalar
        def _(s):
            kb.dwfull(s, "x1s", 256, 256)
            kb.dinc(s.dma_start(out=X1B[0][:], in_=x1src(0)), "x1l3", 0)
            kb.dinc(s.dma_start(out=X1B[1][:], in_=x1src(1)), "x1l3", 1)
            for hc in range(32):
                kb.dw(s, "x1l3", hc)
                kb.w(s, "f3", 2 * hc)
                kb.inc(s.activation(SQT[:], X1B[hc % 2][:], AF.Square), "sq3", 1)
                if hc + 2 < 32:
                    kb.w(s, "f3", 2 * (hc + 1))
                    kb.dinc(s.dma_start(out=X1B[hc % 2][:], in_=x1src(hc + 2)), "x1l3", hc + 2)

        @block.vector
        def _(v):
            for hc in range(32):
                th = hc % 2
                kb.dw(v, "x1l3", hc)
                v.wait_ge(kb.sem("sq3"), hc + 1)
                if hc < 2:
                    i1 = v.tensor_copy(S1A[:, th * TH:(th + 1) * TH], X1B[hc % 2][:])
                    i2 = v.tensor_copy(S2A[:, th * TH:(th + 1) * TH], SQT[:])
                else:
                    i1 = v.tensor_add(S1A[:, th * TH:(th + 1) * TH], S1A[:, th * TH:(th + 1) * TH], X1B[hc % 2][:])
                    i2 = v.tensor_add(S2A[:, th * TH:(th + 1) * TH], S2A[:, th * TH:(th + 1) * TH], SQT[:])
                kb.inc(i1, "f3", 1)
                kb.inc(i2, "f3", 1)

        @block.gpsimd
        def _(g):
            dq.sc_loads(g, 3)

        @block.tensor
        def _(t):
            t.wait_ge(kb.sem("f3"), 64)
            t.wait_ge(kb.sem("epv"), 32)
            for s4 in range(4):
                i = t.matmul(p_big[:, s4 * 512:(s4 + 1) * 512], ONESF[:],
                             S1A[:, s4 * 512:(s4 + 1) * 512], start=True, stop=True)
            kb.inc(i, "c3t", 1)
            t.wait_ge(kb.sem("c3s"), 1)
            for s4 in range(4):
                i = t.matmul(p_big[:, s4 * 512:(s4 + 1) * 512], ONESF[:],
                             S2A[:, s4 * 512:(s4 + 1) * 512], start=True, stop=True)
            kb.inc(i, "c3t", 1)

        @block.scalar
        def _(s):
            s.wait_ge(kb.sem("c3t"), 1)
            kb.inc(s.activation(M1R[:], p_big[:], AF.Identity, scale=CN2K[:, 0:1]), "c3s", 1)
            s.wait_ge(kb.sem("c3t"), 2)
            kb.inc(s.activation(S2A[:], p_big[:], AF.Identity, scale=CN2K[:, 0:1]), "c3s", 1)
            s.wait_ge(kb.sem("c3v"), 1)
            kb.inc(s.activation(RSTD[:], TR1[:], AF.Sqrt), "c3s", 1)
            s.wait_ge(kb.sem("c3v"), 2)
            kb.inc(s.activation(GZ[:], TR1[:], AF.Sqrt), "c3s", 1)

        @block.vector
        def _(v):
            v.wait_ge(kb.sem("c3s"), 2)
            kb.vop(v.tensor_tensor(S1A[:], M1R[:], M1R[:], AL.mult))
            kb.vwait(v)
            kb.vop(v.tensor_tensor(S2A[:], S2A[:], S1A[:], AL.subtract))
            kb.vwait(v)
            kb.vop(v.tensor_scalar(S1A[:], S2A[:], 1e-5, None, AL.add, AL.bypass))
            kb.vwait(v)
            kb.inc(v.reciprocal(TR1[:], S1A[:]), "c3v", 1)
            v.wait_ge(kb.sem("c3s"), 3)
            kb.vwait(v)
            kb.w(v, "c3v", 1)
            kb.vop(v.tensor_tensor(S2A[:], S2A[:], TR1[:], AL.mult))
            kb.vwait(v)
            kb.vop(v.tensor_scalar(S2A[:], S2A[:], 1e-5, None, AL.add, AL.bypass))
            kb.vwait(v)
            kb.inc(v.reciprocal(TR1[:], S2A[:]), "c3v", 1)
            v.wait_ge(kb.sem("c3s"), 4)
            kb.vwait(v)
            kb.inc(v.tensor_tensor(GZ[:], GZ[:], RSTD[:], AL.mult), "c3v", 1)
            for hc in range(32):
                c, th = hc // 2, hc % 2
                kb.dw(v, "x1l3b", hc)
                kb.w(v, "p3z", hc)
                kb.vop(v.tensor_tensor(TR0[:], X1B[hc % 2][:], M1R[:, th * TH:(th + 1) * TH], AL.subtract))
                if hc >= 2:
                    kb.dw(v, "xn1s", hc - 2)
                kb.vwait(v)
                i = v.tensor_tensor(XN1S[hc % 2][:], TR0[:], RSTD[:, th * TH:(th + 1) * TH], AL.mult)
                kb.inc(i, "p3x", 1)
                i = v.scalar_tensor_tensor(Z[:, c * T + th * TH:c * T + (th + 1) * TH], TR0[:],
                                           RWH[:, c:c + 1], GZ[:, th * TH:(th + 1) * TH],
                                           AL.mult, AL.mult)
                kb.inc(i, "p3z", 1)
            dq.vec(v, 2)

        @block.scalar
        def _(s):
            for hc in range(32):
                if hc < 2:
                    s.wait_ge(kb.sem("f3"), 64)
                    s.wait_ge(kb.sem("sq3"), 32)
                else:
                    kb.w(s, "p3z", hc - 1)
                kb.dinc(s.dma_start(out=X1B[hc % 2][:], in_=x1src(hc)), "x1l3b", hc)

        @block.sync
        def _(sync):
            for hc in range(32):
                c, th = hc // 2, hc % 2
                sync.wait_ge(kb.sem("p3x"), hc + 1)
                kb.dinc(sync.dma_start(out=scr["xn1"].ap()[c * 128:(c + 1) * 128, th * TH:(th + 1) * TH],
                                      in_=XN1S[hc % 2][:]), "xn1s", hc)
            dq.pump_sync(sync, 24)

        free("S1A3", "S2A3", "SQT3", "M1R3", "TR03", "TR13", "WPB0", "WPB1", "RSTD3", "GZ3")

    if upto >= 4:
        WM = [[sb(f"WM{w}_{i}", [128, NCH * 128], BF16) for i in range(2)] for w in range(5)]
        AAc = sb("AAc", [128, T], F32)
        UUc = sb("UUc", [128, T], F32)
        HHc = sb("HHc", [128, T], F32)
        CAc = sb("CAc", [128, T], BF16)
        CCc = sb("CCc", [128, T], BF16)
        DDc = sb("DDc", [128, T], BF16)
        YB = [sb(f"YB{i}", [128, T], BF16) for i in range(2)]
        WKEYS = ["CawT", "CbwT", "AwT", "DwT", "BwT"]
        EVENG = ["s", "v", "s", "s", "v"]

        @block.gpsimd
        def _(g):
            for nn in range(NCH):
                if nn < 2:
                    g.wait_ge(kb.sem("ept"), 32)
                    g.wait_ge(kb.sem("p3z"), 32)
                else:
                    kb.w(g, "p4s", 6 * (nn - 1))
                    kb.w(g, "p4v", 4 * (nn - 1))
                for w in range(5):
                    for hf in range(2):
                        kb.inc(g.dma_start(
                            out=WM[w][nn % 2][:].rearrange("p (c o) -> p c o", o=128)[:, hf * 8:(hf + 1) * 8],
                            in_=ins[WKEYS[w]][nn, hf * 8:(hf + 1) * 8].transpose([1, 0, 2])), f"wm_{nn % 2}", 16)
                if nn >= 1:
                    dq.sc_loads(g, 2)

        @block.tensor
        def _(t):
            t.wait_ge(kb.sem("c3v"), 3)
            for nn in range(NCH):
                kb.dw(t, "wm", nn, 160)
                for w in range(5):
                    for th in range(2):
                        fi = 10 * nn + 2 * w + th
                        pf = fi - 2
                        if pf >= 0:
                            peng = EVENG[(pf % 10) // 2]
                            cnt = sum(1 for q in range(pf + 1) if EVENG[(q % 10) // 2] == peng)
                            kb.w(t, "p4" + peng, cnt)
                        half = (fi % 2) * 1024
                        for c in range(NCH):
                            if nn == 0 and w == 0 and th == 0:
                                t.wait_ge(kb.sem("p3z"), 2 * c + 2)
                            for s2 in range(2):
                                i = t.matmul(p_big[:, half + s2 * 512:half + (s2 + 1) * 512],
                                             WM[w][nn % 2][:, c * 128:(c + 1) * 128],
                                             Z[:, c * T + th * TH + s2 * 512:c * T + th * TH + (s2 + 1) * 512],
                                             start=(c == 0), stop=(c == NCH - 1))
                        kb.inc(i, "p4t", 1)

        @block.scalar
        def _(s):
            for nn in range(NCH):
                for w, th, dst, func, bias in [
                        (0, 0, CAc, AF.Silu, None), (0, 1, CAc, AF.Silu, None),
                        (2, 0, AAc, AF.Sigmoid, ABH), (2, 1, AAc, AF.Sigmoid, ABH),
                        (3, 0, DDc, AF.Sigmoid, None), (3, 1, DDc, AF.Sigmoid, None)]:
                    fi = 10 * nn + 2 * w + th
                    s.wait_ge(kb.sem("p4t"), fi + 1)
                    if nn > 0:
                        kb.w(s, "scn", nn)
                        kb.w(s, "p4y", nn)
                    half = (fi % 2) * 1024
                    kw = {}
                    if bias is not None:
                        kw["bias"] = bias[:, nn:nn + 1]
                    i = s.activation(dst[:, th * 1024:(th + 1) * 1024], p_big[:, half:half + 1024],
                                     func, **kw)
                    kb.inc(i, "p4s", 1)
                if nn >= 1:
                    s.wait_ge(kb.sem("p4y"), nn)
                    kb.dinc(s.dma_start(out=scr["y"].ap()[(nn - 1) * 128:nn * 128, :],
                                        in_=YB[(nn - 1) % 2][:]), "ys", nn - 1)

        @block.vector
        def _(v):
            for nn in range(NCH):
                if nn > 0:
                    kb.w(v, "scn", nn)
                for w, th in [(1, 0), (1, 1), (4, 0), (4, 1)]:
                    fi = 10 * nn + 2 * w + th
                    v.wait_ge(kb.sem("p4t"), fi + 1)
                    half = (fi % 2) * 1024
                    if w == 1:
                        v.wait_ge(kb.sem("p4s"), 6 * nn + th + 1)
                        i = v.scalar_tensor_tensor(CCc[:, th * 1024:(th + 1) * 1024],
                                                   p_big[:, half:half + 1024], 1.0,
                                                   CAc[:, th * 1024:(th + 1) * 1024], AL.mult, AL.mult)
                    else:
                        v.wait_ge(kb.sem("p4s"), 6 * nn + 4 + th + 1)
                        i = v.scalar_tensor_tensor(UUc[:, th * 1024:(th + 1) * 1024],
                                                   p_big[:, half:half + 1024], 1.0,
                                                   DDc[:, th * 1024:(th + 1) * 1024], AL.mult, AL.mult)
                    kb.inc(i, "p4v", 1)
                v.wait_ge(kb.sem("p4s"), 6 * nn + 4)
                kb.w(v, "p4v", 4 * nn + 4)
                i = v.tensor_tensor_scan(HHc[:], AAc[:], UUc[:], 0.0, AL.mult, AL.add)
                kb.inc(i, "scn", 1)
                if nn >= 2:
                    kb.dw(v, "ys", nn - 2)
                kb.w(v, "scn", nn + 1)
                i = v.scalar_tensor_tensor(YB[nn % 2][:], HHc[:], 1.0, CCc[:], AL.mult, AL.mult)
                kb.inc(i, "p4y", 1)
                dq.vec(v, 2)

        @block.scalar
        def _(s):
            s.wait_ge(kb.sem("p4y"), NCH)
            kb.dinc(s.dma_start(out=scr["y"].ap()[(NCH - 1) * 128:NCH * 128, :],
                                in_=YB[(NCH - 1) % 2][:]), "ys", NCH - 1)

        @block.sync
        def _(sync):
            dq.pump_sync(sync, 11, lag=3)

    if upto >= 5:
        free("AAc", "UUc", "HHc", "CAc", "CCc", "DDc",
             *[f"WM{w}_{i}" for w in range(5) for i in range(2)])
        YTl = [sb(f"YT{c}", [128, T], BF16) for c in range(NCH)]
        P1S = [sb(f"P1S{i}", [128, TH], F32) for i in range(2)]

        @block.sync
        def _(sync):
            sync.wait_ge(kb.sem("p4y"), NCH)
            for c in range(NCH):
                kb.dw(sync, "ys", c)
                kb.inc(sync.dma_start(out=YTl[c][:],
                                      in_=scr["y"].ap()[c * 128:(c + 1) * 128, :]), "yl", 16)
            sync.wait_ge(kb.sem("p4t"), 160)
            kb.dwfull(sync, "xn1s", 256, 256)
            for c in range(NCH):
                kb.inc(sync.dma_start(out=XN0[:, c * T:(c + 1) * T],
                                      in_=scr["xn1"].ap()[c * 128:(c + 1) * 128, :]), "xn1l", 16)
            dq.pump_sync(sync, 40)

        @block.gpsimd
        def _(g):
            for oc in range(NCH):
                if oc < 2:
                    g.wait_ge(kb.sem("ept"), 2 * NCH)
                else:
                    kb.w(g, "p5t", 2 * (oc - 1))
                for hf in range(2):
                    kb.inc(g.dma_start(
                        out=WPA[oc % 2][:].rearrange("p (c o) -> p c o", o=128)[:, hf * 8:(hf + 1) * 8],
                        in_=ins["OwT"][oc, hf * 8:(hf + 1) * 8].transpose([1, 0, 2])), f"wo5_{oc % 2}", 16)
                dq.sc_loads(g, 2)

        @block.tensor
        def _(t):
            t.wait_ge(kb.sem("p4s"), 96)
            t.wait_ge(kb.sem("p4v"), 64)
            t.wait_ge(kb.sem("yl"), 16 * NCH)
            for oc in range(NCH):
                kb.dw(t, "wo5", oc, 32)
                for th in range(2):
                    fi = 2 * oc + th
                    kb.w(t, "p5v", fi - 1)
                    half = (fi % 2) * 1024
                    tb = th * TH
                    for c in range(NCH):
                        for s2 in range(2):
                            i = t.matmul(p_big[:, half + s2 * 512:half + (s2 + 1) * 512],
                                         WPA[oc % 2][:, c * 128:(c + 1) * 128],
                                         YTl[c][:, tb + s2 * 512:tb + (s2 + 1) * 512],
                                         start=(c == 0), stop=(c == NCH - 1))
                    kb.inc(i, "p5t", 1)

        @block.scalar
        def _(s):
            for hc in range(32):
                c, th = hc // 2, hc % 2
                if hc < 2:
                    kb.dwfull(s, "xn1s", 256, 256)
                    s.wait_ge(kb.sem("p3z"), 32)
                else:
                    kb.w(s, "p5v", hc - 1)
                kb.dinc(s.dma_start(out=XN1S[hc % 2][:],
                                    in_=scr["xn1"].ap()[c * 128:(c + 1) * 128, th * TH:(th + 1) * TH]),
                        "p5l", hc)
                kb.dinc(s.dma_start(out=X1B[hc % 2][:], in_=x1src(hc)), "p5l", hc)
                if hc >= 2:
                    ph, pc, pt = hc - 2, (hc - 2) // 2, (hc - 2) % 2
                    kb.dinc(s.dma_start(out=scr["p1"].ap()[pc * 128:(pc + 1) * 128, pt * TH:(pt + 1) * TH],
                                        in_=P1S[ph % 2][:]), "p1st", ph)

        @block.vector
        def _(v):
            for hc in range(32):
                oc = hc // 2
                v.wait_ge(kb.sem("p5t"), hc + 1)
                kb.dw(v, "p5l", hc, 32)
                if hc >= 2:
                    kb.dw(v, "p1st", hc - 2)
                half = (hc % 2) * 1024
                kb.vop(v.scalar_tensor_tensor(P1S[hc % 2][:], p_big[:, half:half + 1024],
                                              OBH[:, oc:oc + 1], XN1S[hc % 2][:], AL.add, AL.add))
                kb.vwait(v)
                i = v.tensor_add(P1S[hc % 2][:], P1S[hc % 2][:], X1B[hc % 2][:])
                kb.inc(i, "p5v", 1)
                dq.vec(v, 1)

        @block.scalar
        def _(s):
            for hc in range(30, 32):
                c, th = hc // 2, hc % 2
                s.wait_ge(kb.sem("p5v"), hc + 1)
                kb.dinc(s.dma_start(out=scr["p1"].ap()[c * 128:(c + 1) * 128, th * TH:(th + 1) * TH],
                                    in_=P1S[hc % 2][:]), "p1st", hc)


        DQST = dq.nu * 16

    if upto >= 6:
        free("X1B0", "X1B1", "XN1S0", "XN1S1", "P1S0", "P1S1", "YB0", "YB1",
             "dq_pk0", "dq_pk1", "dq_sc0", "dq_sc1", "dq_out",
             "dq_t0", "dq_t1", "dq_t2", "dq_t3",
             "WPA0", "WPA1", *[f"YT{c}" for c in range(NCH)])
        XN1R = XN0
        HQ = sb("HQ", [128, NGC * 512], BF16)
        WG6 = [sb(f"WG6{i}", [128, NCH * 256], BF16) for i in range(2)]
        WGU6 = [sb(f"WGU6{i}", [128, NCH * 256], BF16) for i in range(2)]
        WD6 = [sb(f"WD6{i}", [128, 11 * 512], BF16) for i in range(2)]
        SG6 = sb("SG6", [128, 512], BF16)
        YF = sb("YF", [128, NCH * 512], BF16)
        SQ6 = sb("SQ6", [128, 512], F32)
        S2F = sb("S2F", [128, 512], F32)
        GF = sb("GF", [128, 512], F32)
        P1B = [sb(f"P1B{i}", [128, 512], F32) for i in range(2)]
        X2B = [sb(f"X2B{i}", [128, 512], F32) for i in range(2)]

        @block.sync
        def _(sync):
            sync.wait_ge(kb.sem("p5t"), 2 * NCH)
            kb.dwfull(sync, "p1st", 256, 256)
            sync.wait_ge(kb.sem("dq_st"), 16 * (dq.split if dq.split is not None else dq.nu))
            if dq.split is not None:
                sync.wait_ge(kb.sem("dq_st2"), 16 * (dq.nu - dq.split))
            for tq in range(4):
                for g2 in range(22):
                    li = tq * 22 + g2
                    if tq > 0 and 2 <= g2 <= 17:
                        sdc = g2 - 2
                        sli = (tq - 1) * 16 + sdc
                        sync.wait_ge(kb.sem("p6x2"), sli + 1)
                        kb.dinc(sync.dma_start(out=out[sdc * 128:(sdc + 1) * 128, (tq - 1) * 512:tq * 512],
                                               in_=X2B[sdc % 2][:]), "outs", sli)
                    kb.w(sync, "hqv", (li - 1) * 2)
                    for hf in range(2):
                        kb.inc(sync.dma_start(
                            out=WG6[li % 2][:].rearrange("p (c o) -> p c o", o=256)[:, hf * 8:(hf + 1) * 8],
                            in_=scr["wgu"].ap().rearrange("(c p) o -> p c o", p=128)[
                                :, hf * 8:(hf + 1) * 8, g2 * 256:(g2 + 1) * 256]), f"wg6_{li % 2}", 16)
                        kb.inc(sync.dma_start(
                            out=WGU6[li % 2][:].rearrange("p (c o) -> p c o", o=256)[:, hf * 8:(hf + 1) * 8],
                            in_=scr["wgu"].ap().rearrange("(c p) o -> p c o", p=128)[
                                :, hf * 8:(hf + 1) * 8, IH + g2 * 256:IH + (g2 + 1) * 256]), f"wg6_{li % 2}", 16)
                for dg in range(4):
                    for kq in range(4):
                        di = tq * 16 + dg * 4 + kq
                        kb.w(sync, "p6dk", di - 1)
                        kb.inc(sync.dma_start(
                            out=WD6[di % 2][:].rearrange("p (c o) -> p c o", o=512),
                            in_=scr["wdn"].ap().rearrange("(c p) o -> p c o", p=128)[
                                :, kq * 11:(kq + 1) * 11, dg * 512:(dg + 1) * 512]), f"wd6_{di % 2}", 16)

        @block.tensor
        def _(t):
            t.wait_ge(kb.sem("xn1l"), 16 * NCH)
            for tq in range(4):
                tb = tq * 512
                for g2 in range(22):
                    li = tq * 22 + g2
                    kb.dw(t, "wg6", li, 64)
                    for j in range(2):
                        gi6 = li * 2 + j
                        kb.w(t, "p6gs", gi6)
                        kb.w(t, "hqv", gi6)
                        for c in range(NCH):
                            i = t.matmul(p_m1[:, 0:512],
                                         WG6[li % 2][:, c * 256 + j * 128:c * 256 + (j + 1) * 128],
                                         XN1R[:, c * T + tb:c * T + tb + 512],
                                         start=(c == 0), stop=(c == NCH - 1))
                        kb.inc(i, "p6g", 1)
                        for c in range(NCH):
                            i = t.matmul(p_m2[:, 0:512],
                                         WGU6[li % 2][:, c * 256 + j * 128:c * 256 + (j + 1) * 128],
                                         XN1R[:, c * T + tb:c * T + tb + 512],
                                         start=(c == 0), stop=(c == NCH - 1))
                        kb.inc(i, "p6u", 1)
                for dg in range(4):
                    for kq in range(4):
                        di = tq * 16 + dg * 4 + kq
                        kb.dw(t, "wd6", di, 16)
                        if kq == 0:
                            t.wait_ge(kb.sem("hqv"), 44 * (tq + 1))
                            for dc4 in range(4):
                                fli = tq * 16 + dg * 4 + dc4
                                kb.w(t, "p6sq", fli - 3)
                        for ch in range(11):
                            for dc4 in range(4):
                                i = t.matmul(p_big[:, dc4 * 512:(dc4 + 1) * 512],
                                             WD6[di % 2][:, ch * 512 + dc4 * 128:ch * 512 + dc4 * 128 + 128],
                                             HQ[:, (kq * 11 + ch) * 512:(kq * 11 + ch + 1) * 512],
                                             start=(kq == 0 and ch == 0), stop=(kq == 3 and ch == 10))
                        kb.inc(i, "p6dk", 1)
                t.wait_ge(kb.sem("p6sf"), 16 * (tq + 1))
                kb.inc(t.matmul(p_st[:, 0:512], ONESF[:], S2F[:], start=True, stop=True),
                       "p6st", 1)

        @block.scalar
        def _(s):
            kb.dwfull(s, "p1st", 256, 256)
            kb.dinc(s.dma_start(out=P1B[0][:], in_=scr["p1"].ap()[0:128, 0:512]), "p1l6", 0)
            kb.dinc(s.dma_start(out=P1B[1][:], in_=scr["p1"].ap()[128:256, 0:512]), "p1l6", 1)
            for tq in range(4):
                for og in range(44):
                    gi6 = tq * 44 + og
                    if tq > 0 and og % 2 == 0 and 4 <= og <= 34:
                        pli = (tq - 1) * 16 + og // 2
                        pdc, ptq = pli % 16, pli // 16
                        kb.w(s, "p6x2", pli - 1)
                        kb.dinc(s.dma_start(out=P1B[pdc % 2][:],
                                            in_=scr["p1"].ap()[pdc * 128:(pdc + 1) * 128,
                                                               ptq * 512:(ptq + 1) * 512]),
                                "p1l6", pli)
                    s.wait_ge(kb.sem("p6g"), gi6 + 1)
                    kb.w(s, "hqv", gi6)
                    kb.inc(s.activation(SG6[:], p_m1[:, 0:512], AF.Silu), "p6gs", 1)
                for dgi in range(4):
                    s.wait_ge(kb.sem("p6dk"), tq * 16 + (dgi + 1) * 4)
                    for dc4 in range(4):
                        fli = tq * 16 + dgi * 4 + dc4
                        i = s.activation(YF[:, (dgi * 4 + dc4) * 512:(dgi * 4 + dc4 + 1) * 512],
                                         p_big[:, dc4 * 512:(dc4 + 1) * 512], AF.Copy)
                        kb.inc(i, "p6ys", 1)
                        kb.w(s, "p6sf", fli)
                        i = s.activation(SQ6[:], p_big[:, dc4 * 512:(dc4 + 1) * 512], AF.Square)
                        kb.inc(i, "p6sq", 1)
                s.wait_ge(kb.sem("p6v2"), tq + 1)
                kb.w(s, "p6x2", 16 * tq)
                kb.inc(s.activation(GF[:], S2F[:], AF.Sqrt), "p6s2", 1)
            for li in range(3 * 16 + 2, 64):
                dc = li % 16
                kb.w(s, "p6x2", li - 1)
                kb.dinc(s.dma_start(out=P1B[dc % 2][:],
                                    in_=scr["p1"].ap()[dc * 128:(dc + 1) * 128, 3 * 512:4 * 512]),
                        "p1l6", li)

        @block.vector
        def _(v):
            def x2_chunk(li):
                dc, ltq = li % 16, li // 16
                kb.dw(v, "p1l6", li)
                if li >= 2:
                    kb.dw(v, "outs", li - 2)
                v.wait_ge(kb.sem("p6s2"), ltq + 1)
                kb.vwait(v)
                kb.vop(v.scalar_tensor_tensor(X2B[dc % 2][:], YF[:, dc * 512:(dc + 1) * 512], 1.0,
                                              GF[:], AL.mult, AL.mult))
                kb.vwait(v)
                i = v.tensor_add(X2B[dc % 2][:], X2B[dc % 2][:], P1B[dc % 2][:])
                kb.inc(i, "p6x2", 1)

            for tq in range(4):
                for og in range(44):
                    gi6 = tq * 44 + og
                    if og == 0 and tq > 0:
                        v.wait_ge(kb.sem("p6dk"), 16 * tq)
                    v.wait_ge(kb.sem("p6gs"), gi6 + 1)
                    v.wait_ge(kb.sem("p6u"), gi6 + 1)
                    i = v.scalar_tensor_tensor(HQ[:, og * 512:(og + 1) * 512],
                                               p_m2[:, 0:512], 1.0, SG6[:], AL.mult, AL.mult)
                    kb.inc(i, "hqv", 1)
                    if tq > 0 and og % 2 == 1 and og < 32:
                        x2_chunk((tq - 1) * 16 + og // 2)
                for fi16 in range(16):
                    v.wait_ge(kb.sem("p6sq"), tq * 16 + fi16 + 1)
                    if fi16 == 0:
                        kb.w(v, "p6s2", tq)
                        i = v.tensor_copy(S2F[:], SQ6[:])
                    else:
                        i = v.tensor_add(S2F[:], S2F[:], SQ6[:])
                    kb.inc(i, "p6sf", 1)
                v.wait_ge(kb.sem("p6st"), tq + 1)
                kb.vwait(v)
                kb.vop(v.tensor_scalar(S2F[:], p_st[:, 0:512], 1.0 / D, 1e-6, AL.mult, AL.add))
                kb.vwait(v)
                kb.inc(v.reciprocal(S2F[:], S2F[:]), "p6v2", 1)
            for li in range(3 * 16, 64):
                x2_chunk(li)


        @block.sync
        def _(sync):
            for dc in range(NCH):
                li = 3 * 16 + dc
                sync.wait_ge(kb.sem("p6x2"), li + 1)
                kb.dinc(sync.dma_start(out=out[dc * 128:(dc + 1) * 128, 3 * 512:4 * 512],
                                       in_=X2B[dc % 2][:]), "outs", li)
            kb.dwfull(sync, "outs", 16 * 32, 16 * 32)

    while dq.st_i < dq.nu or dq.vec_i < dq.nu or dq.sc_i < dq.nu or dq.ld_i < dq.nu:
        @block.gpsimd
        def _(g):
            dq.sc_loads(g, 6)

        @block.vector
        def _(v):
            dq.vec(v, 6)

        @block.sync
        def _(sync):
            dq.pump_sync(sync, 6)

    @block.sync
    def _(sync):
        for nm in sorted(kb.dma_sems):
            sync.wait_ge(kb.sem(nm), kb.cnt[nm])

    blockg.__exit__(None, None, None)
    for g in kb._semctx[::-1]:
        g.__exit__(None, None, None)
    return nc


_CACHE = {}


def kernel(**inputs):
    if "nc" not in _CACHE:
        _CACHE["nc"] = build_program()
    nc = _CACHE["nc"]
    shared = stage_shared(inputs)
    x = np.asarray(inputs["L0_attn_x"])
    in_maps = []
    for c in range(B):
        m = dict(shared)
        m["xT"] = np.ascontiguousarray(x[c].T[SIGD])
        in_maps.append(m)
    res = run_bass_kernel_spmd(nc, in_maps, core_ids=list(range(B)))
    outp = np.empty((B, T, D), np.float32)
    for c in range(B):
        x2T = res.results[c]["x2T"]
        fm = np.empty((D, T), np.float32)
        fm[SIGD] = x2T
        outp[c] = fm.T
    return outp


# version 06 (final)
# speedup vs baseline: 1.3384x; 1.0266x over previous
import numpy as np
import ml_dtypes
import concourse.bass as bass
import concourse.mybir as mybir
from concourse.bass_utils import run_bass_kernel_spmd

F32 = mybir.dt.float32
BF16 = mybir.dt.bfloat16
F16 = mybir.dt.float16
I16 = mybir.dt.int16
U16 = mybir.dt.uint16
AL = mybir.AluOpType
AF = mybir.ActivationFunctionType
AX = mybir.AxisListType

B = 8
D = 2048
T = 2048
TH = 1024
NCH = 16
NH = 16
HD = 128
EE = 8
ER = 128
IH = 5632
GUO = 11264
NGC = 44
RSQ_HD = float(1.0 / np.sqrt(128.0))
EXP_BIAS = -4.0
BF = np.dtype(ml_dtypes.bfloat16)

_p = np.arange(D)
SIGD = 4 * (_p % 512) + _p // 512
_q = np.arange(IH)
SIGH = 4 * (_q % 1408) + _q // 1408



def _tile_chunks(wT):
    r = np.empty((NCH, NCH, 128, 128), np.float32)
    for oc in range(NCH):
        r[oc] = wT[:, oc * 128:(oc + 1) * 128].reshape(NCH, 128, 128)
    return np.ascontiguousarray(r)


def stage_shared(inputs):
    g = {}
    sd, sh = SIGD, SIGH
    wq = np.ascontiguousarray(inputs["L0_attn_w_project"]).T[sd]
    t = np.empty((NH, NCH, 128, 384), np.float32)
    for h in range(NH):
        cols = np.concatenate([wq[:, k * D + h * HD:k * D + (h + 1) * HD] for k in range(3)], axis=1)
        t[h] = cols.reshape(NCH, 128, 384)
    g["wqkv"] = np.ascontiguousarray(t)
    g["woT"] = _tile_chunks(np.ascontiguousarray(inputs["L0_attn_w_out"]).T[:, sd])
    g["baseT"] = _tile_chunks(np.ascontiguousarray(inputs["L0_ffn_base_w"]).T[sd][:, sd])
    g["base_b"] = np.ascontiguousarray(np.asarray(inputs["L0_ffn_base_b"])[sd].reshape(NCH, 128).T)
    g["routerT"] = np.ascontiguousarray(
        np.asarray(inputs["L0_ffn_router_w"]).T[sd].reshape(NCH, 128, EE).transpose(1, 0, 2).reshape(128, NCH * EE))
    g["router_b"] = np.ascontiguousarray(np.asarray(inputs["L0_ffn_router_b"]).reshape(EE, 1))
    g["aallT"] = np.ascontiguousarray(
        np.asarray(inputs["L0_ffn_lora_A"]).transpose(1, 0, 2).reshape(D, ER)[sd]
        .reshape(NCH, 128, ER).transpose(1, 0, 2).reshape(128, NCH * ER))
    g["ballT"] = np.ascontiguousarray(np.asarray(inputs["L0_ffn_lora_B"]).reshape(ER, D)[:, sd])
    g["rms_w"] = np.ascontiguousarray(np.asarray(inputs["L1_attn_rms_w"])[sd].reshape(NCH, 128).T)
    for nm, key in [("AwT", "L1_attn_A_w"), ("BwT", "L1_attn_B_w"), ("CawT", "L1_attn_Ca_w"),
                    ("CbwT", "L1_attn_Cb_w"), ("DwT", "L1_attn_D_w")]:
        g[nm] = _tile_chunks(np.ascontiguousarray(inputs[key]).T[sd])
    g["A_b"] = np.ascontiguousarray(np.asarray(inputs["L1_attn_A_b"]).reshape(NCH, 128).T)
    g["OwT"] = _tile_chunks(np.ascontiguousarray(inputs["L1_attn_out_w"]).T[:, sd])
    g["out_b"] = np.ascontiguousarray(np.asarray(inputs["L1_attn_out_b"])[sd].reshape(NCH, 128).T)
    rowp = np.concatenate([sh, IH + sh])
    gu16 = np.ascontiguousarray(np.asarray(inputs["L1_ffn_gate_up_packed"]).view(np.int16)[rowp].T)
    g["gu16"] = np.ascontiguousarray(gu16.reshape(4, 128, GUO))
    gsc = np.asarray(inputs["L1_ffn_gate_up_scales"])[:, rowp]
    g["gusc"] = np.ascontiguousarray(np.repeat(gsc, 32, axis=0).reshape(4, 128, GUO))
    dn16 = np.ascontiguousarray(np.asarray(inputs["L1_ffn_down_packed"])[sd].view(np.int16).T)
    g["dn16"] = np.ascontiguousarray(dn16.reshape(11, 128, D))
    dsc = np.asarray(inputs["L1_ffn_down_scales"])[:, sd]
    g["dnsc"] = np.ascontiguousarray(np.repeat(dsc, 32, axis=0).reshape(11, 128, D))
    return g



_DTSIZE = {F32: 4, BF16: 2, F16: 2, I16: 2, U16: 2}


class KB:

    def __init__(self, nc):
        self.nc = nc
        self.sems = {}
        self.cnt = {}
        self.guards = {}
        self._semctx = []
        self.dma_sems = set()
        base = (nc.sbuf_base + 31) // 32 * 32
        self.freelist = [(base, nc.sbuf_top)]
        self.live = {}
        self.hwm = 0

    def sem(self, name):
        if name not in self.sems:
            g = self.nc.semaphore(name)
            h = g.__enter__()
            self._semctx.append(g)
            self.sems[name] = h
            self.cnt[name] = 0
        return self.sems[name]

    def inc(self, instr, name, k=1):
        instr.then_inc(self.sem(name), k)
        self.cnt[name] += k
        if k >= 16 and k % 16 == 0:
            self.dma_sems.add(name)
        return self.cnt[name]

    def n(self, name):
        self.sem(name)
        return self.cnt[name]

    def w(self, eng, name, val):
        if val > 0:
            eng.wait_ge(self.sem(name), val)

    def dinc(self, instr, name, idx, k=16):
        self.inc(instr, f"{name}_{idx % 2}", k)

    def dw(self, eng, name, idx, k=16):
        eng.wait_ge(self.sem(f"{name}_{idx % 2}"), k * (idx // 2 + 1))

    def dwfull(self, eng, name, total0, total1):
        self.w(eng, f"{name}_0", total0)
        self.w(eng, f"{name}_1", total1)

    def vop(self, instr, name="vch"):
        return self.inc(instr, name, 1)

    def vwait(self, eng, name="vch"):
        self.w(eng, name, self.n(name))

    def sb(self, name, shape, dt):
        nb = 1
        for s in shape[1:]:
            nb *= s
        nb *= _DTSIZE[dt]
        nb = (nb + 31) // 32 * 32
        for idx, (lo, hi) in enumerate(self.freelist):
            if hi - lo >= nb:
                self.freelist[idx] = (lo + nb, hi)
                if lo + nb == hi:
                    self.freelist.pop(idx)
                self.live[name] = (lo, lo + nb)
                used = sum(b - a for (a, b) in (v for v in self.live.values()))
                self.hwm = max(self.hwm, used)
                return self.nc.alloc_sbuf_tensor_at(name, shape, dt, offset=lo)
        raise MemoryError(f"arena OOM for {name} need {nb} free={self.freelist}")

    def free(self, *names):
        for n in names:
            lo, hi = self.live.pop(n)
            self.freelist.append((lo, hi))
        self.freelist.sort()
        merged = []
        for lo, hi in self.freelist:
            if merged and merged[-1][1] >= lo:
                merged[-1] = (merged[-1][0], max(merged[-1][1], hi))
            else:
                merged.append((lo, hi))
        self.freelist = [tuple(x) for x in merged]

    def ps(self, name, shape, dt=F32):
        g = self.nc.psum_tensor(name, shape, dt)
        t = g.__enter__()
        self.guards[name] = g
        return t


class DQ:

    def __init__(self, kb, ins, scr):
        self.kb = kb
        self.units = [("gu", wb, sl) for wb in range(4) for sl in range(22)] + \
                     [("dn", wb, sl) for wb in range(11) for sl in range(4)]
        self.nu = len(self.units)
        self.ld_i = self.sc_i = self.vec_i = self.st_i = 0
        self.ins = ins
        self.scr = scr
        self.pk = [kb.sb(f"dq_pk{i}", [128, 512], I16) for i in range(2)]
        self.sc = [kb.sb(f"dq_sc{i}", [128, 512], F32) for i in range(2)]
        self.out = kb.sb("dq_out", [128, 4 * 512], BF16)
        self.t0 = kb.sb("dq_t0", [128, 512], I16)
        self.t1 = kb.sb("dq_t1", [128, 512], I16)
        self.t2 = kb.sb("dq_t2", [128, 512], I16)
        self.t3 = kb.sb("dq_t3", [128, 512], I16)

    def _src(self, u):
        kind, wb, sl = u
        if kind == "gu":
            return (self.ins["gu16"][wb, :, sl * 512:(sl + 1) * 512],
                    self.ins["gusc"][wb, :, sl * 512:(sl + 1) * 512])
        return (self.ins["dn16"][wb, :, sl * 512:(sl + 1) * 512],
                self.ins["dnsc"][wb, :, sl * 512:(sl + 1) * 512])

    def pump_sync(self, eng, n, lag=1):
        kb = self.kb
        for _ in range(n):
            progressed = False
            while self.st_i < self.nu and (self.st_i <= self.ld_i - lag or self.ld_i >= self.nu):
                self._store(eng)
                progressed = True
            if self.ld_i < self.nu:
                i = self.ld_i
                kb.w(eng, "dq_v", (i - 1) * 4)
                pk_src, _ = self._src(self.units[i])
                kb.dinc(eng.dma_start(out=self.pk[i % 2][:], in_=pk_src), "dq_pk", i)
                self.ld_i += 1
                progressed = True
            if not progressed:
                break

    def _store(self, eng):
        kb = self.kb
        i = self.st_i
        eng.wait_ge(kb.sem("dq_v"), (i + 1) * 4)
        kind, wb, sl = self.units[i]
        scrt = self.scr["wgu"] if kind == "gu" else self.scr["wdn"]
        ncols = GUO if kind == "gu" else D
        R = 4 if kind == "gu" else 11
        off = wb * 128 * ncols + sl * 512
        dst = bass.AP(scrt, off, [[ncols, 128], [R * 128 * ncols, 4], [1, 512]])
        srcap = self.out[:].rearrange("p (j o) -> p j o", j=4)
        kb.inc(eng.dma_start(out=dst, in_=srcap), "dq_st", 16)
        self.st_i += 1

    def sc_loads(self, eng, n):
        kb = self.kb
        for _ in range(n):
            if self.sc_i >= self.nu:
                return
            i = self.sc_i
            kb.w(eng, "dq_v", (i - 1) * 4)
            _, sc_src = self._src(self.units[i])
            kb.dinc(eng.dma_start(out=self.sc[i % 2][:], in_=sc_src), "dq_sc", i)
            self.sc_i += 1

    def vec(self, v, n):
        kb = self.kb
        for _ in range(n):
            if self.vec_i >= self.nu:
                return
            i = self.vec_i
            kb.dw(v, "dq_pk", i)
            kb.dw(v, "dq_sc", i)
            kb.w(v, "dq_st", 16 * i)
            W16, S = self.pk[i % 2], self.sc[i % 2]
            O = self.out
            for jp in range(4):
                sh = 4 * jp - 6
                kb.w(v, "dq_v", 4 * i + jp)
                kb.vwait(v)
                if sh < 0:
                    kb.vop(v.tensor_scalar(self.t0[:], W16[:], -sh, 0x1C0, AL.logical_shift_left, AL.bitwise_and))
                else:
                    kb.vop(v.tensor_scalar(self.t0[:], W16[:], sh, 0x1C0, AL.logical_shift_right, AL.bitwise_and))
                kb.vop(v.tensor_scalar(self.t1[:], W16[:], 4 * jp + 3, 15, AL.logical_shift_right, AL.logical_shift_left))
                kb.vwait(v)
                kb.vop(v.tensor_single_scalar(self.t2[:], self.t0[:], 64, AL.is_ge))
                kb.vwait(v)
                kb.vop(v.scalar_tensor_tensor(self.t2[:], self.t2[:], 16064, self.t0[:], AL.mult, AL.add))
                kb.vop(v.tensor_single_scalar(self.t3[:], self.t0[:], 128, AL.is_ge))
                kb.vwait(v)
                kb.vop(v.scalar_tensor_tensor(self.t2[:], self.t3[:], 64, self.t2[:], AL.mult, AL.add))
                kb.vwait(v)
                kb.vop(v.tensor_tensor(self.t3[:], self.t2[:], self.t1[:], AL.bitwise_or))
                kb.vwait(v)
                i2 = v.tensor_tensor(O[:, jp * 512:(jp + 1) * 512], self.t3[:].bitcast(BF16), S[:], AL.mult)
                kb.inc(i2, "dq_v", 1)
            self.vec_i += 1




def _att_sched():
    groups = []
    for qq in range(4):
        kts = list(range(4 * qq + 4))
        for i in range(0, len(kts), 2):
            rec = []
            for kt in kts[i:i + 2]:
                dt = kt - 4 * qq
                qs = max(0, 128 * dt)
                rec.append((kt, qs, dt if 0 <= dt <= 3 else None))
            groups.append((qq, rec))
    return groups


ATT_GROUPS = _att_sched()
ATT_NG = len(ATT_GROUPS)
ATT_NEXP = [1 if all(qs == 0 for (_, qs, _) in rec) else len(rec) for (_, rec) in ATT_GROUPS]
ATT_CUMEXP = np.cumsum(ATT_NEXP).tolist()
ATT_NMASK = [sum(1 for (_, _, dg) in rec if dg is not None) for (_, rec) in ATT_GROUPS]
ATT_CUMMASK = np.cumsum(ATT_NMASK).tolist()
ATT_QLAST = {}
for _gi, (_qq, _) in enumerate(ATT_GROUPS):
    ATT_QLAST[_qq] = _gi
ATT_TOTEXP = ATT_CUMEXP[-1]
ATT_TOTMASK = ATT_CUMMASK[-1]


def build_program(upto=99):
    nc = bass.Bass("TRN2", target_bir_lowering=False)
    ins = {}
    ins["xT"] = nc.dram_tensor("xT", [D, T], F32, kind="ExternalInput").ap()
    ins["wqkv"] = nc.dram_tensor("wqkv", [NH, NCH, 128, 384], F32, kind="ExternalInput").ap()
    for nm in ["woT", "baseT", "OwT", "AwT", "BwT", "CawT", "CbwT", "DwT"]:
        ins[nm] = nc.dram_tensor(nm, [NCH, NCH, 128, 128], F32, kind="ExternalInput").ap()
    for nm, shape in [("base_b", [128, NCH]), ("A_b", [128, NCH]), ("out_b", [128, NCH]),
                      ("rms_w", [128, NCH]), ("routerT", [128, NCH * EE]),
                      ("router_b", [EE, 1]), ("aallT", [128, NCH * ER]), ("ballT", [ER, D])]:
        ins[nm] = nc.dram_tensor(nm, shape, F32, kind="ExternalInput").ap()
    ins["gu16"] = nc.dram_tensor("gu16", [4, 128, GUO], I16, kind="ExternalInput").ap()
    ins["gusc"] = nc.dram_tensor("gusc", [4, 128, GUO], F16, kind="ExternalInput").ap()
    ins["dn16"] = nc.dram_tensor("dn16", [11, 128, D], I16, kind="ExternalInput").ap()
    ins["dnsc"] = nc.dram_tensor("dnsc", [11, 128, D], F16, kind="ExternalInput").ap()

    out = nc.dram_tensor("x2T", [D, T], F32, kind="ExternalOutput").ap()

    scr = {
        "x1": nc.dram_tensor("s_x1", [D, T], F32, kind="Internal"),
        "xn1": nc.dram_tensor("s_xn1", [D, T], BF16, kind="Internal"),
        "y": nc.dram_tensor("s_y", [D, T], BF16, kind="Internal"),
        "p1": nc.dram_tensor("s_p1", [D, T], F32, kind="Internal"),
        "wgu": nc.dram_tensor("s_wgu", [D, GUO], BF16, kind="Internal"),
        "wdn": nc.dram_tensor("s_wdn", [IH, D], BF16, kind="Internal"),
    }

    ident_np = np.eye(128, dtype=BF).view(np.uint16)
    identf_np = np.eye(128, dtype=np.float32)
    trimask_np = (np.arange(128)[:, None] <= np.arange(128)[None, :]).astype(BF).view(np.uint16)
    emap_np = np.zeros((EE, ER), BF)
    for e in range(EE):
        emap_np[e, e * 16:(e + 1) * 16] = 1.0
    c_ident = nc.inline_tensor(ident_np, "c_ident")
    c_identf = nc.inline_tensor(identf_np, "c_identf")
    c_tri = nc.inline_tensor(trimask_np, "c_tri")
    c_emap = nc.inline_tensor(emap_np.view(np.uint16), "c_emap")

    kb = KB(nc)
    sb, free = kb.sb, kb.free

    ONESF = sb("ONESF", [128, 128], F32)
    ONESB = sb("ONESB", [128, 128], BF16)
    IDENT = sb("IDENT", [128, 128], BF16)
    IDENTF = sb("IDENTF", [128, 128], F32)
    TRIM = sb("TRIM", [128, 128], BF16)
    EMAP = sb("EMAP", [EE, ER], BF16)
    CEXPS = sb("CEXPS", [128, 1], F32)
    CEXPB = sb("CEXPB", [128, 1], F32)
    CN2K = sb("CN2K", [128, 1], F32)
    BBH = sb("BBH", [128, NCH], F32)
    OBH = sb("OBH", [128, NCH], F32)
    ABH = sb("ABH", [128, NCH], F32)
    RWH = sb("RWH", [128, NCH], F32)
    RBH = sb("RBH", [EE, 1], F32)
    BAL = sb("BAL", [ER, D], BF16)
    AAL = sb("AAL", [128, NCH * ER], BF16)
    RTW = sb("RTW", [128, NCH * EE], BF16)
    XN0 = sb("XN0", [128, NCH * T], BF16)
    WQH = [sb(f"WQH{i}", [128, NCH * 384], BF16) for i in range(2)]
    XB = [sb(f"XB{i}", [128, TH], F32) for i in range(2)]
    dq = DQ(kb, ins, scr)

    blockg = nc.Block()
    block = blockg.__enter__()

    @block.gpsimd
    def _(g):
        for tile, val in [(ONESF, 1.0), (ONESB, 1.0), (CEXPS, RSQ_HD),
                          (CEXPB, EXP_BIAS), (CN2K, 1.0 / D)]:
            kb.inc(g.memset(tile[:], val), "inim", 1)
        for dst, src in [(BBH[:], ins["base_b"]), (OBH[:], ins["out_b"]),
                         (ABH[:], ins["A_b"]), (RWH[:], ins["rms_w"]),
                         (RBH[:], ins["router_b"]), (BAL[:], ins["ballT"]),
                         (AAL[:], ins["aallT"]), (RTW[:], ins["routerT"])]:
            kb.inc(g.dma_start(out=dst, in_=src), "ini", 16)
        kb.inc(g.dma_start(out=IDENT[:].bitcast(U16), in_=c_ident.ap()), "ini", 16)
        kb.inc(g.dma_start(out=IDENTF[:], in_=c_identf.ap()), "ini", 16)
        kb.inc(g.dma_start(out=TRIM[:].bitcast(U16), in_=c_tri.ap()), "ini", 16)
        kb.inc(g.dma_start(out=EMAP[:].bitcast(U16), in_=c_emap.ap()), "ini", 16)
    INI = kb.n("ini")
    INIM = kb.n("inim")

    M1R = sb("M1R", [128, T], F32)
    RSTD = sb("RSTD", [128, T], F32)
    S1A = sb("S1A", [128, T], F32)
    S2A = sb("S2A", [128, T], F32)
    SQT = sb("SQT", [128, TH], F32)
    TR0 = sb("TR0", [128, TH], F32)
    p_big = kb.ps("p_big", [128, 2048], F32)
    p_m1 = kb.ps("p_m1", [128, 512], F32)
    p_m2 = kb.ps("p_m2", [128, 512], F32)
    p_tr = kb.ps("p_tr", [128, 1024], BF16)
    p_st = kb.ps("p_st", [128, 512], F32)

    def xsrc(hc):
        c, th = hc // 2, hc % 2
        return ins["xT"][c * 128:(c + 1) * 128, th * TH:(th + 1) * TH]

    @block.sync
    def _(sync):
        for hc in range(32):
            kb.w(sync, "f0", 2 * (hc - 1))
            kb.dinc(sync.dma_start(out=XB[hc % 2][:], in_=xsrc(hc)), "xl0", hc)
        dq.pump_sync(sync, 2, lag=99)

    @block.scalar
    def _(s):
        for hc in range(32):
            kb.dw(s, "xl0", hc)
            kb.w(s, "f0", 2 * hc)
            kb.inc(s.activation(SQT[:], XB[hc % 2][:], AF.Square), "sq0", 1)

    @block.vector
    def _(v):
        for hc in range(32):
            th = hc % 2
            kb.dw(v, "xl0", hc)
            v.wait_ge(kb.sem("sq0"), hc + 1)
            if hc < 2:
                i1 = v.tensor_copy(S1A[:, th * TH:(th + 1) * TH], XB[hc % 2][:])
                i2 = v.tensor_copy(S2A[:, th * TH:(th + 1) * TH], SQT[:])
            else:
                i1 = v.tensor_add(S1A[:, th * TH:(th + 1) * TH], S1A[:, th * TH:(th + 1) * TH], XB[hc % 2][:])
                i2 = v.tensor_add(S2A[:, th * TH:(th + 1) * TH], S2A[:, th * TH:(th + 1) * TH], SQT[:])
            kb.inc(i1, "f0", 1)
            kb.inc(i2, "f0", 1)

    @block.tensor
    def _(t):
        t.wait_ge(kb.sem("ini"), INI)
        t.wait_ge(kb.sem("inim"), INIM)
        t.wait_ge(kb.sem("f0"), 64)
        for s4 in range(4):
            i = t.matmul(p_big[:, s4 * 512:(s4 + 1) * 512], ONESF[:],
                         S1A[:, s4 * 512:(s4 + 1) * 512], start=True, stop=True)
        kb.inc(i, "c0t", 1)
        t.wait_ge(kb.sem("c0s"), 1)
        for s4 in range(4):
            i = t.matmul(p_big[:, s4 * 512:(s4 + 1) * 512], ONESF[:],
                         S2A[:, s4 * 512:(s4 + 1) * 512], start=True, stop=True)
        kb.inc(i, "c0t", 1)

    @block.scalar
    def _(s):
        s.wait_ge(kb.sem("c0t"), 1)
        kb.inc(s.activation(M1R[:], p_big[:], AF.Identity, scale=CN2K[:, 0:1]), "c0s", 1)
        s.wait_ge(kb.sem("c0t"), 2)
        kb.inc(s.activation(S2A[:], p_big[:], AF.Identity, scale=CN2K[:, 0:1]), "c0s", 1)
        s.wait_ge(kb.sem("c0v"), 1)
        kb.inc(s.activation(RSTD[:], S1A[:], AF.Sqrt), "c0s", 1)

    @block.vector
    def _(v):
        v.wait_ge(kb.sem("c0s"), 2)
        kb.vop(v.tensor_tensor(S1A[:], M1R[:], M1R[:], AL.mult))
        kb.vwait(v)
        kb.vop(v.tensor_tensor(S2A[:], S2A[:], S1A[:], AL.subtract))
        kb.vwait(v)
        kb.vop(v.tensor_scalar(S2A[:], S2A[:], 1e-5, None, AL.add, AL.bypass))
        kb.vwait(v)
        kb.inc(v.reciprocal(S1A[:], S2A[:]), "c0v", 1)

    @block.sync
    def _(sync):
        for hc in range(32):
            if hc < 2:
                sync.wait_ge(kb.sem("f0"), 64)
                sync.wait_ge(kb.sem("sq0"), 32)
            else:
                kb.w(sync, "a0", hc - 1)
            kb.dinc(sync.dma_start(out=XB[hc % 2][:], in_=xsrc(hc)), "xl0b", hc)
        dq.pump_sync(sync, 4)

    @block.vector
    def _(v):
        v.wait_ge(kb.sem("c0s"), 3)
        for hc in range(32):
            c, th = hc // 2, hc % 2
            kb.dw(v, "xl0b", hc)
            kb.w(v, "a0", hc)
            kb.vop(v.tensor_tensor(TR0[:], XB[hc % 2][:], M1R[:, th * TH:(th + 1) * TH], AL.subtract))
            kb.vwait(v)
            i = v.tensor_tensor(XN0[:, c * T + th * TH:c * T + (th + 1) * TH], TR0[:],
                                RSTD[:, th * TH:(th + 1) * TH], AL.mult)
            kb.inc(i, "a0", 1)
        dq.vec(v, 4)

    free("S1A", "S2A", "SQT", "TR0", "M1R", "RSTD", "XB0", "XB1")

    if upto >= 1:
        OA = sb("OA", [128, NH * T], BF16)
        QT = [sb(f"QT{i}", [128, T], BF16) for i in range(2)]
        KT = [sb(f"KT{i}", [128, T], BF16) for i in range(2)]
        VTF = [sb(f"VTF{i}", [128, T], BF16) for i in range(2)]
        EB = [sb(f"EB{i}", [128, 1024], BF16) for i in range(2)]
        RD = sb("RD", [128, 512], F32)

        @block.gpsimd
        def _(g):
            for h in range(NH):
                kb.w(g, "wqc", h - 1)
                for half in range(2):
                    src = ins["wqkv"][h, half * 8:(half + 1) * 8, :, :].transpose([1, 0, 2])
                    dst = WQH[h % 2][:].rearrange("p (c o) -> p c o", o=384)[:, half * 8:(half + 1) * 8]
                    kb.dinc(g.dma_start(out=dst, in_=src), "wq", h, 16)
                if h >= 1:
                    dq.sc_loads(g, 8 if h == 1 else 5)
            dq.sc_loads(g, 6)

        @block.tensor
        def _(t):
            t.wait_ge(kb.sem("ini"), INI)
            t.wait_ge(kb.sem("inim"), INIM)

            def qkv_piece(hh, j):
                pi, s4 = j // 4, j % 4
                q = hh * 12 + j
                if j == 0:
                    kb.dw(t, "wq", hh, 32)
                kb.w(t, "qe", q)
                for c in range(NCH):
                    if hh == 0 and j == 0:
                        t.wait_ge(kb.sem("a0"), 2 * c + 2)
                    i = t.matmul(p_st[:, 0:512],
                                 WQH[hh % 2][:, c * 384 + pi * 128:c * 384 + pi * 128 + 128],
                                 XN0[:, c * T + s4 * 512:c * T + (s4 + 1) * 512],
                                 start=(c == 0), stop=(c == NCH - 1))
                if j == 11:
                    kb.inc(i, "wqc", 1)
                else:
                    kb.inc(i, "qp", 1)

            for j in range(12):
                qkv_piece(0, j)

            for h in range(NH):
                t.wait_ge(kb.sem("qe"), h * 12 + 12)
                for bt in range(2):
                    if 2 * h + bt > 0:
                        t.wait_ge(kb.sem("vtk"), 2 * h + bt)
                    for j in range(8):
                        kc = bt * 8 + j
                        i = t.transpose(p_tr[:, j * 128:(j + 1) * 128],
                                        VTF[h % 2][:, kc * 128:(kc + 1) * 128], IDENT[:])
                    kb.inc(i, "ttr", 1)

                base_exp = h * ATT_TOTEXP
                base_msk = h * ATT_TOTMASK
                base_grp = h * ATT_NG
                base_oan = h * 4

                def s_group(gi):
                    qq, rec = ATT_GROUPS[gi]
                    if gi >= 2:
                        t.wait_ge(kb.sem("exf"), base_exp + ATT_CUMEXP[gi - 2])
                    elif h > 0:
                        t.wait_ge(kb.sem("exf"), base_exp)
                    half = (gi % 2) * 1024
                    for kti, (kt, qs, dg) in enumerate(rec):
                        i = t.matmul(p_big[:, half + kti * 512 + qs: half + (kti + 1) * 512],
                                     KT[h % 2][:, kt * 128:(kt + 1) * 128],
                                     QT[h % 2][:, qq * 512 + qs:(qq + 1) * 512],
                                     start=True, stop=True)
                    kb.inc(i, "sgr", 1)

                def do_group(gi):
                    qq, rec = ATT_GROUPS[gi]
                    t.wait_ge(kb.sem("exf"), base_exp + ATT_CUMEXP[gi])
                    if ATT_CUMMASK[gi]:
                        t.wait_ge(kb.sem("msk"), base_msk + ATT_CUMMASK[gi])
                    t.wait_ge(kb.sem("vtk"), 2 * h + 2)
                    first = (gi == 0 or ATT_GROUPS[gi - 1][0] != qq)
                    if first:
                        kb.w(t, "oan", base_oan + qq)
                    for kti, (kt, qs, dg) in enumerate(rec):
                        st = first and kti == 0
                        last = (ATT_QLAST[qq] == gi) and (kti == len(rec) - 1)
                        t.matmul(p_m2[:, qs:512], ONESB[:],
                                 EB[gi % 2][:, kti * 512 + qs:(kti + 1) * 512],
                                 start=st, stop=last)
                        i = t.matmul(p_m1[:, qs:512], VTF[h % 2][:, kt * 128:(kt + 1) * 128],
                                     EB[gi % 2][:, kti * 512 + qs:(kti + 1) * 512],
                                     start=st, stop=last)
                    kb.inc(i, "oco", 1)

                for gi in range(ATT_NG):
                    s_group(gi)
                    if gi >= 1:
                        do_group(gi - 1)
                    if h + 1 < NH and gi < 12:
                        qkv_piece(h + 1, gi)
                do_group(ATT_NG - 1)

        @block.scalar
        def _(s):
            s.wait_ge(kb.sem("ini"), INI)
            s.wait_ge(kb.sem("inim"), INIM)

            def piece_evac(hh, j):
                q = hh * 12 + j
                pi, s4 = j // 4, j % 4
                dst = [QT, KT, VTF][pi][hh % 2]
                if j == 11:
                    kb.w(s, "wqc", hh + 1)
                else:
                    s.wait_ge(kb.sem("qp"), q + 1 - hh)
                if hh >= 2 and j == 0:
                    kb.w(s, "oco", (hh - 1) * ATT_NG)
                kb.inc(s.activation(dst[:, s4 * 512:(s4 + 1) * 512], p_st[:, 0:512], AF.Copy),
                       "qe", 1)

            for j in range(12):
                piece_evac(0, j)
            for h in range(NH):
                for bt in range(2):
                    s.wait_ge(kb.sem("ttr"), 2 * h + bt + 1)
                    i = s.activation(VTF[h % 2][:, bt * 1024:(bt + 1) * 1024], p_tr[:], AF.Copy)
                    kb.inc(i, "vtk", 1)
                base_exp = h * ATT_TOTEXP
                base_grp = h * ATT_NG
                for gi, (qq, rec) in enumerate(ATT_GROUPS):
                    s.wait_ge(kb.sem("sgr"), base_grp + gi + 1)
                    kb.w(s, "oco", base_grp + gi - 1)
                    half = (gi % 2) * 1024
                    if ATT_NEXP[gi] == 1:
                        n = len(rec) * 512
                        i = s.activation(EB[gi % 2][:, 0:n], p_big[:, half:half + n], AF.Exp,
                                         bias=CEXPB[:, 0:1], scale=CEXPS[:, 0:1])
                        kb.inc(i, "exf", 1)
                    else:
                        for kti, (kt, qs, dg) in enumerate(rec):
                            i = s.activation(EB[gi % 2][:, kti * 512 + qs:(kti + 1) * 512],
                                             p_big[:, half + kti * 512 + qs:half + (kti + 1) * 512],
                                             AF.Exp, bias=CEXPB[:, 0:1], scale=CEXPS[:, 0:1])
                            kb.inc(i, "exf", 1)
                    if h + 1 < NH and gi < 12:
                        piece_evac(h + 1, gi)

        @block.vector
        def _(v):
            for h in range(NH):
                base_exp = h * ATT_TOTEXP
                base_msk = h * ATT_TOTMASK
                base_grp = h * ATT_NG
                base_oan = h * 4
                for gi, (qq, rec) in enumerate(ATT_GROUPS):
                    if ATT_NMASK[gi]:
                        v.wait_ge(kb.sem("exf"), base_exp + ATT_CUMEXP[gi])
                        for kti, (kt, qs, dg) in enumerate(rec):
                            if dg is None:
                                continue
                            sl = EB[gi % 2][:, kti * 512 + 128 * dg:kti * 512 + 128 * dg + 128]
                            kb.inc(v.tensor_tensor(sl, sl, TRIM[:], AL.mult), "msk", 1)
                    if ATT_QLAST[qq] == gi:
                        v.wait_ge(kb.sem("oco"), base_grp + gi + 1)
                        kb.w(v, "oan", base_oan + qq)
                        kb.vop(v.reciprocal(RD[:], p_m2[:]))
                        kb.vwait(v)
                        i = v.scalar_tensor_tensor(OA[:, h * T + qq * 512:h * T + (qq + 1) * 512],
                                                   p_m1[:], 1.0, RD[:], AL.mult, AL.mult)
                        kb.inc(i, "oan", 1)
                        dq.vec(v, 1)
                dq.vec(v, 1)

        @block.sync
        def _(sync):
            dq.pump_sync(sync, 78)

        OAN_ALL = 4 * NH

    if upto >= 2:
        free("QT0", "QT1", "KT0", "KT1", "EB0", "EB1", "RD", "VTF0", "VTF1", "WQH0", "WQH1")
        WPA = [sb(f"WPA{i}", [128, NCH * 128], BF16) for i in range(2)]
        WPB = [sb(f"WPB{i}", [128, NCH * 128], BF16) for i in range(2)]
        X1B = [sb(f"X1B{i}", [128, TH], F32) for i in range(2)]
        XB = [sb(f"XB{i}", [128, TH], F32) for i in range(2)]
        LG8 = sb("LG8", [EE, T], F32)
        GX = sb("GX", [128, T], BF16)
        MIDG = sb("MIDG", [128, T], BF16)
        GT8 = sb("GT8", [EE, T], BF16)
        LGD = sb("LGD", [128, 128], F32)
        EL8 = sb("EL8", [128, 128], F32)
        G2D = sb("G2D", [128, 128], BF16)
        TMP8 = sb("TMP8", [128, 8], F32)
        COLS = sb("COLS", [128, 8], F32)
        MRK = sb("MRK", [128, 4], F32)
        NM1A = sb("NM1A", [128, NCH], F32)

        @block.gpsimd
        def _(g):
            for oc in range(NCH):
                if oc < 2:
                    g.wait_ge(kb.sem("oco"), NH * ATT_NG)
                else:
                    kb.w(g, "ept", 2 * (oc - 1))
                for hf in range(2):
                    kb.inc(g.dma_start(
                        out=WPA[oc % 2][:].rearrange("p (c o) -> p c o", o=128)[:, hf * 8:(hf + 1) * 8],
                        in_=ins["baseT"][oc, hf * 8:(hf + 1) * 8].transpose([1, 0, 2])), f"wp2_{oc % 2}", 16)
                    kb.inc(g.dma_start(
                        out=WPB[oc % 2][:].rearrange("p (c o) -> p c o", o=128)[:, hf * 8:(hf + 1) * 8],
                        in_=ins["woT"][oc, hf * 8:(hf + 1) * 8].transpose([1, 0, 2])), f"wp2_{oc % 2}", 16)

        @block.tensor
        def _(t):
            t.wait_ge(kb.sem("exf"), NH * ATT_TOTEXP)
            for c in range(NCH):
                for s4 in range(4):
                    i = t.matmul(p_big[0:EE, s4 * 512:(s4 + 1) * 512],
                                 RTW[:, c * EE:(c + 1) * EE],
                                 XN0[:, c * T + s4 * 512:c * T + (s4 + 1) * 512],
                                 start=(c == 0), stop=(c == NCH - 1))
            kb.inc(i, "c2t", 1)
            t.wait_ge(kb.sem("c2s"), 1)
            t.wait_ge(kb.sem("oan"), OAN_ALL)
            for tc in range(NCH):
                i = t.transpose(p_m1[:, tc * 8:(tc + 1) * 8],
                                LG8[0:EE, tc * 128:(tc + 1) * 128], IDENTF[0:EE, 0:EE])
            kb.inc(i, "c2t", 1)
            t.wait_ge(kb.sem("vtk"), 2 * NH)
            for half in range(2):
                t.wait_ge(kb.sem("c2v"), half + 1)
                if half == 1:
                    t.wait_ge(kb.sem("c2v"), 3)
                for j in range(8):
                    tc = half * 8 + j
                    i = t.transpose(p_tr[0:8, j * 128:(j + 1) * 128],
                                    G2D[:, tc * 8:(tc + 1) * 8], IDENT[:])
                kb.inc(i, "c2t", 1)
            t.wait_ge(kb.sem("c2v"), 4)
            for s4 in range(4):
                i = t.matmul(p_big[:, s4 * 512:(s4 + 1) * 512], EMAP[0:EE, :],
                             GT8[0:EE, s4 * 512:(s4 + 1) * 512], start=True, stop=True)
            kb.inc(i, "c2t", 1)
            t.wait_ge(kb.sem("c2v"), 5)
            for c in range(NCH):
                for s4 in range(4):
                    i = t.matmul(p_big[:, s4 * 512:(s4 + 1) * 512], AAL[:, c * ER:(c + 1) * ER],
                                 XN0[:, c * T + s4 * 512:c * T + (s4 + 1) * 512],
                                 start=(c == 0), stop=(c == NCH - 1))
            kb.inc(i, "c2t", 1)
            t.wait_ge(kb.sem("c2v"), 6)
            for oc in range(NCH):
                kb.dw(t, "wp2", oc, 64)
                for th in range(2):
                    fi = 2 * oc + th
                    kb.w(t, "epv", fi - 1)
                    half = (fi % 2) * 1024
                    tb = th * TH
                    for c in range(NCH):
                        for s2 in range(2):
                            t.matmul(p_big[:, half + s2 * 512:half + (s2 + 1) * 512],
                                     WPA[oc % 2][:, c * 128:(c + 1) * 128],
                                     XN0[:, c * T + tb + s2 * 512:c * T + tb + (s2 + 1) * 512],
                                     start=(c == 0), stop=False)
                    for c in range(NCH):
                        for s2 in range(2):
                            t.matmul(p_big[:, half + s2 * 512:half + (s2 + 1) * 512],
                                     WPB[oc % 2][:, c * 128:(c + 1) * 128],
                                     OA[:, c * T + tb + s2 * 512:c * T + tb + (s2 + 1) * 512],
                                     start=False, stop=False)
                    for s2 in range(2):
                        i = t.matmul(p_big[:, half + s2 * 512:half + (s2 + 1) * 512],
                                     BAL[:, oc * 128:(oc + 1) * 128],
                                     MIDG[:, tb + s2 * 512:tb + (s2 + 1) * 512],
                                     start=False, stop=True)
                    kb.inc(i, "ept", 1)

        @block.scalar
        def _(s):
            s.wait_ge(kb.sem("c2t"), 1)
            kb.inc(s.activation(LG8[0:EE, :], p_big[0:EE, :], AF.Identity, bias=RBH[0:EE, 0:1]), "c2s", 1)
            s.wait_ge(kb.sem("c2v0"), 1)
            for tc in range(NCH):
                i = s.activation(EL8[:, tc * 8:(tc + 1) * 8], LGD[:, tc * 8:(tc + 1) * 8],
                                 AF.Exp, bias=NM1A[:, tc:tc + 1], scale=1.0)
            kb.inc(i, "c2s", 1)

        @block.sync
        def _(sync):
            for hc in range(32):
                if hc < 2:
                    sync.wait_ge(kb.sem("a0"), 32)
                else:
                    kb.w(sync, "epv", hc - 1)
                kb.dinc(sync.dma_start(out=XB[hc % 2][:], in_=xsrc(hc)), "xl2", hc)

        @block.scalar
        def _(s):
            for hc in range(32):
                c, th = hc // 2, hc % 2
                s.wait_ge(kb.sem("epv"), hc + 1)
                kb.dinc(s.dma_start(out=scr["x1"].ap()[c * 128:(c + 1) * 128, th * TH:(th + 1) * TH],
                                   in_=X1B[hc % 2][:]), "x1s", hc)

        @block.vector
        def _(v):
            v.wait_ge(kb.sem("c2t"), 2)
            kb.vop(v.tensor_copy(LGD[:], p_m1[:, 0:128]))
            for tc in range(NCH):
                sl = LGD[:, tc * 8:(tc + 1) * 8]
                kb.vwait(v)
                kb.vop(v.reduce_max(COLS[:, 0:1], sl, axis=AX.X))
                kb.vwait(v)
                i = v.tensor_scalar(NM1A[:, tc:tc + 1], COLS[:, 0:1], -1.0, None, AL.mult, AL.bypass)
                kb.vop(i)
            kb.vwait(v)
            kb.inc(v.tensor_copy(MRK[:, 0:1], CN2K[:, 0:1]), "c2v0", 1)
            v.wait_ge(kb.sem("c2s"), 2)
            for half in range(2):
                for j in range(8):
                    tc = half * 8 + j
                    sl = LGD[:, tc * 8:(tc + 1) * 8]
                    kb.vwait(v)
                    kb.vop(v.reduce_max(COLS[:, 0:1], sl, axis=AX.X))
                    kb.vwait(v)
                    kb.vop(v.tensor_single_scalar(TMP8[:], sl, COLS[:, 0:1], AL.is_equal))
                    kb.vwait(v)
                    kb.vop(v.scalar_tensor_tensor(TMP8[:], TMP8[:], -1e9, sl, AL.mult, AL.add))
                    kb.vwait(v)
                    kb.vop(v.reduce_max(COLS[:, 1:2], TMP8[:], axis=AX.X))
                    kb.vwait(v)
                    kb.vop(v.tensor_single_scalar(TMP8[:], sl, COLS[:, 1:2], AL.is_ge))
                    kb.vwait(v)
                    kb.vop(v.tensor_tensor(TMP8[:], TMP8[:], EL8[:, tc * 8:(tc + 1) * 8], AL.mult))
                    kb.vwait(v)
                    kb.vop(v.reduce_sum(COLS[:, 3:4], TMP8[:], axis=AX.X))
                    kb.vwait(v)
                    kb.vop(v.reciprocal(COLS[:, 4:5], COLS[:, 3:4]))
                    kb.vwait(v)
                    i = v.tensor_single_scalar(G2D[:, tc * 8:(tc + 1) * 8], TMP8[:], COLS[:, 4:5], AL.mult)
                    kb.vop(i)
                kb.vwait(v)
                kb.inc(v.tensor_copy(MRK[:, 1 + half:2 + half], CN2K[:, 0:1]), "c2v", 1)
            for half in range(2):
                v.wait_ge(kb.sem("c2t"), 3 + half)
                i = v.tensor_copy(GT8[0:EE, half * 1024:(half + 1) * 1024], p_tr[0:8, :])
                kb.inc(i, "c2v", 1)
            v.wait_ge(kb.sem("c2t"), 5)
            kb.inc(v.tensor_copy(GX[:], p_big[:]), "c2v", 1)
            v.wait_ge(kb.sem("c2t"), 6)
            kb.inc(v.scalar_tensor_tensor(MIDG[:], p_big[:], 2.0, GX[:], AL.mult, AL.mult), "c2v", 1)
            for hc in range(32):
                v.wait_ge(kb.sem("ept"), hc + 1)
                kb.dw(v, "xl2", hc)
                if hc >= 2:
                    kb.dw(v, "x1s", hc - 2)
                half = (hc % 2) * 1024
                oc = hc // 2
                i = v.scalar_tensor_tensor(X1B[hc % 2][:], p_big[:, half:half + 1024],
                                           BBH[:, oc:oc + 1], XB[hc % 2][:], AL.add, AL.add)
                kb.inc(i, "epv", 1)

    if upto >= 3:
        free("LG8", "LGD", "TMP8", "COLS", "MRK", "NM1A", "EL8", "G2D", "GT8", "GX", "MIDG")
        free("OA", "XB0", "XB1")
        S1A = sb("S1A3", [128, T], F32)
        S2A = sb("S2A3", [128, T], F32)
        SQT = sb("SQT3", [128, TH], F32)
        M1R = sb("M1R3", [128, T], F32)
        RSTD = sb("RSTD3", [128, T], F32)
        GZ = sb("GZ3", [128, T], F32)
        TR0 = sb("TR03", [128, TH], F32)
        TR1 = sb("TR13", [128, T], F32)
        XN1S = [sb(f"XN1S{i}", [128, TH], BF16) for i in range(2)]
        Z = XN0

        def x1src(hc):
            c, th = hc // 2, hc % 2
            return scr["x1"].ap()[c * 128:(c + 1) * 128, th * TH:(th + 1) * TH]

        @block.scalar
        def _(s):
            kb.dwfull(s, "x1s", 256, 256)
            kb.dinc(s.dma_start(out=X1B[0][:], in_=x1src(0)), "x1l3", 0)
            kb.dinc(s.dma_start(out=X1B[1][:], in_=x1src(1)), "x1l3", 1)
            for hc in range(32):
                kb.dw(s, "x1l3", hc)
                kb.w(s, "f3", 2 * hc)
                kb.inc(s.activation(SQT[:], X1B[hc % 2][:], AF.Square), "sq3", 1)
                if hc + 2 < 32:
                    kb.w(s, "f3", 2 * (hc + 1))
                    kb.dinc(s.dma_start(out=X1B[hc % 2][:], in_=x1src(hc + 2)), "x1l3", hc + 2)

        @block.vector
        def _(v):
            for hc in range(32):
                th = hc % 2
                kb.dw(v, "x1l3", hc)
                v.wait_ge(kb.sem("sq3"), hc + 1)
                if hc < 2:
                    i1 = v.tensor_copy(S1A[:, th * TH:(th + 1) * TH], X1B[hc % 2][:])
                    i2 = v.tensor_copy(S2A[:, th * TH:(th + 1) * TH], SQT[:])
                else:
                    i1 = v.tensor_add(S1A[:, th * TH:(th + 1) * TH], S1A[:, th * TH:(th + 1) * TH], X1B[hc % 2][:])
                    i2 = v.tensor_add(S2A[:, th * TH:(th + 1) * TH], S2A[:, th * TH:(th + 1) * TH], SQT[:])
                kb.inc(i1, "f3", 1)
                kb.inc(i2, "f3", 1)

        @block.gpsimd
        def _(g):
            dq.sc_loads(g, 3)

        @block.tensor
        def _(t):
            t.wait_ge(kb.sem("f3"), 64)
            t.wait_ge(kb.sem("epv"), 32)
            for s4 in range(4):
                i = t.matmul(p_big[:, s4 * 512:(s4 + 1) * 512], ONESF[:],
                             S1A[:, s4 * 512:(s4 + 1) * 512], start=True, stop=True)
            kb.inc(i, "c3t", 1)
            t.wait_ge(kb.sem("c3s"), 1)
            for s4 in range(4):
                i = t.matmul(p_big[:, s4 * 512:(s4 + 1) * 512], ONESF[:],
                             S2A[:, s4 * 512:(s4 + 1) * 512], start=True, stop=True)
            kb.inc(i, "c3t", 1)

        @block.scalar
        def _(s):
            s.wait_ge(kb.sem("c3t"), 1)
            kb.inc(s.activation(M1R[:], p_big[:], AF.Identity, scale=CN2K[:, 0:1]), "c3s", 1)
            s.wait_ge(kb.sem("c3t"), 2)
            kb.inc(s.activation(S2A[:], p_big[:], AF.Identity, scale=CN2K[:, 0:1]), "c3s", 1)
            s.wait_ge(kb.sem("c3v"), 1)
            kb.inc(s.activation(RSTD[:], TR1[:], AF.Sqrt), "c3s", 1)
            s.wait_ge(kb.sem("c3v"), 2)
            kb.inc(s.activation(GZ[:], TR1[:], AF.Sqrt), "c3s", 1)

        @block.vector
        def _(v):
            v.wait_ge(kb.sem("c3s"), 2)
            kb.vop(v.tensor_tensor(S1A[:], M1R[:], M1R[:], AL.mult))
            kb.vwait(v)
            kb.vop(v.tensor_tensor(S2A[:], S2A[:], S1A[:], AL.subtract))
            kb.vwait(v)
            kb.vop(v.tensor_scalar(S1A[:], S2A[:], 1e-5, None, AL.add, AL.bypass))
            kb.vwait(v)
            kb.inc(v.reciprocal(TR1[:], S1A[:]), "c3v", 1)
            v.wait_ge(kb.sem("c3s"), 3)
            kb.vwait(v)
            kb.w(v, "c3v", 1)
            kb.vop(v.tensor_tensor(S2A[:], S2A[:], TR1[:], AL.mult))
            kb.vwait(v)
            kb.vop(v.tensor_scalar(S2A[:], S2A[:], 1e-5, None, AL.add, AL.bypass))
            kb.vwait(v)
            kb.inc(v.reciprocal(TR1[:], S2A[:]), "c3v", 1)
            v.wait_ge(kb.sem("c3s"), 4)
            kb.vwait(v)
            kb.inc(v.tensor_tensor(GZ[:], GZ[:], RSTD[:], AL.mult), "c3v", 1)
            for hc in range(32):
                c, th = hc // 2, hc % 2
                kb.dw(v, "x1l3b", hc)
                kb.w(v, "p3z", hc)
                kb.vop(v.tensor_tensor(TR0[:], X1B[hc % 2][:], M1R[:, th * TH:(th + 1) * TH], AL.subtract))
                if hc >= 2:
                    kb.dw(v, "xn1s", hc - 2)
                kb.vwait(v)
                i = v.tensor_tensor(XN1S[hc % 2][:], TR0[:], RSTD[:, th * TH:(th + 1) * TH], AL.mult)
                kb.inc(i, "p3x", 1)
                i = v.scalar_tensor_tensor(Z[:, c * T + th * TH:c * T + (th + 1) * TH], TR0[:],
                                           RWH[:, c:c + 1], GZ[:, th * TH:(th + 1) * TH],
                                           AL.mult, AL.mult)
                kb.inc(i, "p3z", 1)
            dq.vec(v, 2)

        @block.scalar
        def _(s):
            for hc in range(32):
                if hc < 2:
                    s.wait_ge(kb.sem("f3"), 64)
                    s.wait_ge(kb.sem("sq3"), 32)
                else:
                    kb.w(s, "p3z", hc - 1)
                kb.dinc(s.dma_start(out=X1B[hc % 2][:], in_=x1src(hc)), "x1l3b", hc)

        @block.sync
        def _(sync):
            for hc in range(32):
                c, th = hc // 2, hc % 2
                sync.wait_ge(kb.sem("p3x"), hc + 1)
                kb.dinc(sync.dma_start(out=scr["xn1"].ap()[c * 128:(c + 1) * 128, th * TH:(th + 1) * TH],
                                      in_=XN1S[hc % 2][:]), "xn1s", hc)
            dq.pump_sync(sync, 24)

        free("S1A3", "S2A3", "SQT3", "M1R3", "TR03", "TR13", "WPB0", "WPB1", "RSTD3", "GZ3")

    if upto >= 4:
        WM = [[sb(f"WM{w}_{i}", [128, NCH * 128], BF16) for i in range(2)] for w in range(5)]
        AAc = sb("AAc", [128, T], F32)
        UUc = sb("UUc", [128, T], F32)
        HHc = sb("HHc", [128, T], F32)
        CAc = sb("CAc", [128, T], BF16)
        CCc = sb("CCc", [128, T], BF16)
        DDc = sb("DDc", [128, T], BF16)
        YB = [sb(f"YB{i}", [128, T], BF16) for i in range(2)]
        WKEYS = ["CawT", "CbwT", "AwT", "DwT", "BwT"]
        EVENG = ["s", "v", "s", "s", "v"]

        @block.gpsimd
        def _(g):
            for nn in range(NCH):
                if nn < 2:
                    g.wait_ge(kb.sem("ept"), 32)
                    g.wait_ge(kb.sem("p3z"), 32)
                else:
                    kb.w(g, "p4s", 6 * (nn - 1))
                    kb.w(g, "p4v", 4 * (nn - 1))
                for w in range(5):
                    for hf in range(2):
                        kb.inc(g.dma_start(
                            out=WM[w][nn % 2][:].rearrange("p (c o) -> p c o", o=128)[:, hf * 8:(hf + 1) * 8],
                            in_=ins[WKEYS[w]][nn, hf * 8:(hf + 1) * 8].transpose([1, 0, 2])), f"wm_{nn % 2}", 16)
                if nn >= 1:
                    dq.sc_loads(g, 3)

        @block.tensor
        def _(t):
            t.wait_ge(kb.sem("c3v"), 3)
            for nn in range(NCH):
                kb.dw(t, "wm", nn, 160)
                for w in range(5):
                    for th in range(2):
                        fi = 10 * nn + 2 * w + th
                        pf = fi - 2
                        if pf >= 0:
                            peng = EVENG[(pf % 10) // 2]
                            cnt = sum(1 for q in range(pf + 1) if EVENG[(q % 10) // 2] == peng)
                            kb.w(t, "p4" + peng, cnt)
                        half = (fi % 2) * 1024
                        for c in range(NCH):
                            if nn == 0 and w == 0 and th == 0:
                                t.wait_ge(kb.sem("p3z"), 2 * c + 2)
                            for s2 in range(2):
                                i = t.matmul(p_big[:, half + s2 * 512:half + (s2 + 1) * 512],
                                             WM[w][nn % 2][:, c * 128:(c + 1) * 128],
                                             Z[:, c * T + th * TH + s2 * 512:c * T + th * TH + (s2 + 1) * 512],
                                             start=(c == 0), stop=(c == NCH - 1))
                        kb.inc(i, "p4t", 1)

        @block.scalar
        def _(s):
            for nn in range(NCH):
                for w, th, dst, func, bias in [
                        (0, 0, CAc, AF.Silu, None), (0, 1, CAc, AF.Silu, None),
                        (2, 0, AAc, AF.Sigmoid, ABH), (2, 1, AAc, AF.Sigmoid, ABH),
                        (3, 0, DDc, AF.Sigmoid, None), (3, 1, DDc, AF.Sigmoid, None)]:
                    fi = 10 * nn + 2 * w + th
                    s.wait_ge(kb.sem("p4t"), fi + 1)
                    if nn > 0:
                        kb.w(s, "scn", nn)
                        kb.w(s, "p4y", nn)
                    half = (fi % 2) * 1024
                    kw = {}
                    if bias is not None:
                        kw["bias"] = bias[:, nn:nn + 1]
                    i = s.activation(dst[:, th * 1024:(th + 1) * 1024], p_big[:, half:half + 1024],
                                     func, **kw)
                    kb.inc(i, "p4s", 1)
                if nn >= 1:
                    s.wait_ge(kb.sem("p4y"), nn)
                    kb.dinc(s.dma_start(out=scr["y"].ap()[(nn - 1) * 128:nn * 128, :],
                                        in_=YB[(nn - 1) % 2][:]), "ys", nn - 1)

        @block.vector
        def _(v):
            for nn in range(NCH):
                if nn > 0:
                    kb.w(v, "scn", nn)
                for w, th in [(1, 0), (1, 1), (4, 0), (4, 1)]:
                    fi = 10 * nn + 2 * w + th
                    v.wait_ge(kb.sem("p4t"), fi + 1)
                    half = (fi % 2) * 1024
                    if w == 1:
                        v.wait_ge(kb.sem("p4s"), 6 * nn + th + 1)
                        i = v.scalar_tensor_tensor(CCc[:, th * 1024:(th + 1) * 1024],
                                                   p_big[:, half:half + 1024], 1.0,
                                                   CAc[:, th * 1024:(th + 1) * 1024], AL.mult, AL.mult)
                    else:
                        v.wait_ge(kb.sem("p4s"), 6 * nn + 4 + th + 1)
                        i = v.scalar_tensor_tensor(UUc[:, th * 1024:(th + 1) * 1024],
                                                   p_big[:, half:half + 1024], 1.0,
                                                   DDc[:, th * 1024:(th + 1) * 1024], AL.mult, AL.mult)
                    kb.inc(i, "p4v", 1)
                dq.vec(v, 2)
                v.wait_ge(kb.sem("p4s"), 6 * nn + 4)
                kb.w(v, "p4v", 4 * nn + 4)
                i = v.tensor_tensor_scan(HHc[:], AAc[:], UUc[:], 0.0, AL.mult, AL.add)
                kb.inc(i, "scn", 1)
                if nn >= 2:
                    kb.dw(v, "ys", nn - 2)
                kb.w(v, "scn", nn + 1)
                i = v.scalar_tensor_tensor(YB[nn % 2][:], HHc[:], 1.0, CCc[:], AL.mult, AL.mult)
                kb.inc(i, "p4y", 1)
                dq.vec(v, 1)

        @block.scalar
        def _(s):
            s.wait_ge(kb.sem("p4y"), NCH)
            kb.dinc(s.dma_start(out=scr["y"].ap()[(NCH - 1) * 128:NCH * 128, :],
                                in_=YB[(NCH - 1) % 2][:]), "ys", NCH - 1)

        @block.sync
        def _(sync):
            dq.pump_sync(sync, 30)

    if upto >= 5:
        free("AAc", "UUc", "HHc", "CAc", "CCc", "DDc",
             *[f"WM{w}_{i}" for w in range(5) for i in range(2)])
        YTl = [sb(f"YT{c}", [128, T], BF16) for c in range(NCH)]
        P1S = [sb(f"P1S{i}", [128, TH], F32) for i in range(2)]

        @block.sync
        def _(sync):
            sync.wait_ge(kb.sem("p4y"), NCH)
            for c in range(NCH):
                kb.dw(sync, "ys", c)
                kb.inc(sync.dma_start(out=YTl[c][:],
                                      in_=scr["y"].ap()[c * 128:(c + 1) * 128, :]), "yl", 16)
            sync.wait_ge(kb.sem("p4t"), 160)
            kb.dwfull(sync, "xn1s", 256, 256)
            for c in range(NCH):
                kb.inc(sync.dma_start(out=XN0[:, c * T:(c + 1) * T],
                                      in_=scr["xn1"].ap()[c * 128:(c + 1) * 128, :]), "xn1l", 16)

        @block.gpsimd
        def _(g):
            for oc in range(NCH):
                if oc < 2:
                    g.wait_ge(kb.sem("ept"), 2 * NCH)
                else:
                    kb.w(g, "p5t", 2 * (oc - 1))
                for hf in range(2):
                    kb.inc(g.dma_start(
                        out=WPA[oc % 2][:].rearrange("p (c o) -> p c o", o=128)[:, hf * 8:(hf + 1) * 8],
                        in_=ins["OwT"][oc, hf * 8:(hf + 1) * 8].transpose([1, 0, 2])), f"wo5_{oc % 2}", 16)
                dq.sc_loads(g, 2)

        @block.tensor
        def _(t):
            t.wait_ge(kb.sem("p4s"), 96)
            t.wait_ge(kb.sem("p4v"), 64)
            t.wait_ge(kb.sem("yl"), 16 * NCH)
            for oc in range(NCH):
                kb.dw(t, "wo5", oc, 32)
                for th in range(2):
                    fi = 2 * oc + th
                    kb.w(t, "p5v", fi - 1)
                    half = (fi % 2) * 1024
                    tb = th * TH
                    for c in range(NCH):
                        for s2 in range(2):
                            i = t.matmul(p_big[:, half + s2 * 512:half + (s2 + 1) * 512],
                                         WPA[oc % 2][:, c * 128:(c + 1) * 128],
                                         YTl[c][:, tb + s2 * 512:tb + (s2 + 1) * 512],
                                         start=(c == 0), stop=(c == NCH - 1))
                    kb.inc(i, "p5t", 1)

        @block.scalar
        def _(s):
            for hc in range(32):
                c, th = hc // 2, hc % 2
                if hc < 2:
                    kb.dwfull(s, "xn1s", 256, 256)
                    s.wait_ge(kb.sem("p3z"), 32)
                else:
                    kb.w(s, "p5v", hc - 1)
                kb.dinc(s.dma_start(out=XN1S[hc % 2][:],
                                    in_=scr["xn1"].ap()[c * 128:(c + 1) * 128, th * TH:(th + 1) * TH]),
                        "p5l", hc)
                kb.dinc(s.dma_start(out=X1B[hc % 2][:], in_=x1src(hc)), "p5l", hc)
                if hc >= 2:
                    ph, pc, pt = hc - 2, (hc - 2) // 2, (hc - 2) % 2
                    kb.dinc(s.dma_start(out=scr["p1"].ap()[pc * 128:(pc + 1) * 128, pt * TH:(pt + 1) * TH],
                                        in_=P1S[ph % 2][:]), "p1st", ph)

        @block.vector
        def _(v):
            for hc in range(32):
                oc = hc // 2
                v.wait_ge(kb.sem("p5t"), hc + 1)
                kb.dw(v, "p5l", hc, 32)
                if hc >= 2:
                    kb.dw(v, "p1st", hc - 2)
                half = (hc % 2) * 1024
                kb.vop(v.scalar_tensor_tensor(P1S[hc % 2][:], p_big[:, half:half + 1024],
                                              OBH[:, oc:oc + 1], XN1S[hc % 2][:], AL.add, AL.add))
                kb.vwait(v)
                i = v.tensor_add(P1S[hc % 2][:], P1S[hc % 2][:], X1B[hc % 2][:])
                kb.inc(i, "p5v", 1)
                dq.vec(v, 1)

        @block.scalar
        def _(s):
            for hc in range(30, 32):
                c, th = hc // 2, hc % 2
                s.wait_ge(kb.sem("p5v"), hc + 1)
                kb.dinc(s.dma_start(out=scr["p1"].ap()[c * 128:(c + 1) * 128, th * TH:(th + 1) * TH],
                                    in_=P1S[hc % 2][:]), "p1st", hc)


        DQST = dq.nu * 16

    if upto >= 6:
        free("X1B0", "X1B1", "XN1S0", "XN1S1", "P1S0", "P1S1", "YB0", "YB1",
             "dq_pk0", "dq_pk1", "dq_sc0", "dq_sc1", "dq_out",
             "dq_t0", "dq_t1", "dq_t2", "dq_t3",
             "WPA0", "WPA1", *[f"YT{c}" for c in range(NCH)])
        XN1R = XN0
        HQ = sb("HQ", [128, NGC * 512], BF16)
        WG6 = [sb(f"WG6{i}", [128, NCH * 256], BF16) for i in range(2)]
        WGU6 = [sb(f"WGU6{i}", [128, NCH * 256], BF16) for i in range(2)]
        WD6 = [sb(f"WD6{i}", [128, 11 * 512], BF16) for i in range(2)]
        SG6 = sb("SG6", [128, 512], BF16)
        YF = sb("YF", [128, NCH * 512], BF16)
        SQ6 = sb("SQ6", [128, 512], F32)
        S2F = sb("S2F", [128, 512], F32)
        GF = sb("GF", [128, 512], F32)
        P1B = [sb(f"P1B{i}", [128, 512], F32) for i in range(2)]
        X2B = [sb(f"X2B{i}", [128, 512], F32) for i in range(2)]

        @block.sync
        def _(sync):
            sync.wait_ge(kb.sem("p5t"), 2 * NCH)
            kb.dwfull(sync, "p1st", 256, 256)
            sync.wait_ge(kb.sem("dq_st"), DQST)
            for tq in range(4):
                for g2 in range(22):
                    li = tq * 22 + g2
                    if tq > 0 and 2 <= g2 <= 17:
                        sdc = g2 - 2
                        sli = (tq - 1) * 16 + sdc
                        sync.wait_ge(kb.sem("p6x2"), sli + 1)
                        kb.dinc(sync.dma_start(out=out[sdc * 128:(sdc + 1) * 128, (tq - 1) * 512:tq * 512],
                                               in_=X2B[sdc % 2][:]), "outs", sli)
                    kb.w(sync, "hqv", (li - 1) * 2)
                    for hf in range(2):
                        kb.inc(sync.dma_start(
                            out=WG6[li % 2][:].rearrange("p (c o) -> p c o", o=256)[:, hf * 8:(hf + 1) * 8],
                            in_=scr["wgu"].ap().rearrange("(c p) o -> p c o", p=128)[
                                :, hf * 8:(hf + 1) * 8, g2 * 256:(g2 + 1) * 256]), f"wg6_{li % 2}", 16)
                        kb.inc(sync.dma_start(
                            out=WGU6[li % 2][:].rearrange("p (c o) -> p c o", o=256)[:, hf * 8:(hf + 1) * 8],
                            in_=scr["wgu"].ap().rearrange("(c p) o -> p c o", p=128)[
                                :, hf * 8:(hf + 1) * 8, IH + g2 * 256:IH + (g2 + 1) * 256]), f"wg6_{li % 2}", 16)
                for dg in range(4):
                    for kq in range(4):
                        di = tq * 16 + dg * 4 + kq
                        kb.w(sync, "p6dk", di - 1)
                        kb.inc(sync.dma_start(
                            out=WD6[di % 2][:].rearrange("p (c o) -> p c o", o=512),
                            in_=scr["wdn"].ap().rearrange("(c p) o -> p c o", p=128)[
                                :, kq * 11:(kq + 1) * 11, dg * 512:(dg + 1) * 512]), f"wd6_{di % 2}", 16)

        @block.tensor
        def _(t):
            t.wait_ge(kb.sem("xn1l"), 16 * NCH)
            for tq in range(4):
                tb = tq * 512
                for g2 in range(22):
                    li = tq * 22 + g2
                    kb.dw(t, "wg6", li, 64)
                    for j in range(2):
                        gi6 = li * 2 + j
                        kb.w(t, "p6gs", gi6)
                        kb.w(t, "hqv", gi6)
                        for c in range(NCH):
                            i = t.matmul(p_m1[:, 0:512],
                                         WG6[li % 2][:, c * 256 + j * 128:c * 256 + (j + 1) * 128],
                                         XN1R[:, c * T + tb:c * T + tb + 512],
                                         start=(c == 0), stop=(c == NCH - 1))
                        kb.inc(i, "p6g", 1)
                        for c in range(NCH):
                            i = t.matmul(p_m2[:, 0:512],
                                         WGU6[li % 2][:, c * 256 + j * 128:c * 256 + (j + 1) * 128],
                                         XN1R[:, c * T + tb:c * T + tb + 512],
                                         start=(c == 0), stop=(c == NCH - 1))
                        kb.inc(i, "p6u", 1)
                for dg in range(4):
                    for kq in range(4):
                        di = tq * 16 + dg * 4 + kq
                        kb.dw(t, "wd6", di, 16)
                        if kq == 0:
                            t.wait_ge(kb.sem("hqv"), 44 * (tq + 1))
                            for dc4 in range(4):
                                fli = tq * 16 + dg * 4 + dc4
                                kb.w(t, "p6sq", fli - 3)
                        for ch in range(11):
                            for dc4 in range(4):
                                i = t.matmul(p_big[:, dc4 * 512:(dc4 + 1) * 512],
                                             WD6[di % 2][:, ch * 512 + dc4 * 128:ch * 512 + dc4 * 128 + 128],
                                             HQ[:, (kq * 11 + ch) * 512:(kq * 11 + ch + 1) * 512],
                                             start=(kq == 0 and ch == 0), stop=(kq == 3 and ch == 10))
                        kb.inc(i, "p6dk", 1)
                t.wait_ge(kb.sem("p6sf"), 16 * (tq + 1))
                kb.inc(t.matmul(p_st[:, 0:512], ONESF[:], S2F[:], start=True, stop=True),
                       "p6st", 1)

        @block.scalar
        def _(s):
            kb.dwfull(s, "p1st", 256, 256)
            kb.dinc(s.dma_start(out=P1B[0][:], in_=scr["p1"].ap()[0:128, 0:512]), "p1l6", 0)
            kb.dinc(s.dma_start(out=P1B[1][:], in_=scr["p1"].ap()[128:256, 0:512]), "p1l6", 1)
            for tq in range(4):
                for og in range(44):
                    gi6 = tq * 44 + og
                    if tq > 0 and og % 2 == 0 and 4 <= og <= 34:
                        pli = (tq - 1) * 16 + og // 2
                        pdc, ptq = pli % 16, pli // 16
                        kb.w(s, "p6x2", pli - 1)
                        kb.dinc(s.dma_start(out=P1B[pdc % 2][:],
                                            in_=scr["p1"].ap()[pdc * 128:(pdc + 1) * 128,
                                                               ptq * 512:(ptq + 1) * 512]),
                                "p1l6", pli)
                    s.wait_ge(kb.sem("p6g"), gi6 + 1)
                    kb.w(s, "hqv", gi6)
                    kb.inc(s.activation(SG6[:], p_m1[:, 0:512], AF.Silu), "p6gs", 1)
                for dgi in range(4):
                    s.wait_ge(kb.sem("p6dk"), tq * 16 + (dgi + 1) * 4)
                    for dc4 in range(4):
                        fli = tq * 16 + dgi * 4 + dc4
                        i = s.activation(YF[:, (dgi * 4 + dc4) * 512:(dgi * 4 + dc4 + 1) * 512],
                                         p_big[:, dc4 * 512:(dc4 + 1) * 512], AF.Copy)
                        kb.inc(i, "p6ys", 1)
                        kb.w(s, "p6sf", fli)
                        i = s.activation(SQ6[:], p_big[:, dc4 * 512:(dc4 + 1) * 512], AF.Square)
                        kb.inc(i, "p6sq", 1)
                s.wait_ge(kb.sem("p6v2"), tq + 1)
                kb.w(s, "p6x2", 16 * tq)
                kb.inc(s.activation(GF[:], S2F[:], AF.Sqrt), "p6s2", 1)
            for li in range(3 * 16 + 2, 64):
                dc = li % 16
                kb.w(s, "p6x2", li - 1)
                kb.dinc(s.dma_start(out=P1B[dc % 2][:],
                                    in_=scr["p1"].ap()[dc * 128:(dc + 1) * 128, 3 * 512:4 * 512]),
                        "p1l6", li)

        @block.vector
        def _(v):
            def x2_chunk(li):
                dc, ltq = li % 16, li // 16
                kb.dw(v, "p1l6", li)
                if li >= 2:
                    kb.dw(v, "outs", li - 2)
                v.wait_ge(kb.sem("p6s2"), ltq + 1)
                kb.vwait(v)
                kb.vop(v.scalar_tensor_tensor(X2B[dc % 2][:], YF[:, dc * 512:(dc + 1) * 512], 1.0,
                                              GF[:], AL.mult, AL.mult))
                kb.vwait(v)
                i = v.tensor_add(X2B[dc % 2][:], X2B[dc % 2][:], P1B[dc % 2][:])
                kb.inc(i, "p6x2", 1)

            for tq in range(4):
                for og in range(44):
                    gi6 = tq * 44 + og
                    if og == 0 and tq > 0:
                        v.wait_ge(kb.sem("p6dk"), 16 * tq)
                    v.wait_ge(kb.sem("p6gs"), gi6 + 1)
                    v.wait_ge(kb.sem("p6u"), gi6 + 1)
                    i = v.scalar_tensor_tensor(HQ[:, og * 512:(og + 1) * 512],
                                               p_m2[:, 0:512], 1.0, SG6[:], AL.mult, AL.mult)
                    kb.inc(i, "hqv", 1)
                    if tq > 0 and og % 2 == 1 and og < 32:
                        x2_chunk((tq - 1) * 16 + og // 2)
                for fi16 in range(16):
                    v.wait_ge(kb.sem("p6sq"), tq * 16 + fi16 + 1)
                    if fi16 == 0:
                        kb.w(v, "p6s2", tq)
                        i = v.tensor_copy(S2F[:], SQ6[:])
                    else:
                        i = v.tensor_add(S2F[:], S2F[:], SQ6[:])
                    kb.inc(i, "p6sf", 1)
                v.wait_ge(kb.sem("p6st"), tq + 1)
                kb.vwait(v)
                kb.vop(v.tensor_scalar(S2F[:], p_st[:, 0:512], 1.0 / D, 1e-6, AL.mult, AL.add))
                kb.vwait(v)
                kb.inc(v.reciprocal(S2F[:], S2F[:]), "p6v2", 1)
            for li in range(3 * 16, 64):
                x2_chunk(li)


        @block.sync
        def _(sync):
            for dc in range(NCH):
                li = 3 * 16 + dc
                sync.wait_ge(kb.sem("p6x2"), li + 1)
                kb.dinc(sync.dma_start(out=out[dc * 128:(dc + 1) * 128, 3 * 512:4 * 512],
                                       in_=X2B[dc % 2][:]), "outs", li)
            kb.dwfull(sync, "outs", 16 * 32, 16 * 32)

    while dq.st_i < dq.nu or dq.vec_i < dq.nu or dq.sc_i < dq.nu or dq.ld_i < dq.nu:
        @block.gpsimd
        def _(g):
            dq.sc_loads(g, 6)

        @block.vector
        def _(v):
            dq.vec(v, 6)

        @block.sync
        def _(sync):
            dq.pump_sync(sync, 6)

    @block.sync
    def _(sync):
        for nm in sorted(kb.dma_sems):
            sync.wait_ge(kb.sem(nm), kb.cnt[nm])

    blockg.__exit__(None, None, None)
    for g in kb._semctx[::-1]:
        g.__exit__(None, None, None)
    return nc


_CACHE = {}


def kernel(**inputs):
    if "nc" not in _CACHE:
        _CACHE["nc"] = build_program()
    nc = _CACHE["nc"]
    shared = stage_shared(inputs)
    x = np.asarray(inputs["L0_attn_x"])
    in_maps = []
    for c in range(B):
        m = dict(shared)
        m["xT"] = np.ascontiguousarray(x[c].T[SIGD])
        in_maps.append(m)
    res = run_bass_kernel_spmd(nc, in_maps, core_ids=list(range(B)))
    outp = np.empty((B, T, D), np.float32)
    for c in range(B):
        x2T = res.results[c]["x2T"]
        fm = np.empty((D, T), np.float32)
        fm[SIGD] = x2T
        outp[c] = fm.T
    return outp
